# version 02
# baseline (speedup 1.0000x reference)
import contextlib
import numpy as np
import concourse.bass as bass
import concourse.mybir as mybir

F32 = mybir.dt.float32
BF16 = mybir.dt.bfloat16
AF = mybir.ActivationFunctionType
ALU = mybir.AluOpType
AX = mybir.AxisListType

T, D = 4096, 2048
HID = 8192
RT = 12582912.0
SCALE = 1.0 / (128.0 ** 0.5)
NEG = -1.0e30
EPS = 1e-5
G4 = [[0, 1, 2, 3], [4, 5, 6, 7]]
G8 = [[0, 1, 2, 3, 4, 5, 6, 7]]
N_GV = float(16384 * 2048)
N_WO = float(2048 * 8192)

ENGS = ("sync", "scalar", "tensor", "vector", "gpsimd")


class Plan:

    def __init__(self, nc, sems, cnt):
        self.nc, self.sems, self.cnt = nc, sems, cnt
        self.ops = {e: [] for e in ENGS}

    def q(self, eng, fn):
        self.ops[eng].append(fn)

    def w(self, eng, sem, val=None):
        v = self.cnt[sem] if val is None else val
        assert v <= self.cnt[sem], f"forward wait {sem}:{v}>{self.cnt[sem]}"
        if v <= 0:
            return
        s = self.sems[sem]
        self.ops[eng].append(lambda e, s=s, v=v: e.wait_ge(s, v))

    def sw(self, eng, sem):
        self.w(eng, sem)

    def i(self, eng, fn, sem, n=1):
        self.cnt[sem] += n
        s = self.sems[sem]
        self.ops[eng].append(lambda e, fn=fn, s=s, n=n: fn(e).then_inc(s, n))
        return self.cnt[sem]

    def run(self, name):
        with self.nc.Block(name) as blk:
            for eng in ENGS:
                ops = self.ops[eng]
                if ops:
                    getattr(blk, eng)(lambda e, ops=ops: [f(e) for f in ops])


class C:
    pass


def build(debug=0, stop_after=None, no_tern=False, no_wout=False, tern_mode=4):
    nc = bass.Bass("TRN2", target_bir_lowering=False, dynamic_dma_scratch_size=16384,
                   detect_race_conditions=False)
    c = C()
    c.debug = debug

    c.d_xb = nc.dram_tensor("xb", [T, D], F32, kind="ExternalInput").ap()
    c.d_xown = nc.dram_tensor("xown", [1024, D], F32, kind="ExternalInput").ap()
    c.d_wqkv = nc.dram_tensor("wqkv", [D, 1536], F32, kind="ExternalInput").ap()
    c.d_bqkv = nc.dram_tensor("bqkv", [1536], F32, kind="ExternalInput").ap()
    c.d_woutc = nc.dram_tensor("woutc", [512, D], F32, kind="ExternalInput").ap()
    c.d_bout = nc.dram_tensor("boutb", [128, D], F32, kind="ExternalInput").ap()
    c.d_gamma = nc.dram_tensor("gammab", [128, D], F32, kind="ExternalInput").ap()
    c.d_wgv = nc.dram_tensor("wgv_sh", [256, 16384], F32, kind="ExternalInput").ap()
    c.d_wo = nc.dram_tensor("wo_sh", [1024, D], F32, kind="ExternalInput").ap()
    c.d_y = nc.dram_tensor("y", [1024, D], F32, kind="ExternalOutput").ap()

    c.d_ar_in = nc.dram_tensor("ar_in", [1, 2], F32, kind="Internal").ap()
    c.d_ar_out = nc.dram_tensor("ar_out", [1, 2], F32, kind="Internal").ap()
    c.d_tgv_in = nc.dram_tensor("tgv_in", [256, 16384], BF16, kind="Internal").ap()
    c.d_tgv_out = nc.dram_tensor("tgv_out", [D, 16384], BF16, kind="Internal", addr_space="Shared").ap()
    c.d_two_in = nc.dram_tensor("two_in", [1024, D], BF16, kind="Internal").ap()
    c.d_two_out = nc.dram_tensor("two_out", [HID, D], BF16, kind="Internal", addr_space="Shared").ap()
    c.d_rs_in = nc.dram_tensor("rs_in", [T, D], BF16, kind="Internal").ap()
    c.d_rs_out = nc.dram_tensor("rs_out", [1024, D], BF16, kind="Internal").ap()
    c.d_h2 = nc.dram_tensor("h2d", [8, 128, HID], F32, kind="Internal").ap()
    c.d_q2 = nc.dram_tensor("q2d", [8, 128, HID], BF16, kind="Internal").ap()
    c.d_o2 = nc.dram_tensor("o2d", [8, 128, D], F32, kind="Internal").ap()

    c.no_tern = no_tern
    c.no_wout = no_wout
    c.tern_mode = tern_mode
    c.dbg = {}
    if debug:
        for nm, shp, dt in [("dbg_q", [128, 4 * T], BF16), ("dbg_k", [128, 4 * T], BF16),
                            ("dbg_v", [128, 4 * T], BF16), ("dbg_bias", [128, 128], F32),
                            ("dbg_sfac", [1, 4], F32), ("dbg_av", [128, 4 * 512], BF16),
                            ("dbg_d", [128, 8], F32), ("dbg_q1", [128, 16 * 1024], BF16),
                            ("dbg_r1", [128, 8 * D], F32),
                            ("dbg_pn0", [128, T], BF16), ("dbg_pn1", [128, T], BF16),
                            ("dbg_pn2", [128, T], BF16), ("dbg_pn3", [128, T], BF16),
                            ("dbg_praw", [128, T], F32), ("dbg_ell", [128, 8], F32),
                            ("dbg_pnh0", [4, 128, T], BF16), ("dbg_prh0", [128, T], F32),
                            ("dbg_s", [128, 512], F32), ("dbg_qs", [128, 128], BF16),
                            ("dbg_ks", [128, 512], BF16), ("dbg_rso", [1024, D], BF16),
                            ("dbg_o1", [128, D], F32), ("dbg_x2", [128, D], F32)]:
            c.dbg[nm] = nc.dram_tensor(nm, shp, dt, kind="ExternalOutput").ap()

    sems, cnt = {}, {}
    for s in ("ld", "wd", "gd", "pe", "dv", "ac", "cc"):
        sems[s] = nc.alloc_semaphore(s)
        cnt[s] = 0

    def sb(stk, name, shape, dt):
        return stk.enter_context(nc.sbuf_tensor(name, shape, dt))

    def ps(stk, name, shape, dt=F32):
        return stk.enter_context(nc.psum_tensor(name, shape, dt))

    with contextlib.ExitStack() as est:
        S1 = est.enter_context(contextlib.ExitStack())
        c.ident_b = sb(S1, "ident_b", [128, 128], BF16)
        c.ident_f = sb(S1, "ident_f", [128, 128], F32)
        c.tri_neg = sb(S1, "tri_neg", [128, 128], F32)
        c.ones_row = sb(S1, "ones_row", [1, 128], F32)
        c.ones_col_b = sb(S1, "ones_col_b", [128, 1], BF16)
        c.half_neg = sb(S1, "half_neg", [128, 1], F32)
        c.half_pos = sb(S1, "half_pos", [128, 1], F32)
        c.eps_col = sb(S1, "eps_col", [128, 1], F32)
        for nm in ("c_sx", "c_ss", "c_mu", "c_bias", "c_std", "c_rstd",
                   "c_t1", "c_t2", "c_ell", "c_rinv",
                   "invs_gv", "invs_wo", "sgvf_b", "swof_b",
                   "c_am", "c_amc", "c_rs1", "c_f2"):
            setattr(c, nm, sb(S1, nm, [128, 1], F32))
        c.ell_segs = sb(S1, "ell_segs", [128, 8], F32)
        c.trow = sb(S1, "trow", [1, 128], F32)
        c.tsc = sb(S1, "tsc", [1, 4], F32)
        c.svals = sb(S1, "svals", [1, 2], F32)
        c.sfac = sb(S1, "sfac", [1, 4], F32)
        c.spsums = sb(S1, "spsums", [128, 96], F32)
        c.d_allc = sb(S1, "d_allc", [128, 8], F32)
        c.d2_allc = sb(S1, "d2_allc", [128, 8], F32)
        c.ln3mu = sb(S1, "ln3mu", [128, 8], F32)
        c.ln3rs = sb(S1, "ln3rs", [128, 8], F32)
        c.mks = sb(S1, "mks", [128, 4], F32)

        S2a = est.enter_context(contextlib.ExitStack())
        c.q_bf = sb(S2a, "q_bf", [128, 4 * T], BF16)
        c.k_bf = sb(S2a, "k_bf", [128, 4 * T], BF16)
        c.v_tm = sb(S2a, "v_tm", [128, 4 * T], BF16)
        c.bq_sb = sb(S2a, "bq_sb", [128, 12], F32)
        c.bias_tm = sb(S2a, "bias_tm", [128, 128], F32)
        c.normsb = sb(S2a, "normsb", [128, 256], F32)

        S3 = est.enter_context(contextlib.ExitStack())
        c.x_in = [sb(S3, f"x_in{i}", [128, D], F32) for i in range(2)]
        c.x1g = sb(S3, "x1g", [128, 2 * D], BF16)
        c.x1_fm = sb(S3, "x1_fm", [128, 16 * 512], BF16)
        c.wq_f32 = sb(S3, "wq_f32", [128, 1536], F32)
        c.wqkv_bf = sb(S3, "wqkv_bf", [128, 16 * 1536], BF16)
        c.v_stage = sb(S3, "v_stage", [128, 512], BF16)
        c.sq_stage = sb(S3, "sq_stage", [128, 512], BF16)
        c.sp_buf = [sb(S3, f"sp_buf{i}", [128, 512], F32) for i in range(2)]

        PS0 = est.enter_context(contextlib.ExitStack())
        c.ps_qkv = [ps(PS0, f"ps_qkv{i}", [128, 512]) for i in range(2)]
        c.ps_t = [ps(PS0, f"ps_t{i}", [128, 512], BF16) for i in range(2)]
        c.ps_norm = ps(PS0, "ps_norm", [128, 256])
        c.ps_misc = ps(PS0, "ps_misc", [128, 128])

        p = Plan(nc, sems, cnt)
        emit_consts(p, c)
        emit_p0(p, c)
        p.run("p0")

        if stop_after == "p0":
            return nc
        S3.close()

        S2b = est.enter_context(contextlib.ExitStack())
        c.p_raw = sb(S2b, "p_raw", [128, T], F32)
        c.woutc_bf = sb(S2b, "woutc_bf", [128, 4 * D], BF16)
        c.p_nrm = [sb(S2b, f"p_nrm{l}", [128, T], BF16) for l in range(4)]
        c.pt_sb = [sb(S2b, f"pt_sb{i}", [128, 512], BF16) for i in range(2)]
        c.av_sb = [sb(S2b, f"av_sb{i}", [128, 4 * 512], BF16) for i in range(2)]
        c.wo_stage = [sb(S2b, f"wo_stage{i}", [128, D], BF16) for i in range(2)]
        c.tz_f32 = [sb(S2b, f"tz_f32{i}", [128, 512], F32) for i in range(2)]
        c.tz_s1 = sb(S2b, "tz_s1", [128, 512], BF16)
        c.tz_out = [sb(S2b, f"tz_out{i}", [128, 512], BF16) for i in range(2)]

        PS0.close()
        PS1 = est.enter_context(contextlib.ExitStack())
        c.ps_s = [ps(PS1, f"ps_s{i}", [128, 512]) for i in range(2)]
        c.ps_pt = [ps(PS1, f"ps_pt{i}", [128, 512], BF16) for i in range(2)]
        c.ps_av = [ps(PS1, f"ps_av{i}", [128, 512]) for i in range(2)]
        c.ps_w = [ps(PS1, f"ps_w{i}", [128, 512]) for i in range(2)]

        p = Plan(nc, sems, cnt)
        emit_p1(p, c)
        p.run("p1")

        if stop_after == "p1":
            return nc
        S2b.close()
        S2a.close()
        PS1.close()

        S4 = est.enter_context(contextlib.ExitStack())
        c.r1 = sb(S4, "r1", [128, 8 * D], F32)

        S4a1 = est.enter_context(contextlib.ExitStack())
        c.q1_fm = sb(S4a1, "q1_fm", [128, 16 * 1024], BF16)
        S4a = est.enter_context(contextlib.ExitStack())
        c.bout_b = sb(S4a, "bout_b", [128, D], F32)
        c.gamma_b = sb(S4a, "gamma_b", [128, D], F32)
        c.ln_t1 = sb(S4a, "ln_t1", [128, D], F32)
        c.ln_t2 = sb(S4a, "ln_t2", [128, D], F32)
        c.rsx = [sb(S4a, f"rsx{i}", [128, D], BF16) for i in range(2)]
        c.xo_f32 = [sb(S4a, f"xo_f32{i}", [128, D], F32) for i in range(2)]
        c.q1_bf = sb(S4a, "q1_bf", [128, D], BF16)

        PS2 = est.enter_context(contextlib.ExitStack())
        c.ps_a = [ps(PS2, f"ps_a{i}", [128, 512]) for i in range(2)]
        c.ps_t2 = [ps(PS2, f"ps_t2{i}", [128, 512], BF16) for i in range(2)]

        p = Plan(nc, sems, cnt)
        emit_p2a(p, c)

        S4a.close()
        S4b = est.enter_context(contextlib.ExitStack())
        c.pwh = [sb(S4b, f"pwh{i}", [128, 8 * 1024], BF16) for i in range(3)]
        c.silu_sb = sb(S4b, "silu_sb", [128, 512], F32)
        c.h2st = [sb(S4b, f"h2st{i}", [128, 512], F32) for i in range(2)]

        emit_p2b(p, c)
        p.run("p2ab")
        if stop_after == "p2b":
            return nc
        S4b.close()
        S4a1.close()

        S4c = est.enter_context(contextlib.ExitStack())
        c.h2l = [sb(S4c, f"h2l{i}", [128, 4096], F32) for i in range(2)]
        c.q2_bf = sb(S4c, "q2_bf", [128, HID], BF16)
        c.q2fm_st = sb(S4c, "q2fm_st", [128, HID], BF16)

        p = Plan(nc, sems, cnt)
        emit_p2c(p, c)
        p.run("p2c")
        S4c.close()

        S4d1 = est.enter_context(contextlib.ExitStack())
        c.wD = sb(S4d1, "wD", [128, 64 * 512], BF16)
        c.q2sb = [sb(S4d1, f"q2sb{i}", [128, HID], BF16) for i in range(2)]
        c.o2st = [sb(S4d1, f"o2st{i}", [128, 512], F32) for i in range(2)]

        p = Plan(nc, sems, cnt)
        emit_p2d(p, c)
        p.run("p2d")
        S4d1.close()

        S4d2 = est.enter_context(contextlib.ExitStack())
        c.o2sb = [sb(S4d2, f"o2sb{i}", [128, D], F32) for i in range(2)]
        c.y_st = [sb(S4d2, f"y_st{i}", [128, D], F32) for i in range(2)]
        c.tl_t1 = sb(S4d2, "tl_t1", [128, D], F32)
        c.tl_t2 = sb(S4d2, "tl_t2", [128, D], F32)

        p = Plan(nc, sems, cnt)
        emit_tail(p, c)
        p.run("tail")
        S4d2.close()
        PS2.close()

    return nc


def emit_consts(p, c):
    gsem = "cc"
    p.i("gpsimd", lambda e: e.memset(c.ident_b[:], 0.0), gsem, 1)
    p.w("gpsimd", gsem)
    p.i("gpsimd", lambda e: e.affine_select(
        out=c.ident_b[:], in_=c.ident_b[:], compare_op=ALU.not_equal,
        fill=1.0, base=0, pattern=[[-1, 128]], channel_multiplier=1), gsem, 1)
    p.i("gpsimd", lambda e: e.memset(c.ident_f[:], 0.0), gsem, 1)
    p.w("gpsimd", gsem)
    p.i("gpsimd", lambda e: e.affine_select(
        out=c.ident_f[:], in_=c.ident_f[:], compare_op=ALU.not_equal,
        fill=1.0, base=0, pattern=[[-1, 128]], channel_multiplier=1), gsem, 1)
    p.i("gpsimd", lambda e: e.memset(c.tri_neg[:], 0.0), gsem, 1)
    p.w("gpsimd", gsem)
    p.i("gpsimd", lambda e: e.affine_select(
        out=c.tri_neg[:], in_=c.tri_neg[:], compare_op=ALU.is_ge,
        fill=NEG, base=0, pattern=[[-1, 128]], channel_multiplier=1), gsem, 1)
    p.i("gpsimd", lambda e: e.memset(c.ones_row[:], 1.0), gsem, 1)
    p.i("gpsimd", lambda e: e.memset(c.half_neg[:], -0.5), gsem, 1)
    p.i("gpsimd", lambda e: e.memset(c.half_pos[:], 0.5), gsem, 1)
    p.i("gpsimd", lambda e: e.memset(c.eps_col[:], EPS), gsem, 1)
    p.i("gpsimd", lambda e: e.memset(c.ones_col_b[:], 1.0), gsem, 1)
    c.CC_CONSTS = p.cnt["cc"]


def _ln_stats(p, c, src_ap, sq_out_ap, inv_n, eps=EPS):
    p.i("vector", lambda e: e.reduce_sum(out=c.c_sx[:], in_=src_ap, axis=AX.X), "dv")
    sq_ac = p.i("scalar", lambda e: e.activation(out=sq_out_ap, in_=src_ap, func=AF.Square,
                                                 accum_out=c.c_ss[:]), "ac")
    p.sw("vector", "dv")
    p.i("vector", lambda e: e.tensor_scalar(out=c.c_mu[:], in0=c.c_sx[:],
                                            scalar1=-inv_n, scalar2=None,
                                            op0=ALU.mult), "dv")
    p.sw("vector", "dv")
    p.i("vector", lambda e: e.tensor_tensor(out=c.c_t1[:], in0=c.c_mu[:],
                                            in1=c.c_mu[:], op=ALU.mult), "dv")
    p.w("vector", "ac", getattr(c, "_sqrt_ac", 0))
    p.sw("vector", "dv")
    dv_b = p.i("vector", lambda e: e.tensor_scalar(out=c.c_bias[:], in0=c.c_t1[:],
                                                   scalar1=-1.0, scalar2=eps,
                                                   op0=ALU.mult, op1=ALU.add), "dv")
    p.w("scalar", "dv", dv_b)
    p.w("scalar", "dv", getattr(c, "_recip_dv", 0))
    p.w("scalar", "ac", sq_ac)
    ac_s = p.i("scalar", lambda e: e.activation(out=c.c_std[:], in_=c.c_ss[:],
                                                func=AF.Sqrt, bias=c.c_bias[:],
                                                scale=inv_n), "ac")
    c._sqrt_ac = ac_s
    p.w("vector", "ac", ac_s)
    r = p.i("vector", lambda e: e.reciprocal(out=c.c_rstd[:], in_=c.c_std[:]), "dv")
    c._recip_dv = r
    p.sw("vector", "dv")
    return r


def emit_p0(p, c):
    cnt = p.cnt

    for fc in range(12):
        p.i("sync", lambda e, fc=fc: e.dma_start(
            out=c.bq_sb[:, fc:fc + 1],
            in_=c.d_bqkv[fc * 128:(fc + 1) * 128]), "ld", 16)
    LD_BQ = p.cnt["ld"]
    p.w("scalar", "ld", LD_BQ)

    wq_rel = 0
    for dc in range(16):
        p.w("sync", "dv", wq_rel)
        p.i("sync", lambda e, dc=dc: e.dma_start(
            out=c.wq_f32[:, 0:1536],
            in_=c.d_wqkv[dc * 128:(dc + 1) * 128, :]), "ld", 16)
        p.w("vector", "ld")
        wq_rel = p.i("vector", lambda e, dc=dc: e.tensor_copy(
            out=c.wqkv_bf[:, dc * 1536:(dc + 1) * 1536],
            in_=c.wq_f32[:, 0:1536]), "dv")
    p.w("tensor", "cc", c.CC_CONSTS)

    x_rel = [(0, 0), (0, 0)]
    x1g_rel = 0
    x1fm_rel = 0
    pst_rel = [0, 0]
    psqkv_rel = [0, 0]
    sq_rel = 0
    vst_rel = 0
    spi = 0
    sp_rel = [0, 0]

    for sc in range(8):
        evac_done = 0
        for half in range(2):
            for tcl2 in range(2):
                tcl = half * 2 + tcl2
                tc = sc * 4 + tcl
                b = tc % 2
                p.w("sync", "dv", x_rel[b][0])
                p.w("sync", "ac", x_rel[b][1])
                ldx = p.i("sync", lambda e, tc=tc, b=b: e.dma_start(
                    out=c.x_in[b][:], in_=c.d_xb[tc * 128:(tc + 1) * 128, :]), "ld", 16)
                p.w("vector", "ld", ldx)
                p.w("scalar", "ld", ldx)
                p.w("scalar", "pe", x1g_rel)
                _ln_stats(p, c, c.x_in[b][:], c.x1g[:, tcl2 * D:(tcl2 + 1) * D], 1.0 / D)
                ac_used = cnt["ac"]
                p.w("vector", "pe", x1g_rel)
                dv_ap = p.i("vector", lambda e, b=b, tcl2=tcl2: e.tensor_scalar(
                    out=c.x1g[:, tcl2 * D:(tcl2 + 1) * D], in0=c.x_in[b][:],
                    scalar1=c.c_mu[:], scalar2=c.c_rstd[:],
                    op0=ALU.add, op1=ALU.mult), "dv")
                x_rel[b] = (dv_ap, ac_used)
            dv_half = cnt["dv"]

            p.w("tensor", "dv", dv_half)
            if half == 0:
                p.w("tensor", "pe", x1fm_rel)
            for dc in range(16):
                bb = dc % 2
                p.w("tensor", "dv", pst_rel[bb])
                for tcl2 in range(2):
                    fn = lambda e, dc=dc, tcl2=tcl2, bb=bb: e.transpose(
                        c.ps_t[bb][:, tcl2 * 128:(tcl2 + 1) * 128],
                        c.x1g[:, tcl2 * D + dc * 128: tcl2 * D + (dc + 1) * 128],
                        c.ident_b[:])
                    if tcl2 == 0:
                        p.q("tensor", fn)
                    else:
                        pe_t = p.i("tensor", fn, "pe")
                p.w("vector", "pe", pe_t)
                pst_rel[bb] = p.i("vector", lambda e, dc=dc, half=half, bb=bb: e.tensor_copy(
                    out=c.x1_fm[:, dc * 512 + half * 256: dc * 512 + half * 256 + 256],
                    in_=c.ps_t[bb][:, 0:256]), "dv")
            x1g_rel = cnt["pe"]
        evac_done = cnt["dv"]

        for k in range(12):
            i = spi
            spi += 1
            b = i % 2
            if i < 64:
                pc, ct = i // 32, i % 32
                src = c.d_wgv[pc * 128:(pc + 1) * 128, ct * 512:(ct + 1) * 512]
            else:
                jj = i - 64
                pc, ct = jj // 4, jj % 4
                src = c.d_wo[pc * 128:(pc + 1) * 128, ct * 512:(ct + 1) * 512]
            p.w("scalar", "dv", sp_rel[b])
            p.i("scalar", lambda e, src=src, b=b: e.dma_start(
                out=c.sp_buf[b][:], in_=src), "wd", 16)
            p.w("vector", "wd")
            sp_rel[b] = p.i("vector", lambda e, i=i, b=b: e.reduce_sum(
                out=c.spsums[:, i:i + 1], in_=c.sp_buf[b][:], axis=AX.X,
                apply_absolute_value=True), "dv")

        p.w("tensor", "dv", evac_done)
        for fc in range(12):
            bb = fc % 2
            p.w("tensor", "ac", psqkv_rel[bb])
            for dc in range(16):
                fn = lambda e, fc=fc, dc=dc, bb=bb: e.matmul(
                    c.ps_qkv[bb][:],
                    c.wqkv_bf[:, dc * 1536 + fc * 128: dc * 1536 + (fc + 1) * 128],
                    c.x1_fm[:, dc * 512:(dc + 1) * 512],
                    start=(dc == 0), stop=(dc == 15))
                if dc < 15:
                    p.q("tensor", fn)
                else:
                    pe_mm = p.i("tensor", fn, "pe")
            h, role = fc % 4, fc // 4
            if role == 0:
                slot = c.q_bf[:, h * T + sc * 512: h * T + sc * 512 + 512]
            elif role == 1:
                slot = c.k_bf[:, h * T + sc * 512: h * T + sc * 512 + 512]
            else:
                slot = c.v_stage[:]
                p.w("scalar", "pe", vst_rel)
            p.w("scalar", "pe", pe_mm)
            ac_ev = p.i("scalar", lambda e, slot=slot, fc=fc, bb=bb: e.activation(
                out=slot, in_=c.ps_qkv[bb][:], func=AF.Identity,
                bias=c.bq_sb[:, fc:fc + 1]), "ac")
            if role < 2:
                p.w("scalar", "pe", sq_rel)
                ac_ev = p.i("scalar", lambda e, fc=fc, bb=bb: e.activation(
                    out=c.sq_stage[:], in_=c.ps_qkv[bb][:], func=AF.Square,
                    bias=c.bq_sb[:, fc:fc + 1]), "ac")
                p.w("tensor", "ac", ac_ev)
                for kk in range(4):
                    col = (role * 4 + h) * 32 + sc * 4 + kk
                    fn = lambda e, kk=kk, col=col: e.matmul(
                        c.ps_norm[:, col:col + 1],
                        c.sq_stage[:, kk * 128:(kk + 1) * 128],
                        c.ones_col_b[:], start=True, stop=True)
                    if kk < 3:
                        p.q("tensor", fn)
                    else:
                        sq_rel = p.i("tensor", fn, "pe")
            else:
                p.w("tensor", "ac", ac_ev)
                tb = fc % 2
                p.w("tensor", "dv", pst_rel[tb])
                for kk in range(4):
                    fn = lambda e, kk=kk, tb=tb: e.transpose(
                        c.ps_t[tb][:, kk * 128:(kk + 1) * 128],
                        c.v_stage[:, kk * 128:(kk + 1) * 128], c.ident_b[:])
                    if kk < 3:
                        p.q("tensor", fn)
                    else:
                        pe_vt = p.i("tensor", fn, "pe")
                vst_rel = pe_vt
                p.w("vector", "pe", pe_vt)
                pst_rel[tb] = p.i("vector", lambda e, h=h, sc=sc, tb=tb: e.tensor_copy(
                    out=c.v_tm[:, (h * 32 + sc * 4) * 128:(h * 32 + sc * 4 + 4) * 128],
                    in_=c.ps_t[tb][:]), "dv")
            psqkv_rel[bb] = cnt["ac"]
        x1fm_rel = cnt["pe"]

    p.w("scalar", "pe")
    ac_n = p.i("scalar", lambda e: e.activation(
        out=c.normsb[:], in_=c.ps_norm[:], func=AF.Sqrt), "ac")
    dv_mk_prev = 0
    for h in range(4):
        p.w("vector", "ac", ac_n)
        dv_m = p.i("vector", lambda e, h=h: e.reduce_max(
            out=c.c_t1[:], in_=c.normsb[:, (4 + h) * 32:(5 + h) * 32], axis=AX.X), "dv")
        p.w("tensor", "dv", dv_m)
        p.w("tensor", "dv", dv_mk_prev)
        pe_tr = p.i("tensor", lambda e: e.transpose(
            c.ps_misc[0:1, 0:128], c.c_t1[:], c.ident_f[:]), "pe")
        p.w("vector", "pe", pe_tr)
        p.i("vector", lambda e: e.tensor_copy(out=c.trow[:],
                                              in_=c.ps_misc[0:1, 0:128]), "dv")
        p.sw("vector", "dv")
        p.i("vector", lambda e: e.reduce_max(out=c.tsc[0:1, 0:1], in_=c.trow[:],
                                             axis=AX.X), "dv")
        p.sw("vector", "dv")
        dv_r = p.i("vector", lambda e: e.tensor_scalar(
            out=c.tsc[0:1, 1:2], in0=c.tsc[0:1, 0:1], scalar1=-SCALE, scalar2=None,
            op0=ALU.mult), "dv")
        p.w("tensor", "dv", dv_r)
        pe_b = p.i("tensor", lambda e: e.matmul(
            c.ps_misc[:, 0:1], c.ones_row[:], c.tsc[0:1, 1:2],
            start=True, stop=True), "pe")
        p.w("vector", "pe", pe_b)
        dv_mk_prev = p.i("vector", lambda e, h=h: e.tensor_copy(
            out=c.mks[:, h:h + 1], in_=c.ps_misc[:, 0:1]), "dv")
        p.sw("vector", "dv")
        p.i("vector", lambda e, h=h: e.tensor_scalar(
            out=c.bias_tm[:, h * 32:(h + 1) * 32],
            in0=c.normsb[:, h * 32:(h + 1) * 32],
            scalar1=c.mks[:, h:h + 1], scalar2=None, op0=ALU.mult), "dv")

    p.i("vector", lambda e: e.reduce_sum(out=c.c_t1[:], in_=c.spsums[:, 0:64],
                                         axis=AX.X), "dv")
    dv_s0 = p.i("vector", lambda e: e.reduce_sum(out=c.c_t2[:], in_=c.spsums[:, 64:96],
                                                 axis=AX.X), "dv")
    p.w("tensor", "dv", dv_s0)
    p.w("tensor", "dv", dv_mk_prev)
    pe_t1 = p.i("tensor", lambda e: e.transpose(
        c.ps_misc[0:1, 0:128], c.c_t1[:], c.ident_f[:]), "pe")
    p.w("vector", "pe", pe_t1)
    p.i("vector", lambda e: e.tensor_copy(out=c.trow[:], in_=c.ps_misc[0:1, 0:128]), "dv")
    p.sw("vector", "dv")
    dv_s1 = p.i("vector", lambda e: e.reduce_sum(
        out=c.svals[0:1, 0:1], in_=c.trow[:], axis=AX.X), "dv")
    p.w("tensor", "dv", dv_s1)
    pe_t2 = p.i("tensor", lambda e: e.transpose(
        c.ps_misc[0:1, 0:128], c.c_t2[:], c.ident_f[:]), "pe")
    p.w("vector", "pe", pe_t2)
    p.i("vector", lambda e: e.tensor_copy(out=c.trow[:], in_=c.ps_misc[0:1, 0:128]), "dv")
    p.sw("vector", "dv")
    dv_s2 = p.i("vector", lambda e: e.reduce_sum(
        out=c.svals[0:1, 1:2], in_=c.trow[:], axis=AX.X), "dv")
    p.w("gpsimd", "dv", dv_s2)
    p.i("gpsimd", lambda e: e.dma_start(out=c.d_ar_in, in_=c.svals[:]), "gd", 16)
    p.w("gpsimd", "gd")
    p.i("gpsimd", lambda e: e.collective_compute(
        "AllReduce", ALU.add, replica_groups=G8,
        ins=[c.d_ar_in], outs=[c.d_ar_out]), "cc", 1)
    p.w("gpsimd", "cc")
    p.i("gpsimd", lambda e: e.dma_start(out=c.svals[:], in_=c.d_ar_out), "gd", 16)
    c.GD_AR = cnt["gd"]
    p.w("vector", "gd", c.GD_AR)
    p.w("vector", "dv")
    p.i("vector", lambda e: e.reciprocal(out=c.tsc[0:1, 0:2], in_=c.svals[0:1, 0:2]), "dv")
    p.sw("vector", "dv")
    p.i("vector", lambda e: e.tensor_scalar(out=c.sfac[0:1, 0:1], in0=c.tsc[0:1, 0:1],
                                            scalar1=N_GV, scalar2=None, op0=ALU.mult), "dv")
    p.i("vector", lambda e: e.tensor_scalar(out=c.sfac[0:1, 1:2], in0=c.tsc[0:1, 1:2],
                                            scalar1=N_WO, scalar2=None, op0=ALU.mult), "dv")
    p.i("vector", lambda e: e.tensor_scalar(out=c.sfac[0:1, 2:3], in0=c.svals[0:1, 0:1],
                                            scalar1=1.0 / (254.0 * N_GV), scalar2=None,
                                            op0=ALU.mult), "dv")
    dv_f = p.i("vector", lambda e: e.tensor_scalar(
        out=c.sfac[0:1, 3:4], in0=c.svals[0:1, 1:2],
        scalar1=1.0 / (254.0 * N_WO), scalar2=None, op0=ALU.mult), "dv")
    p.w("tensor", "dv", dv_f)
    p.w("tensor", "dv", dv_mk_prev)
    for i, dst in enumerate((c.invs_gv, c.invs_wo, c.sgvf_b, c.swof_b)):
        pe_bb = p.i("tensor", lambda e, i=i: e.matmul(
            c.ps_misc[:, i:i + 1], c.ones_row[:], c.sfac[0:1, i:i + 1],
            start=True, stop=True), "pe")
        p.w("vector", "pe", pe_bb)
        p.i("vector", lambda e, dst=dst, i=i: e.tensor_copy(
            out=dst[:], in_=c.ps_misc[:, i:i + 1]), "dv")
    c.DV_FAC = cnt["dv"]

    if c.debug:
        dv_all = cnt["dv"]
        p.w("gpsimd", "dv", dv_all)
        p.w("gpsimd", "ac")
        for nm, src in [("dbg_q", c.q_bf), ("dbg_k", c.k_bf), ("dbg_v", c.v_tm)]:
            p.i("gpsimd", lambda e, nm=nm, src=src: e.dma_start(
                out=c.dbg[nm], in_=src[:]), "gd", 16)
        p.i("gpsimd", lambda e: e.dma_start(out=c.dbg["dbg_bias"][0:128, 0:128],
                                            in_=c.bias_tm[:]), "gd", 16)
        p.i("gpsimd", lambda e: e.dma_start(out=c.dbg["dbg_sfac"], in_=c.sfac[:]), "gd", 16)


def emit_p1(p, c):
    cnt = p.cnt
    wc_rel = 0
    for hc in range(4):
        p.w("sync", "dv", wc_rel)
        p.i("sync", lambda e, hc=hc: e.dma_start(
            out=c.p_raw[:, 0:2048],
            in_=c.d_woutc[hc * 128:(hc + 1) * 128, :]), "ld", 16)
        p.w("vector", "ld")
        wc_rel = p.i("vector", lambda e, hc=hc: e.tensor_copy(
            out=c.woutc_bf[:, hc * D:(hc + 1) * D], in_=c.p_raw[:, 0:2048]), "dv")
    z = []
    for buf in [c.p_raw] + c.p_nrm:
        z.append(p.i("vector", lambda e, buf=buf: e.memset(buf[:], 0.0), "dv"))
    dv_zero = cnt["dv"]

    tz_i = [0]
    tzf_rel = [(0, 0), (0, 0)]
    tzo_rel = [0, 0]
    ag_issued = [False]

    def tern_tiles(n):
        if c.no_tern:
            c.CC_AG = p.cnt["cc"]
            return
        for _ in range(n):
            i = tz_i[0]
            if i >= 96:
                if not ag_issued[0]:
                    ag_issued[0] = True
                    p.w("gpsimd", "ld")
                    p.w("gpsimd", "gd")
                    p.i("gpsimd", lambda e: e.collective_compute(
                        "AllGather", ALU.bypass, replica_groups=G8,
                        ins=[c.d_tgv_in], outs=[c.d_tgv_out]), "cc", 1)
                    p.i("gpsimd", lambda e: e.collective_compute(
                        "AllGather", ALU.bypass, replica_groups=G8,
                        ins=[c.d_two_in], outs=[c.d_two_out]), "cc", 1)
                    c.CC_AG = cnt["cc"]
                return
            tz_i[0] += 1
            b = i % 2
            if i < 64:
                pc, ct = i // 32, i % 32
                src = c.d_wgv[pc * 128:(pc + 1) * 128, ct * 512:(ct + 1) * 512]
                dst = c.d_tgv_in[pc * 128:(pc + 1) * 128, ct * 512:(ct + 1) * 512]
                inv = c.invs_gv
            else:
                jj = i - 64
                pc, ct = jj // 4, jj % 4
                src = c.d_wo[pc * 128:(pc + 1) * 128, ct * 512:(ct + 1) * 512]
                dst = c.d_two_in[pc * 128:(pc + 1) * 128, ct * 512:(ct + 1) * 512]
                inv = c.invs_wo
            p.w("scalar", "ac", tzf_rel[b][1])
            ldw = p.i("scalar", lambda e, src=src, b=b: e.dma_start(
                out=c.tz_f32[b][:], in_=src), "wd", 16)
            p.w("scalar", "wd", ldw)
            if i == 0:
                p.w("scalar", "dv", c.DV_FAC)
            if c.tern_mode == 5:
                p.i("scalar", lambda e, b=b: e.activation(
                    out=c.tz_s1[:], in_=c.tz_f32[b][:], func=AF.Sign), "ac")
                p.w("scalar", "ld", tzo_rel[b])
                ac_s2 = p.i("scalar", lambda e, b=b: e.activation(
                    out=c.tz_out[b][:], in_=c.tz_f32[b][:], func=AF.Sign), "ac")
            elif c.tern_mode == 6:
                p.i("scalar", lambda e, b=b, inv=inv: e.activation(
                    out=c.tz_s1[:], in_=c.tz_f32[b][:], func=AF.Sign,
                    scale=inv[:]), "ac")
                p.w("scalar", "ld", tzo_rel[b])
                ac_s2 = p.i("scalar", lambda e, b=b, inv=inv: e.activation(
                    out=c.tz_out[b][:], in_=c.tz_f32[b][:], func=AF.Sign,
                    scale=inv[:]), "ac")
            elif c.tern_mode == 7:
                p.i("scalar", lambda e, b=b: e.activation(
                    out=c.tz_s1[:], in_=c.tz_f32[b][:], func=AF.Identity,
                    bias=c.half_neg[:]), "ac")
                p.w("scalar", "ld", tzo_rel[b])
                ac_s2 = p.i("scalar", lambda e, b=b: e.activation(
                    out=c.tz_out[b][:], in_=c.tz_f32[b][:], func=AF.Identity,
                    bias=c.half_pos[:]), "ac")
            elif c.tern_mode >= 2:
                p.i("scalar", lambda e, b=b, inv=inv: e.activation(
                    out=c.tz_s1[:], in_=c.tz_f32[b][:], func=AF.Sign,
                    scale=inv[:], bias=c.half_neg[:]), "ac")
                p.w("scalar", "ld", tzo_rel[b])
                ac_s2 = p.i("scalar", lambda e, b=b, inv=inv: e.activation(
                    out=c.tz_out[b][:], in_=c.tz_f32[b][:], func=AF.Sign,
                    scale=inv[:], bias=c.half_pos[:]), "ac")
            else:
                p.w("scalar", "ld", tzo_rel[b])
                ac_s2 = p.i("scalar", lambda e, b=b: e.activation(
                    out=c.tz_out[b][:], in_=c.tz_f32[b][:], func=AF.Identity), "ac")
            tzf_rel[b] = (0, ac_s2)
            p.w("vector", "ac", ac_s2)
            if c.tern_mode >= 3:
                dv_t = p.i("vector", lambda e, b=b: e.tensor_tensor(
                    out=c.tz_out[b][:], in0=c.tz_s1[:], in1=c.tz_out[b][:],
                    op=ALU.add), "dv")
            else:
                dv_t = p.i("vector", lambda e, b=b: e.tensor_scalar(
                    out=c.tz_out[b][:], in0=c.tz_out[b][:], scalar1=1.0,
                    scalar2=None, op0=ALU.mult), "dv")
            p.w("sync", "dv", dv_t)
            tzo_rel[b] = p.i("sync", lambda e, b=b, dst=dst: e.dma_start(
                out=dst, in_=c.tz_out[b][:]), "ld", 16)

    praw_rel = 0
    pnrm_rel = [0, 0, 0, 0]
    pss_rel = [0, 0]
    pspt_rel = [0, 0]
    ptsb_rel = [0, 0]
    psav_rel = [0, 0]
    avsb_rel = [0, 0]
    psw_rel = [0, 0]
    wost_rel = [0, 0]

    si = [0]
    ti = [0]
    p.w("tensor", "dv", dv_zero)

    for j in range(8):
        for h in range(4):
            av_i = (j * 4 + h) % 2
            for l in range(4):
                qc = 4 * j + l
                nseg = j + 1
                exp_acs = []
                for seg in range(nseg):
                    n = 512 if seg < j else 128 * (l + 1)
                    b = si[0] % 2
                    si[0] += 1
                    p.w("tensor", "ac", pss_rel[b])
                    pe_s = p.i("tensor", lambda e, h=h, qc=qc, seg=seg, n=n, b=b: e.matmul(
                        c.ps_s[b][:, 0:n],
                        c.q_bf[:, h * T + qc * 128:h * T + (qc + 1) * 128],
                        c.k_bf[:, h * T + seg * 512: h * T + seg * 512 + n],
                        start=True, stop=True), "pe")
                    if c.debug and j == 7 and h == 0 and l == 0 and seg == 0:
                        p.w("vector", "pe", pe_s)
                        dv_tap = p.i("vector", lambda e, b=b: e.tensor_copy(
                            out=c.wo_stage[0][:, 0:512], in_=c.ps_s[b][:]), "dv")
                        p.w("gpsimd", "dv", dv_tap)
                        p.i("gpsimd", lambda e: e.dma_start(
                            out=c.dbg["dbg_s"][0:128, 0:256],
                            in_=c.wo_stage[0][:, 0:256]), "gd", 16)
                        p.i("gpsimd", lambda e, h=h, qc=qc: e.dma_start(
                            out=c.dbg["dbg_qs"],
                            in_=c.q_bf[:, h * T + qc * 128: h * T + (qc + 1) * 128]), "gd", 16)
                        gd_t = p.i("gpsimd", lambda e, h=h: e.dma_start(
                            out=c.dbg["dbg_ks"],
                            in_=c.k_bf[:, h * T: h * T + 512]), "gd", 16)
                        p.w("scalar", "gd", gd_t)
                        p.w("tensor", "gd", gd_t)
                    if seg == j:
                        p.w("vector", "pe", pe_s)
                        dv_msk = p.i("vector", lambda e, b=b, l=l: e.tensor_tensor(
                            out=c.ps_s[b][:, l * 128:(l + 1) * 128],
                            in0=c.ps_s[b][:, l * 128:(l + 1) * 128],
                            in1=c.tri_neg[:], op=ALU.add), "dv")
                        p.w("scalar", "dv", dv_msk)
                    p.w("scalar", "pe", pe_s)
                    if seg == 0:
                        p.w("scalar", "dv", praw_rel)
                    pss_rel[b] = p.i("scalar", lambda e, b=b, seg=seg, n=n, h=h, qc=qc: e.activation(
                        out=c.p_raw[:, seg * 512: seg * 512 + n],
                        in_=c.ps_s[b][:, 0:n], func=AF.Exp,
                        bias=c.bias_tm[:, h * 32 + qc: h * 32 + qc + 1],
                        scale=SCALE, accum_out=c.ell_segs[:, seg:seg + 1]), "ac")
                ac_exp = cnt["ac"]
                p.w("vector", "ac", ac_exp)
                p.i("vector", lambda e, nseg=nseg: e.reduce_sum(
                    out=c.c_ell[:], in_=c.ell_segs[:, 0:nseg], axis=AX.X), "dv")
                p.w("vector", "dv")
                p.i("vector", lambda e: e.reciprocal(out=c.c_rinv[:], in_=c.c_ell[:]), "dv")
                p.sw("vector", "dv")
                ext = 512 * j + 128 * (l + 1)
                p.w("vector", "pe", pnrm_rel[l])
                praw_rel = p.i("vector", lambda e, l=l, ext=ext: e.tensor_scalar(
                    out=c.p_nrm[l][:, 0:ext], in0=c.p_raw[:, 0:ext],
                    scalar1=c.c_rinv[:], scalar2=None, op0=ALU.mult), "dv")
            dv_nrm = cnt["dv"]
            if c.debug and j == 7 and h == 0:
                p.w("gpsimd", "dv", dv_nrm)
                for l in range(4):
                    p.i("gpsimd", lambda e, l=l: e.dma_start(
                        out=c.dbg["dbg_pnh0"][l], in_=c.p_nrm[l][:]), "gd", 16)
                gd_tap = p.i("gpsimd", lambda e: e.dma_start(
                    out=c.dbg["dbg_prh0"], in_=c.p_raw[:]), "gd", 16)
                p.w("tensor", "gd", gd_tap)
                p.w("scalar", "gd", gd_tap)

            p.w("tensor", "dv", dv_nrm)
            p.w("tensor", "ac", psav_rel[av_i])
            first = True
            for seg in range(j + 1):
                for kc in range(4):
                    tb = ti[0] % 2
                    ti[0] += 1
                    p.w("tensor", "dv", pspt_rel[tb])
                    for l in range(4):
                        fn = lambda e, seg=seg, kc=kc, l=l, tb=tb: e.transpose(
                            c.ps_pt[tb][:, l * 128:(l + 1) * 128],
                            c.p_nrm[l][:, seg * 512 + kc * 128: seg * 512 + (kc + 1) * 128],
                            c.ident_b[:])
                        if l < 3:
                            p.q("tensor", fn)
                        else:
                            pe_pt = p.i("tensor", fn, "pe")
                    p.w("vector", "pe", pe_pt)
                    p.w("vector", "pe", ptsb_rel[tb])
                    pspt_rel[tb] = p.i("vector", lambda e, tb=tb: e.tensor_copy(
                        out=c.pt_sb[tb][:], in_=c.ps_pt[tb][:]), "dv")
                    p.w("tensor", "dv", pspt_rel[tb])
                    kci = seg * 4 + kc
                    last = (seg == j and kc == 3)
                    fn = lambda e, h=h, kci=kci, tb=tb, first=first, last=last, av_i=av_i: e.matmul(
                        c.ps_av[av_i][:],
                        c.v_tm[:, (h * 32 + kci) * 128:(h * 32 + kci + 1) * 128],
                        c.pt_sb[tb][:], start=first, stop=last,
                        skip_group_check=True)
                    ptsb_rel[tb] = p.i("tensor", fn, "pe")
                    first = False
            pnrm_rel = [cnt["pe"]] * 4
            p.w("scalar", "pe")
            if h == 0:
                p.w("scalar", "pe", avsb_rel[j % 2])
            psav_rel[av_i] = p.i("scalar", lambda e, h=h, av_i=av_i, j=j: e.activation(
                out=c.av_sb[j % 2][:, h * 512:(h + 1) * 512],
                in_=c.ps_av[av_i][:], func=AF.Copy), "ac")
            tern_tiles(4)
        ac_av = cnt["ac"]

        for tl in range(4):
            if c.no_wout:
                break
            wb = tl % 2
            p.w("tensor", "ac", ac_av)
            for fc in range(4):
                p.w("tensor", "dv", psw_rel[wb])
                for hc in range(4):
                    fn = lambda e, hc=hc, fc=fc, tl=tl, wb=wb, j=j: e.matmul(
                        c.ps_w[wb][:],
                        c.av_sb[j % 2][:, hc * 512 + tl * 128: hc * 512 + (tl + 1) * 128],
                        c.woutc_bf[:, hc * D + fc * 512: hc * D + fc * 512 + 512],
                        start=(hc == 0), stop=(hc == 3))
                    if hc < 3:
                        p.q("tensor", fn)
                    else:
                        pe_w = p.i("tensor", fn, "pe")
                p.w("vector", "pe", pe_w)
                p.w("vector", "gd", wost_rel[wb])
                psw_rel[wb] = p.i("vector", lambda e, fc=fc, wb=wb: e.tensor_copy(
                    out=c.wo_stage[wb][:, fc * 512:(fc + 1) * 512],
                    in_=c.ps_w[wb][:]), "dv")
            p.w("gpsimd", "dv", psw_rel[wb])
            row = j * 512 + tl * 128
            wost_rel[wb] = p.i("gpsimd", lambda e, row=row, wb=wb: e.dma_start(
                out=c.d_rs_in[row:row + 128, :], in_=c.wo_stage[wb][:]), "gd", 16)
        avsb_rel[j % 2] = cnt["pe"]

    tern_tiles(200)

    if c.debug:
        p.w("gpsimd", "ac")
        p.w("gpsimd", "dv")
        p.i("gpsimd", lambda e: e.dma_start(out=c.dbg["dbg_av"],
                                            in_=c.av_sb[7 % 2][:]), "gd", 16)
        for l in range(4):
            p.i("gpsimd", lambda e, l=l: e.dma_start(out=c.dbg[f"dbg_pn{l}"],
                                                     in_=c.p_nrm[l][:]), "gd", 16)
        p.i("gpsimd", lambda e: e.dma_start(out=c.dbg["dbg_praw"], in_=c.p_raw[:]), "gd", 16)
        p.i("gpsimd", lambda e: e.dma_start(out=c.dbg["dbg_ell"], in_=c.ell_segs[:]), "gd", 16)

    p.w("gpsimd", "gd")
    p.w("gpsimd", "ld")
    p.i("gpsimd", lambda e: e.collective_compute(
        "ReduceScatter", ALU.add, replica_groups=G4,
        ins=[c.d_rs_in], outs=[c.d_rs_out]), "cc", 1)
    c.CC_RS = cnt["cc"]


def emit_p2a(p, c):
    cnt = p.cnt
    p.w("sync", "cc", c.CC_RS)
    p.i("sync", lambda e: e.dma_start(out=c.bout_b[:], in_=c.d_bout), "ld", 16)
    p.i("sync", lambda e: e.dma_start(out=c.gamma_b[:], in_=c.d_gamma), "ld", 16)
    LD_BG = cnt["ld"]

    if c.debug:
        p.w("gpsimd", "cc", c.CC_RS)
        p.i("gpsimd", lambda e: e.dma_start(out=c.dbg["dbg_rso"], in_=c.d_rs_out), "gd", 16)
    rs_rel = [0, 0]
    xo_rel = [0, 0]
    pst2_rel = [0, 0]
    q1fm_parts = []

    p.w("sync", "cc", c.CC_RS)
    for tc in range(8):
        b = tc % 2
        p.w("sync", "dv", rs_rel[b])
        ld_rs = p.i("sync", lambda e, tc=tc, b=b: e.dma_start(
            out=c.rsx[b][:], in_=c.d_rs_out[tc * 128:(tc + 1) * 128, :]), "ld", 16)
        p.w("sync", "dv", xo_rel[b])
        ld_xo = p.i("sync", lambda e, tc=tc, b=b: e.dma_start(
            out=c.xo_f32[b][:], in_=c.d_xown[tc * 128:(tc + 1) * 128, :]), "ld", 16)
        p.w("vector", "ld", ld_rs)
        if tc == 0:
            p.w("vector", "ld", LD_BG)
        rs_rel[b] = p.i("vector", lambda e, b=b: e.tensor_tensor(
            out=c.ln_t1[:], in0=c.rsx[b][:], in1=c.bout_b[:], op=ALU.add), "dv")
        if c.debug and tc == 0:
            p.w("gpsimd", "dv")
            p.i("gpsimd", lambda e: e.dma_start(out=c.dbg["dbg_o1"], in_=c.ln_t1[:]), "gd", 16)
        p.w("scalar", "dv")
        _ln_stats(p, c, c.ln_t1[:], c.ln_t2[:], 1.0 / D)
        p.i("vector", lambda e: e.tensor_scalar(
            out=c.ln_t2[:], in0=c.ln_t1[:], scalar1=c.c_mu[:], scalar2=c.c_rstd[:],
            op0=ALU.add, op1=ALU.mult), "dv")
        p.w("vector", "ld", ld_xo)
        p.sw("vector", "dv")
        xo_rel[b] = p.i("vector", lambda e, tc=tc, b=b: e.tensor_tensor(
            out=c.r1[:, tc * D:(tc + 1) * D], in0=c.ln_t2[:], in1=c.xo_f32[b][:],
            op=ALU.add), "dv")
        p.w("scalar", "dv")
        _ln_stats(p, c, c.r1[:, tc * D:(tc + 1) * D], c.ln_t2[:], 1.0 / D)
        p.i("vector", lambda e, tc=tc: e.tensor_copy(
            out=c.ln3mu[:, tc:tc + 1], in_=c.c_mu[:]), "dv")
        p.i("vector", lambda e, tc=tc: e.tensor_copy(
            out=c.ln3rs[:, tc:tc + 1], in_=c.c_rstd[:]), "dv")
        dv_x2 = p.i("vector", lambda e, tc=tc: e.tensor_scalar(
            out=c.ln_t1[:], in0=c.r1[:, tc * D:(tc + 1) * D],
            scalar1=c.c_mu[:], scalar2=c.c_rstd[:], op0=ALU.add, op1=ALU.mult), "dv")
        if c.debug and tc == 0:
            p.w("gpsimd", "dv", dv_x2)
            gdt = p.i("gpsimd", lambda e: e.dma_start(out=c.dbg["dbg_x2"], in_=c.ln_t1[:]), "gd", 16)
            p.w("scalar", "gd", gdt)
            p.w("vector", "gd", gdt)
        p.w("scalar", "dv", dv_x2)
        p.i("scalar", lambda e: e.activation(out=c.ln_t2[:], in_=c.ln_t1[:],
                                             func=AF.Square, accum_out=c.c_ss[:]), "ac")
        p.w("scalar", "dv", getattr(c, "_recip_dv", 0))
        p.sw("scalar", "ac")
        ac_r = p.i("scalar", lambda e: e.activation(out=c.c_std[:], in_=c.c_ss[:],
                                                    func=AF.Sqrt, bias=c.eps_col[:],
                                                    scale=1.0 / D), "ac")
        c._sqrt_ac = ac_r
        p.w("vector", "ac", ac_r)
        c._recip_dv = p.i("vector", lambda e: e.reciprocal(out=c.c_rstd[:], in_=c.c_std[:]), "dv")
        p.sw("vector", "dv")
        p.i("vector", lambda e: e.tensor_scalar(
            out=c.ln_t1[:], in0=c.ln_t1[:], scalar1=c.c_rstd[:], scalar2=None,
            op0=ALU.mult), "dv")
        p.sw("vector", "dv")
        p.i("vector", lambda e: e.tensor_tensor(
            out=c.ln_t2[:], in0=c.ln_t1[:], in1=c.gamma_b[:], op=ALU.mult), "dv")
        p.i("vector", lambda e: e.reduce_max(out=c.c_am[:], in_=c.ln_t2[:], axis=AX.X,
                                             apply_absolute_value=True), "dv")
        p.sw("vector", "dv")
        p.i("vector", lambda e: e.tensor_scalar(
            out=c.c_amc[:], in0=c.c_am[:], scalar1=1e-5, scalar2=None,
            op0=ALU.max), "dv")
        p.w("vector", "dv")
        p.i("vector", lambda e: e.reciprocal(out=c.c_t1[:], in_=c.c_amc[:]), "dv")
        p.w("vector", "dv")
        p.i("vector", lambda e: e.tensor_scalar(
            out=c.c_rs1[:], in0=c.c_t1[:], scalar1=127.0, scalar2=None,
            op0=ALU.mult), "dv")
        p.i("vector", lambda e, tc=tc: e.scalar_tensor_tensor(
            out=c.d_allc[:, tc:tc + 1], in0=c.c_amc[:], scalar=1.0,
            in1=c.sgvf_b[:], op0=ALU.mult, op1=ALU.mult), "dv")
        p.sw("vector", "dv")
        p.i("vector", lambda e: e.tensor_scalar(
            out=c.ln_t1[:], in0=c.ln_t2[:], scalar1=c.c_rs1[:], scalar2=RT,
            op0=ALU.mult, op1=ALU.add), "dv")
        p.sw("vector", "dv")
        dv_q1 = p.i("vector", lambda e: e.tensor_scalar(
            out=c.q1_bf[:], in0=c.ln_t1[:], scalar1=RT, scalar2=None,
            op0=ALU.subtract), "dv")
        p.w("tensor", "dv", dv_q1)
        for g in range(4):
            tb = g % 2
            p.w("tensor", "dv", pst2_rel[tb])
            for kk in range(4):
                dcg = g * 4 + kk
                fn = lambda e, dcg=dcg, kk=kk, tb=tb: e.transpose(
                    c.ps_t2[tb][:, kk * 128:(kk + 1) * 128],
                    c.q1_bf[:, dcg * 128:(dcg + 1) * 128], c.ident_b[:])
                if kk < 3:
                    p.q("tensor", fn)
                else:
                    pe_t = p.i("tensor", fn, "pe")
            p.w("vector", "pe", pe_t)
            dst = c.q1_fm[:].rearrange("p (dc t) -> p dc t", dc=16)[:, g * 4:(g + 1) * 4,
                                                                   tc * 128:(tc + 1) * 128]
            src = c.ps_t2[tb][:].rearrange("p (a b) -> p a b", a=4)
            pst2_rel[tb] = p.i("vector", lambda e, dst=dst, src=src: e.tensor_copy(
                out=dst, in_=src), "dv")
    c.DV_P2A = cnt["dv"]
    c.PST2A = list(pst2_rel)
    if c.debug:
        p.w("gpsimd", "dv", cnt["dv"])
        p.i("gpsimd", lambda e: e.dma_start(out=c.dbg["dbg_r1"], in_=c.r1[:]), "gd", 16)
        p.i("gpsimd", lambda e: e.dma_start(out=c.dbg["dbg_q1"], in_=c.q1_fm[:]), "gd", 16)
        p.i("gpsimd", lambda e: e.dma_start(out=c.dbg["dbg_d"], in_=c.d_allc[:]), "gd", 16)


def emit_p2b(p, c):
    cnt = p.cnt
    pwh_rel = [0, 0, 0]
    silu_rel = 0
    h2_rel = [0, 0]
    psvg_rel = [0, 0]
    psvg_rel_dv = [0, 0]

    p.w("scalar", "cc", c.CC_AG)
    p.w("scalar", "cc", c.CC_RS)
    p.w("scalar", "dv", c.DV_P2A)
    hi = [0]
    for pr in range(16):
        for hf in range(2):
            wb = hi[0] % 3
            hi[0] += 1
            p.w("scalar", "pe", pwh_rel[wb])
            for dl in range(8):
                dc = hf * 8 + dl
                p.i("scalar", lambda e, dc=dc, dl=dl, pr=pr, wb=wb: e.dma_start(
                    out=c.pwh[wb][:, dl * 1024: dl * 1024 + 512],
                    in_=c.d_tgv_out[dc * 128:(dc + 1) * 128, pr * 512:(pr + 1) * 512]),
                    "wd", 16)
                p.i("scalar", lambda e, dc=dc, dl=dl, pr=pr, wb=wb: e.dma_start(
                    out=c.pwh[wb][:, dl * 1024 + 512: dl * 1024 + 1024],
                    in_=c.d_tgv_out[dc * 128:(dc + 1) * 128,
                                    8192 + pr * 512: 8192 + (pr + 1) * 512]),
                    "wd", 16)
            WD_H = cnt["wd"]
            if hf == 0:
                wd_half0 = WD_H
                wb_half0 = wb
            else:
                wd_half1 = WD_H
                wb_half1 = wb
        for tc in range(8):
            pb = 0
            p.w("tensor", "wd", wd_half1)
            p.w("tensor", "dv", psvg_rel_dv[0])
            p.w("tensor", "ac", psvg_rel[1])
            if tc == 0:
                p.w("tensor", "dv", c.DV_P2A)
            for dc in range(16):
                wb = wb_half0 if dc < 8 else wb_half1
                dl = dc % 8
                for gg in range(2):
                    fn = lambda e, dc=dc, dl=dl, gg=gg, tc=tc, wb=wb: e.matmul(
                        c.ps_a[gg][:],
                        c.q1_fm[:, dc * 1024 + tc * 128: dc * 1024 + (tc + 1) * 128],
                        c.pwh[wb][:, dl * 1024 + gg * 512: dl * 1024 + (gg + 1) * 512],
                        start=(dc == 0), stop=(dc == 15), skip_group_check=True)
                    if dc == 15 and gg == 1:
                        pe_mm = p.i("tensor", fn, "pe")
                    else:
                        p.q("tensor", fn)
            p.w("scalar", "pe", pe_mm)
            p.w("scalar", "dv", silu_rel)
            ac_g = p.i("scalar", lambda e, tc=tc: e.activation(
                out=c.silu_sb[:], in_=c.ps_a[1][:], func=AF.Silu,
                scale=c.d_allc[:, tc:tc + 1]), "ac")
            psvg_rel[1] = ac_g
            p.w("vector", "ac", ac_g)
            p.w("vector", "ld", h2_rel[tc % 2])
            dv_h = p.i("vector", lambda e, tc=tc: e.scalar_tensor_tensor(
                out=c.h2st[tc % 2][:], in0=c.ps_a[0][:], scalar=1.0,
                in1=c.silu_sb[:], op0=ALU.mult, op1=ALU.mult), "dv")
            silu_rel = dv_h
            psvg_rel_dv[0] = dv_h
            p.w("sync", "dv", dv_h)
            h2_rel[tc % 2] = p.i("sync", lambda e, tc=tc, pr=pr: e.dma_start(
                out=c.d_h2[tc, :, pr * 512:(pr + 1) * 512], in_=c.h2st[tc % 2][:]),
                "ld", 16)
        pwh_rel[wb_half0] = cnt["pe"]
        pwh_rel[wb_half1] = cnt["pe"]
    c.LD_P2B = cnt["ld"]
    c.DV_P2B = cnt["dv"]
    c.AC_P2B = cnt["ac"]


def emit_p2c(p, c):
    cnt = p.cnt
    h2_rel = [0, 0]
    pst2_rel = list(c.PST2A)
    q2fm_rel = 0

    p.w("sync", "ld", c.LD_P2B)
    for tc in range(8):
        for hf in range(2):
            b = hf
            p.w("sync", "dv", h2_rel[b])
            p.i("sync", lambda e, tc=tc, hf=hf, b=b: e.dma_start(
                out=c.h2l[b][:], in_=c.d_h2[tc, :, hf * 4096:(hf + 1) * 4096]), "ld", 16)
        LD_H2 = cnt["ld"]
        p.w("vector", "ld", LD_H2)
        p.i("vector", lambda e: e.reduce_max(out=c.c_t1[:], in_=c.h2l[0][:], axis=AX.X,
                                             apply_absolute_value=True), "dv")
        p.i("vector", lambda e: e.reduce_max(out=c.c_t2[:], in_=c.h2l[1][:], axis=AX.X,
                                             apply_absolute_value=True), "dv")
        p.sw("vector", "dv")
        p.i("vector", lambda e: e.tensor_tensor(out=c.c_am[:], in0=c.c_t1[:],
                                                in1=c.c_t2[:], op=ALU.max), "dv")
        p.sw("vector", "dv")
        p.i("vector", lambda e, tc=tc: e.scalar_tensor_tensor(
            out=c.c_am[:], in0=c.c_am[:], scalar=1.0, in1=c.d_allc[:, tc:tc + 1],
            op0=ALU.mult, op1=ALU.mult), "dv")
        p.sw("vector", "dv")
        p.i("vector", lambda e: e.tensor_scalar(out=c.c_amc[:], in0=c.c_am[:],
                                                scalar1=1e-5, scalar2=None,
                                                op0=ALU.max), "dv")
        p.sw("vector", "dv")
        p.i("vector", lambda e, tc=tc: e.scalar_tensor_tensor(
            out=c.d2_allc[:, tc:tc + 1], in0=c.c_amc[:], scalar=1.0, in1=c.swof_b[:],
            op0=ALU.mult, op1=ALU.mult), "dv")
        p.w("vector", "dv")
        p.i("vector", lambda e: e.reciprocal(out=c.c_t1[:], in_=c.c_amc[:]), "dv")
        p.w("vector", "dv")
        p.i("vector", lambda e: e.tensor_scalar(out=c.c_t1[:], in0=c.c_t1[:],
                                                scalar1=127.0, scalar2=None,
                                                op0=ALU.mult), "dv")
        p.sw("vector", "dv")
        p.i("vector", lambda e, tc=tc: e.scalar_tensor_tensor(
            out=c.c_f2[:], in0=c.c_t1[:], scalar=1.0, in1=c.d_allc[:, tc:tc + 1],
            op0=ALU.mult, op1=ALU.mult), "dv")
        for hf in range(2):
            p.sw("vector", "dv")
            p.i("vector", lambda e, hf=hf: e.tensor_scalar(
                out=c.h2l[hf][:], in0=c.h2l[hf][:], scalar1=c.c_f2[:], scalar2=RT,
                op0=ALU.mult, op1=ALU.add), "dv")
            p.sw("vector", "dv")
            h2_rel[hf] = p.i("vector", lambda e, hf=hf: e.tensor_scalar(
                out=c.q2_bf[:, hf * 4096:(hf + 1) * 4096], in0=c.h2l[hf][:],
                scalar1=RT, scalar2=None, op0=ALU.subtract), "dv")
        dv_q2 = cnt["dv"]
        p.w("tensor", "dv", dv_q2)
        p.w("vector", "ld", q2fm_rel)
        for g in range(16):
            tb = g % 2
            p.w("tensor", "dv", pst2_rel[tb])
            for kk in range(4):
                hcg = g * 4 + kk
                fn = lambda e, hcg=hcg, kk=kk, tb=tb: e.transpose(
                    c.ps_t2[tb][:, kk * 128:(kk + 1) * 128],
                    c.q2_bf[:, hcg * 128:(hcg + 1) * 128], c.ident_b[:])
                if kk < 3:
                    p.q("tensor", fn)
                else:
                    pe_t = p.i("tensor", fn, "pe")
            p.w("vector", "pe", pe_t)
            pst2_rel[tb] = p.i("vector", lambda e, g=g, tb=tb: e.tensor_copy(
                out=c.q2fm_st[:, g * 512:(g + 1) * 512], in_=c.ps_t2[tb][:]), "dv")
        p.w("sync", "dv", pst2_rel[1])
        q2fm_rel = p.i("sync", lambda e, tc=tc: e.dma_start(
            out=c.d_q2[tc], in_=c.q2fm_st[:]), "ld", 16)
    c.LD_Q2 = cnt["ld"]


def emit_p2d(p, c):
    cnt = p.cnt
    q2_rel = [0, 0]
    o2st_rel = [0, 0]
    psa_rel = [0, 0]
    wD_rel = 0
    p.w("tensor", "dv", c.DV_P2B)
    p.w("tensor", "ac", c.AC_P2B)
    p.w("scalar", "ld", c.LD_Q2)

    for fc in range(4):
        p.w("scalar", "pe", wD_rel)
        for hc in range(64):
            p.i("scalar", lambda e, hc=hc, fc=fc: e.dma_start(
                out=c.wD[:, hc * 512:(hc + 1) * 512],
                in_=c.d_two_out[hc * 128:(hc + 1) * 128, fc * 512:(fc + 1) * 512]),
                "wd", 16)
        WD_W = cnt["wd"]
        for tc in range(8):
            b = tc % 2
            p.w("sync", "pe", q2_rel[b])
            if fc == 0 and tc == 0:
                p.w("sync", "ld", c.LD_Q2)
            ld_q = p.i("sync", lambda e, tc=tc, b=b: e.dma_start(
                out=c.q2sb[b][:], in_=c.d_q2[tc]), "ld", 16)
            ab = tc % 2
            p.w("tensor", "ld", ld_q)
            p.w("tensor", "wd", WD_W)
            p.w("tensor", "ac", psa_rel[ab])
            for hc in range(64):
                fn = lambda e, hc=hc, b=b, ab=ab: e.matmul(
                    c.ps_a[ab][:],
                    c.q2sb[b][:, hc * 128:(hc + 1) * 128],
                    c.wD[:, hc * 512:(hc + 1) * 512],
                    start=(hc == 0), stop=(hc == 63))
                if hc < 63:
                    p.q("tensor", fn)
                else:
                    pe_o = p.i("tensor", fn, "pe")
            q2_rel[b] = pe_o
            p.w("scalar", "pe", pe_o)
            p.w("scalar", "ld", o2st_rel[tc % 2])
            ac_e = p.i("scalar", lambda e, tc=tc, ab=ab: e.activation(
                out=c.o2st[tc % 2][:], in_=c.ps_a[ab][:], func=AF.Copy,
                scale=c.d2_allc[:, tc:tc + 1]), "ac")
            psa_rel[ab] = ac_e
            p.w("sync", "ac", ac_e)
            o2st_rel[tc % 2] = p.i("sync", lambda e, tc=tc, fc=fc: e.dma_start(
                out=c.d_o2[tc, :, fc * 512:(fc + 1) * 512], in_=c.o2st[tc % 2][:]),
                "ld", 16)
        wD_rel = cnt["pe"]
    c.LD_O2 = cnt["ld"]


def emit_tail(p, c):
    cnt = p.cnt
    o2_rel = [0, 0]
    y_rel = [0, 0]
    p.w("sync", "ld", c.LD_O2)
    for tc in range(8):
        b = tc % 2
        p.w("sync", "dv", o2_rel[b])
        ld_o = p.i("sync", lambda e, tc=tc, b=b: e.dma_start(
            out=c.o2sb[b][:], in_=c.d_o2[tc]), "ld", 16)
        p.i("vector", lambda e, tc=tc: e.tensor_scalar(
            out=c.tl_t1[:], in0=c.r1[:, tc * D:(tc + 1) * D],
            scalar1=c.ln3mu[:, tc:tc + 1], scalar2=c.ln3rs[:, tc:tc + 1],
            op0=ALU.add, op1=ALU.mult), "dv")
        p.w("vector", "ld", ld_o)
        p.sw("vector", "dv")
        o2_rel[b] = p.i("vector", lambda e, b=b: e.tensor_tensor(
            out=c.tl_t1[:], in0=c.tl_t1[:], in1=c.o2sb[b][:], op=ALU.add), "dv")
        p.w("vector", "ld", y_rel[b])
        p.w("scalar", "ld", y_rel[b])
        p.w("scalar", "dv")
        _ln_stats(p, c, c.tl_t1[:], c.y_st[b][:], 1.0 / D)
        p.i("vector", lambda e: e.tensor_scalar(
            out=c.tl_t2[:], in0=c.tl_t1[:], scalar1=c.c_mu[:], scalar2=c.c_rstd[:],
            op0=ALU.add, op1=ALU.mult), "dv")
        p.sw("vector", "dv")
        dv_y = p.i("vector", lambda e, tc=tc, b=b: e.tensor_tensor(
            out=c.y_st[b][:], in0=c.tl_t2[:], in1=c.r1[:, tc * D:(tc + 1) * D],
            op=ALU.add), "dv")
        p.w("sync", "dv", dv_y)
        y_rel[b] = p.i("sync", lambda e, tc=tc, b=b: e.dma_start(
            out=c.d_y[tc * 128:(tc + 1) * 128, :], in_=c.y_st[b][:]), "ld", 16)
    p.w("sync", "ld")
    p.w("gpsimd", "gd")


def shard_inputs(x, Wqkv, bqkv, Wout, bout, norm_w, w_gv, w_out):
    x = np.ascontiguousarray(np.asarray(x, dtype=np.float32))
    Wqkv = np.asarray(Wqkv, dtype=np.float32)
    bqkv = np.asarray(bqkv, dtype=np.float32)
    Wout = np.asarray(Wout, dtype=np.float32)
    bout = np.asarray(bout, dtype=np.float32)
    norm_w = np.asarray(norm_w, dtype=np.float32)
    wgvt = np.ascontiguousarray(np.asarray(w_gv, dtype=np.float32).T)
    woutt = np.ascontiguousarray(np.asarray(w_out, dtype=np.float32).T)
    bout_b = np.ascontiguousarray(np.broadcast_to(bout, (128, D)))
    gamma_b = np.ascontiguousarray(np.broadcast_to(norm_w, (128, D)))
    in_maps = []
    for cid in range(8):
        b, r = cid // 4, cid % 4
        heads = range(4 * r, 4 * r + 4)
        cols = []
        for role in range(3):
            for h in heads:
                cols.extend(range(role * D + h * 128, role * D + (h + 1) * 128))
        cols = np.asarray(cols)
        in_maps.append({
            "xb": x[b],
            "xown": np.ascontiguousarray(x[b, r * 1024:(r + 1) * 1024]),
            "wqkv": np.ascontiguousarray(Wqkv[:, cols]),
            "bqkv": np.ascontiguousarray(bqkv[cols]),
            "woutc": np.ascontiguousarray(Wout[4 * r * 128:(4 * r + 4) * 128, :]),
            "boutb": bout_b,
            "gammab": gamma_b,
            "wgv_sh": np.ascontiguousarray(wgvt[cid * 256:(cid + 1) * 256, :]),
            "wo_sh": np.ascontiguousarray(woutt[cid * 1024:(cid + 1) * 1024, :]),
        })
    return in_maps


def assemble(results):
    out = np.empty((2, T, D), dtype=np.float32)
    for cid in range(8):
        b, r = cid // 4, cid % 4
        out[b, r * 1024:(r + 1) * 1024, :] = results[cid]["y"]
    return out


from concourse.bass_utils import run_bass_kernel_spmd

_NC_CACHE = [None]


def kernel(L0_attn_x, L0_attn_Wqkv, L0_attn_bqkv, L0_attn_Wout, L0_attn_bout,
           L0_ffn_norm_w, L0_ffn_w_gv, L0_ffn_w_out):
    in_maps = shard_inputs(L0_attn_x, L0_attn_Wqkv, L0_attn_bqkv, L0_attn_Wout,
                           L0_attn_bout, L0_ffn_norm_w, L0_ffn_w_gv, L0_ffn_w_out)
    if _NC_CACHE[0] is None:
        _NC_CACHE[0] = build(debug=0)
    nc = _NC_CACHE[0]
    res = run_bass_kernel_spmd(nc, in_maps, core_ids=list(range(8)))
    return assemble(res.results)


# version 03
# speedup vs baseline: 1.0739x; 1.0739x over previous
import contextlib
import numpy as np
import concourse.bass as bass
import concourse.mybir as mybir

F32 = mybir.dt.float32
BF16 = mybir.dt.bfloat16
AF = mybir.ActivationFunctionType
ALU = mybir.AluOpType
AX = mybir.AxisListType

T, D = 4096, 2048
HID = 8192
RT = 12582912.0
SCALE = 1.0 / (128.0 ** 0.5)
NEG = -1.0e30
EPS = 1e-5
G4 = [[0, 1, 2, 3], [4, 5, 6, 7]]
G8 = [[0, 1, 2, 3, 4, 5, 6, 7]]
N_GV = float(16384 * 2048)
N_WO = float(2048 * 8192)

ENGS = ("sync", "scalar", "tensor", "vector", "gpsimd")


class Plan:

    def __init__(self, nc, sems, cnt):
        self.nc, self.sems, self.cnt = nc, sems, cnt
        self.ops = {e: [] for e in ENGS}

    def q(self, eng, fn):
        self.ops[eng].append(fn)

    def w(self, eng, sem, val=None):
        v = self.cnt[sem] if val is None else val
        assert v <= self.cnt[sem], f"forward wait {sem}:{v}>{self.cnt[sem]}"
        if v <= 0:
            return
        s = self.sems[sem]
        self.ops[eng].append(lambda e, s=s, v=v: e.wait_ge(s, v))

    def sw(self, eng, sem):
        self.w(eng, sem)

    def i(self, eng, fn, sem, n=1):
        self.cnt[sem] += n
        s = self.sems[sem]
        self.ops[eng].append(lambda e, fn=fn, s=s, n=n: fn(e).then_inc(s, n))
        return self.cnt[sem]

    def run(self, name):
        with self.nc.Block(name) as blk:
            for eng in ENGS:
                ops = self.ops[eng]
                if ops:
                    getattr(blk, eng)(lambda e, ops=ops: [f(e) for f in ops])


class C:
    pass


def build(debug=0, stop_after=None, no_tern=False, no_wout=False, tern_mode=4):
    nc = bass.Bass("TRN2", target_bir_lowering=False, dynamic_dma_scratch_size=16384,
                   detect_race_conditions=False)
    c = C()
    c.debug = debug

    c.d_xb = nc.dram_tensor("xb", [T, D], F32, kind="ExternalInput").ap()
    c.d_xown = nc.dram_tensor("xown", [1024, D], F32, kind="ExternalInput").ap()
    c.d_wqkv = nc.dram_tensor("wqkv", [D, 1536], F32, kind="ExternalInput").ap()
    c.d_bqkv = nc.dram_tensor("bqkv", [1536], F32, kind="ExternalInput").ap()
    c.d_woutc = nc.dram_tensor("woutc", [512, D], F32, kind="ExternalInput").ap()
    c.d_bout = nc.dram_tensor("boutb", [128, D], F32, kind="ExternalInput").ap()
    c.d_gamma = nc.dram_tensor("gammab", [128, D], F32, kind="ExternalInput").ap()
    c.d_wgv = nc.dram_tensor("wgv_sh", [256, 16384], F32, kind="ExternalInput").ap()
    c.d_wo = nc.dram_tensor("wo_sh", [1024, D], F32, kind="ExternalInput").ap()
    c.d_y = nc.dram_tensor("y", [1024, D], F32, kind="ExternalOutput").ap()

    c.d_ar_in = nc.dram_tensor("ar_in", [1, 2], F32, kind="Internal").ap()
    c.d_ar_out = nc.dram_tensor("ar_out", [1, 2], F32, kind="Internal").ap()
    c.d_tgv_in = nc.dram_tensor("tgv_in", [256, 16384], BF16, kind="Internal").ap()
    c.d_tgv_out = nc.dram_tensor("tgv_out", [D, 16384], BF16, kind="Internal", addr_space="Shared").ap()
    c.d_two_in = nc.dram_tensor("two_in", [1024, D], BF16, kind="Internal").ap()
    c.d_two_out = nc.dram_tensor("two_out", [HID, D], BF16, kind="Internal", addr_space="Shared").ap()
    c.d_rs_inA = nc.dram_tensor("rs_inA", [T // 2, D], BF16, kind="Internal").ap()
    c.d_rs_inB = nc.dram_tensor("rs_inB", [T // 2, D], BF16, kind="Internal").ap()
    c.d_rs_outA = nc.dram_tensor("rs_outA", [512, D], BF16, kind="Internal").ap()
    c.d_rs_outB = nc.dram_tensor("rs_outB", [512, D], BF16, kind="Internal").ap()
    c.d_h2 = nc.dram_tensor("h2d", [8, 128, HID], F32, kind="Internal").ap()
    c.d_q2 = nc.dram_tensor("q2d", [8, 128, HID], BF16, kind="Internal").ap()
    c.d_o2 = nc.dram_tensor("o2d", [8, 128, D], F32, kind="Internal").ap()

    c.no_tern = no_tern
    c.no_wout = no_wout
    c.tern_mode = tern_mode
    c.dbg = {}
    if debug:
        for nm, shp, dt in [("dbg_q", [128, 4 * T], BF16), ("dbg_k", [128, 4 * T], BF16),
                            ("dbg_v", [128, 4 * T], BF16), ("dbg_bias", [128, 128], F32),
                            ("dbg_sfac", [1, 4], F32), ("dbg_av", [128, 4 * 512], BF16),
                            ("dbg_d", [128, 8], F32), ("dbg_q1", [128, 16 * 1024], BF16),
                            ("dbg_r1", [128, 8 * D], F32),
                            ("dbg_pn0", [128, T], BF16), ("dbg_pn1", [128, T], BF16),
                            ("dbg_pn2", [128, T], BF16), ("dbg_pn3", [128, T], BF16),
                            ("dbg_praw", [128, T], F32), ("dbg_ell", [128, 8], F32),
                            ("dbg_pnh0", [4, 128, T], BF16), ("dbg_prh0", [128, T], F32),
                            ("dbg_s", [128, 512], F32), ("dbg_qs", [128, 128], BF16),
                            ("dbg_ks", [128, 512], BF16), ("dbg_rso", [1024, D], BF16),
                            ("dbg_o1", [128, D], F32), ("dbg_x2", [128, D], F32)]:
            c.dbg[nm] = nc.dram_tensor(nm, shp, dt, kind="ExternalOutput").ap()

    sems, cnt = {}, {}
    for s in ("ld", "wd", "gd", "pe", "dv", "ac", "cc"):
        sems[s] = nc.alloc_semaphore(s)
        cnt[s] = 0

    def sb(stk, name, shape, dt):
        return stk.enter_context(nc.sbuf_tensor(name, shape, dt))

    def ps(stk, name, shape, dt=F32):
        return stk.enter_context(nc.psum_tensor(name, shape, dt))

    with contextlib.ExitStack() as est:
        S1 = est.enter_context(contextlib.ExitStack())
        c.ident_b = sb(S1, "ident_b", [128, 128], BF16)
        c.ident_f = sb(S1, "ident_f", [128, 128], F32)
        c.tri_neg = sb(S1, "tri_neg", [128, 128], F32)
        c.ones_row = sb(S1, "ones_row", [1, 128], F32)
        c.ones_col_b = sb(S1, "ones_col_b", [128, 1], BF16)
        c.half_neg = sb(S1, "half_neg", [128, 1], F32)
        c.half_pos = sb(S1, "half_pos", [128, 1], F32)
        c.eps_col = sb(S1, "eps_col", [128, 1], F32)
        for nm in ("c_sx", "c_ss", "c_mu", "c_bias", "c_std", "c_rstd",
                   "c_t1", "c_t2", "c_ell", "c_rinv",
                   "invs_gv", "invs_wo", "sgvf_b", "swof_b",
                   "c_am", "c_amc", "c_rs1", "c_f2"):
            setattr(c, nm, sb(S1, nm, [128, 1], F32))
        c.ell_segs = sb(S1, "ell_segs", [128, 8], F32)
        c.trow = sb(S1, "trow", [1, 128], F32)
        c.tsc = sb(S1, "tsc", [1, 4], F32)
        c.svals = sb(S1, "svals", [1, 2], F32)
        c.sfac = sb(S1, "sfac", [1, 4], F32)
        c.spsums = sb(S1, "spsums", [128, 96], F32)
        c.d_allc = sb(S1, "d_allc", [128, 8], F32)
        c.d2_allc = sb(S1, "d2_allc", [128, 8], F32)
        c.ln3mu = sb(S1, "ln3mu", [128, 8], F32)
        c.ln3rs = sb(S1, "ln3rs", [128, 8], F32)
        c.mks = sb(S1, "mks", [128, 4], F32)

        S2a = est.enter_context(contextlib.ExitStack())
        c.q_bf = sb(S2a, "q_bf", [128, 4 * T], BF16)
        c.k_bf = sb(S2a, "k_bf", [128, 4 * T], BF16)
        c.v_tm = sb(S2a, "v_tm", [128, 4 * T], BF16)
        c.bq_sb = sb(S2a, "bq_sb", [128, 12], F32)
        c.bias_tm = sb(S2a, "bias_tm", [128, 128], F32)
        c.normsb = sb(S2a, "normsb", [128, 256], F32)

        S3 = est.enter_context(contextlib.ExitStack())
        c.x_in = [sb(S3, f"x_in{i}", [128, D], F32) for i in range(2)]
        c.x1g = sb(S3, "x1g", [128, 2 * D], BF16)
        c.x1_fm = sb(S3, "x1_fm", [128, 16 * 512], BF16)
        c.wq_f32 = sb(S3, "wq_f32", [128, 1536], F32)
        c.wqkv_bf = sb(S3, "wqkv_bf", [128, 16 * 1536], BF16)
        c.v_stage = sb(S3, "v_stage", [128, 512], BF16)
        c.sq_stage = sb(S3, "sq_stage", [128, 512], BF16)
        c.sp_buf = [sb(S3, f"sp_buf{i}", [128, 512], F32) for i in range(2)]

        PS0 = est.enter_context(contextlib.ExitStack())
        c.ps_qkv = [ps(PS0, f"ps_qkv{i}", [128, 512]) for i in range(2)]
        c.ps_t = [ps(PS0, f"ps_t{i}", [128, 512], BF16) for i in range(2)]
        c.ps_norm = ps(PS0, "ps_norm", [128, 256])
        c.ps_misc = ps(PS0, "ps_misc", [128, 128])

        p = Plan(nc, sems, cnt)
        emit_consts(p, c)
        emit_p0(p, c)
        p.run("p0")

        if stop_after == "p0":
            return nc
        S3.close()

        S2b = est.enter_context(contextlib.ExitStack())
        c.p_raw = sb(S2b, "p_raw", [128, T], F32)
        c.woutc_bf = sb(S2b, "woutc_bf", [128, 4 * D], BF16)
        c.p_nrm = [sb(S2b, f"p_nrm{l}", [128, T], BF16) for l in range(4)]
        c.pt_sb = [sb(S2b, f"pt_sb{i}", [128, 512], BF16) for i in range(2)]
        c.av_sb = [sb(S2b, f"av_sb{i}", [128, 4 * 512], BF16) for i in range(2)]
        c.wo_stage = [sb(S2b, f"wo_stage{i}", [128, D], BF16) for i in range(2)]
        c.tz_f32 = [sb(S2b, f"tz_f32{i}", [128, 512], F32) for i in range(2)]
        c.tz_s1 = sb(S2b, "tz_s1", [128, 512], BF16)
        c.tz_out = [sb(S2b, f"tz_out{i}", [128, 512], BF16) for i in range(2)]

        PS0.close()
        PS1 = est.enter_context(contextlib.ExitStack())
        c.ps_s = [ps(PS1, f"ps_s{i}", [128, 512]) for i in range(2)]
        c.ps_pt = [ps(PS1, f"ps_pt{i}", [128, 512], BF16) for i in range(2)]
        c.ps_av = [ps(PS1, f"ps_av{i}", [128, 512]) for i in range(2)]
        c.ps_w = [ps(PS1, f"ps_w{i}", [128, 512]) for i in range(2)]

        p = Plan(nc, sems, cnt)
        emit_p1(p, c)
        p.run("p1")

        if stop_after == "p1":
            return nc
        S2b.close()
        S2a.close()
        PS1.close()

        S4 = est.enter_context(contextlib.ExitStack())
        c.r1 = sb(S4, "r1", [128, 8 * D], F32)

        S4a1 = est.enter_context(contextlib.ExitStack())
        c.q1_fm = sb(S4a1, "q1_fm", [128, 16 * 1024], BF16)
        S4a = est.enter_context(contextlib.ExitStack())
        c.bout_b = sb(S4a, "bout_b", [128, D], F32)
        c.gamma_b = sb(S4a, "gamma_b", [128, D], F32)
        c.ln_t1 = sb(S4a, "ln_t1", [128, D], F32)
        c.ln_t2 = sb(S4a, "ln_t2", [128, D], F32)
        c.rsx = [sb(S4a, f"rsx{i}", [128, D], BF16) for i in range(2)]
        c.xo_f32 = [sb(S4a, f"xo_f32{i}", [128, D], F32) for i in range(2)]
        c.q1_bf = sb(S4a, "q1_bf", [128, D], BF16)

        PS2 = est.enter_context(contextlib.ExitStack())
        c.ps_a = [ps(PS2, f"ps_a{i}", [128, 512]) for i in range(2)]
        c.ps_t2 = [ps(PS2, f"ps_t2{i}", [128, 512], BF16) for i in range(2)]

        p = Plan(nc, sems, cnt)
        emit_p2a(p, c)

        S4a.close()
        S4b = est.enter_context(contextlib.ExitStack())
        c.pwh = [sb(S4b, f"pwh{i}", [128, 16 * 1024], BF16) for i in range(3)]
        c.silu_sb = sb(S4b, "silu_sb", [128, 512], F32)
        c.h2st = [sb(S4b, f"h2st{i}", [128, 512], F32) for i in range(2)]

        emit_p2b(p, c)
        p.run("p2ab")
        if stop_after == "p2b":
            return nc
        S4b.close()
        S4a1.close()

        S4c = est.enter_context(contextlib.ExitStack())
        c.h2l = [sb(S4c, f"h2l{i}", [128, 4096], F32) for i in range(2)]
        c.q2_bf = sb(S4c, "q2_bf", [128, HID], BF16)
        c.q2fm_st = sb(S4c, "q2fm_st", [128, HID], BF16)

        p = Plan(nc, sems, cnt)
        emit_p2c(p, c)
        p.run("p2c")
        S4c.close()

        S4d1 = est.enter_context(contextlib.ExitStack())
        c.wD = sb(S4d1, "wD", [128, 64 * 512], BF16)
        c.q2sb = [sb(S4d1, f"q2sb{i}", [128, HID], BF16) for i in range(2)]
        c.o2st = [sb(S4d1, f"o2st{i}", [128, 512], F32) for i in range(2)]

        p = Plan(nc, sems, cnt)
        emit_p2d(p, c)
        p.run("p2d")
        S4d1.close()

        S4d2 = est.enter_context(contextlib.ExitStack())
        c.o2sb = [sb(S4d2, f"o2sb{i}", [128, D], F32) for i in range(2)]
        c.y_st = [sb(S4d2, f"y_st{i}", [128, D], F32) for i in range(2)]
        c.tl_t1 = sb(S4d2, "tl_t1", [128, D], F32)
        c.tl_t2 = sb(S4d2, "tl_t2", [128, D], F32)

        p = Plan(nc, sems, cnt)
        emit_tail(p, c)
        p.run("tail")
        S4d2.close()
        PS2.close()

    return nc


def emit_consts(p, c):
    gsem = "cc"
    p.i("gpsimd", lambda e: e.memset(c.ident_b[:], 0.0), gsem, 1)
    p.w("gpsimd", gsem)
    p.i("gpsimd", lambda e: e.affine_select(
        out=c.ident_b[:], in_=c.ident_b[:], compare_op=ALU.not_equal,
        fill=1.0, base=0, pattern=[[-1, 128]], channel_multiplier=1), gsem, 1)
    p.i("gpsimd", lambda e: e.memset(c.ident_f[:], 0.0), gsem, 1)
    p.w("gpsimd", gsem)
    p.i("gpsimd", lambda e: e.affine_select(
        out=c.ident_f[:], in_=c.ident_f[:], compare_op=ALU.not_equal,
        fill=1.0, base=0, pattern=[[-1, 128]], channel_multiplier=1), gsem, 1)
    p.i("gpsimd", lambda e: e.memset(c.tri_neg[:], 0.0), gsem, 1)
    p.w("gpsimd", gsem)
    p.i("gpsimd", lambda e: e.affine_select(
        out=c.tri_neg[:], in_=c.tri_neg[:], compare_op=ALU.is_ge,
        fill=NEG, base=0, pattern=[[-1, 128]], channel_multiplier=1), gsem, 1)
    p.i("gpsimd", lambda e: e.memset(c.ones_row[:], 1.0), gsem, 1)
    p.i("gpsimd", lambda e: e.memset(c.half_neg[:], -0.5), gsem, 1)
    p.i("gpsimd", lambda e: e.memset(c.half_pos[:], 0.5), gsem, 1)
    p.i("gpsimd", lambda e: e.memset(c.eps_col[:], EPS), gsem, 1)
    p.i("gpsimd", lambda e: e.memset(c.ones_col_b[:], 1.0), gsem, 1)
    c.CC_CONSTS = p.cnt["cc"]


def _ln_stats(p, c, src_ap, sq_out_ap, inv_n, eps=EPS):
    p.i("vector", lambda e: e.reduce_sum(out=c.c_sx[:], in_=src_ap, axis=AX.X), "dv")
    sq_ac = p.i("scalar", lambda e: e.activation(out=sq_out_ap, in_=src_ap, func=AF.Square,
                                                 accum_out=c.c_ss[:]), "ac")
    p.sw("vector", "dv")
    p.i("vector", lambda e: e.tensor_scalar(out=c.c_mu[:], in0=c.c_sx[:],
                                            scalar1=-inv_n, scalar2=None,
                                            op0=ALU.mult), "dv")
    p.sw("vector", "dv")
    p.i("vector", lambda e: e.tensor_tensor(out=c.c_t1[:], in0=c.c_mu[:],
                                            in1=c.c_mu[:], op=ALU.mult), "dv")
    p.w("vector", "ac", getattr(c, "_sqrt_ac", 0))
    p.sw("vector", "dv")
    dv_b = p.i("vector", lambda e: e.tensor_scalar(out=c.c_bias[:], in0=c.c_t1[:],
                                                   scalar1=-1.0, scalar2=eps,
                                                   op0=ALU.mult, op1=ALU.add), "dv")
    p.w("scalar", "dv", dv_b)
    p.w("scalar", "dv", getattr(c, "_recip_dv", 0))
    p.w("scalar", "ac", sq_ac)
    ac_s = p.i("scalar", lambda e: e.activation(out=c.c_std[:], in_=c.c_ss[:],
                                                func=AF.Sqrt, bias=c.c_bias[:],
                                                scale=inv_n), "ac")
    c._sqrt_ac = ac_s
    p.w("vector", "ac", ac_s)
    r = p.i("vector", lambda e: e.reciprocal(out=c.c_rstd[:], in_=c.c_std[:]), "dv")
    c._recip_dv = r
    p.sw("vector", "dv")
    return r


def emit_p0(p, c):
    cnt = p.cnt

    for fc in range(12):
        p.i("sync", lambda e, fc=fc: e.dma_start(
            out=c.bq_sb[:, fc:fc + 1],
            in_=c.d_bqkv[fc * 128:(fc + 1) * 128]), "ld", 16)
    LD_BQ = p.cnt["ld"]
    p.w("scalar", "ld", LD_BQ)

    wq_rel = 0
    for dc in range(16):
        p.w("sync", "dv", wq_rel)
        p.i("sync", lambda e, dc=dc: e.dma_start(
            out=c.wq_f32[:, 0:1536],
            in_=c.d_wqkv[dc * 128:(dc + 1) * 128, :]), "ld", 16)
        p.w("vector", "ld")
        wq_rel = p.i("vector", lambda e, dc=dc: e.tensor_copy(
            out=c.wqkv_bf[:, dc * 1536:(dc + 1) * 1536],
            in_=c.wq_f32[:, 0:1536]), "dv")
    p.w("tensor", "cc", c.CC_CONSTS)

    x_rel = [(0, 0), (0, 0)]
    x1g_rel = 0
    x1fm_rel = 0
    pst_rel = [0, 0]
    psqkv_rel = [0, 0]
    sq_rel = 0
    vst_rel = 0
    spi = 0
    sp_rel = [0, 0]

    for sc in range(8):
        evac_done = 0
        for half in range(2):
            for tcl2 in range(2):
                tcl = half * 2 + tcl2
                tc = sc * 4 + tcl
                b = tc % 2
                p.w("sync", "dv", x_rel[b][0])
                p.w("sync", "ac", x_rel[b][1])
                ldx = p.i("sync", lambda e, tc=tc, b=b: e.dma_start(
                    out=c.x_in[b][:], in_=c.d_xb[tc * 128:(tc + 1) * 128, :]), "ld", 16)
                p.w("vector", "ld", ldx)
                p.w("scalar", "ld", ldx)
                p.w("scalar", "pe", x1g_rel)
                _ln_stats(p, c, c.x_in[b][:], c.x1g[:, tcl2 * D:(tcl2 + 1) * D], 1.0 / D)
                ac_used = cnt["ac"]
                p.w("vector", "pe", x1g_rel)
                dv_ap = p.i("vector", lambda e, b=b, tcl2=tcl2: e.tensor_scalar(
                    out=c.x1g[:, tcl2 * D:(tcl2 + 1) * D], in0=c.x_in[b][:],
                    scalar1=c.c_mu[:], scalar2=c.c_rstd[:],
                    op0=ALU.add, op1=ALU.mult), "dv")
                x_rel[b] = (dv_ap, ac_used)
            dv_half = cnt["dv"]

            p.w("tensor", "dv", dv_half)
            if half == 0:
                p.w("tensor", "pe", x1fm_rel)
            for dc in range(16):
                bb = dc % 2
                p.w("tensor", "dv", pst_rel[bb])
                for tcl2 in range(2):
                    fn = lambda e, dc=dc, tcl2=tcl2, bb=bb: e.transpose(
                        c.ps_t[bb][:, tcl2 * 128:(tcl2 + 1) * 128],
                        c.x1g[:, tcl2 * D + dc * 128: tcl2 * D + (dc + 1) * 128],
                        c.ident_b[:])
                    if tcl2 == 0:
                        p.q("tensor", fn)
                    else:
                        pe_t = p.i("tensor", fn, "pe")
                p.w("vector", "pe", pe_t)
                pst_rel[bb] = p.i("vector", lambda e, dc=dc, half=half, bb=bb: e.tensor_copy(
                    out=c.x1_fm[:, dc * 512 + half * 256: dc * 512 + half * 256 + 256],
                    in_=c.ps_t[bb][:, 0:256]), "dv")
            x1g_rel = cnt["pe"]
        evac_done = cnt["dv"]

        for k in range(12):
            i = spi
            spi += 1
            b = i % 2
            if i < 64:
                pc, ct = i // 32, i % 32
                src = c.d_wgv[pc * 128:(pc + 1) * 128, ct * 512:(ct + 1) * 512]
            else:
                jj = i - 64
                pc, ct = jj // 4, jj % 4
                src = c.d_wo[pc * 128:(pc + 1) * 128, ct * 512:(ct + 1) * 512]
            p.w("scalar", "dv", sp_rel[b])
            p.i("scalar", lambda e, src=src, b=b: e.dma_start(
                out=c.sp_buf[b][:], in_=src), "wd", 16)
            p.w("vector", "wd")
            sp_rel[b] = p.i("vector", lambda e, i=i, b=b: e.reduce_sum(
                out=c.spsums[:, i:i + 1], in_=c.sp_buf[b][:], axis=AX.X,
                apply_absolute_value=True), "dv")

        p.w("tensor", "dv", evac_done)
        for fc in range(12):
            bb = fc % 2
            p.w("tensor", "ac", psqkv_rel[bb])
            for dc in range(16):
                fn = lambda e, fc=fc, dc=dc, bb=bb: e.matmul(
                    c.ps_qkv[bb][:],
                    c.wqkv_bf[:, dc * 1536 + fc * 128: dc * 1536 + (fc + 1) * 128],
                    c.x1_fm[:, dc * 512:(dc + 1) * 512],
                    start=(dc == 0), stop=(dc == 15))
                if dc < 15:
                    p.q("tensor", fn)
                else:
                    pe_mm = p.i("tensor", fn, "pe")
            h, role = fc % 4, fc // 4
            if role == 0:
                slot = c.q_bf[:, h * T + sc * 512: h * T + sc * 512 + 512]
            elif role == 1:
                slot = c.k_bf[:, h * T + sc * 512: h * T + sc * 512 + 512]
            else:
                slot = c.v_stage[:]
                p.w("scalar", "pe", vst_rel)
            p.w("scalar", "pe", pe_mm)
            ac_ev = p.i("scalar", lambda e, slot=slot, fc=fc, bb=bb: e.activation(
                out=slot, in_=c.ps_qkv[bb][:], func=AF.Identity,
                bias=c.bq_sb[:, fc:fc + 1]), "ac")
            if role < 2:
                p.w("scalar", "pe", sq_rel)
                ac_ev = p.i("scalar", lambda e, fc=fc, bb=bb: e.activation(
                    out=c.sq_stage[:], in_=c.ps_qkv[bb][:], func=AF.Square,
                    bias=c.bq_sb[:, fc:fc + 1]), "ac")
                p.w("tensor", "ac", ac_ev)
                for kk in range(4):
                    col = (role * 4 + h) * 32 + sc * 4 + kk
                    fn = lambda e, kk=kk, col=col: e.matmul(
                        c.ps_norm[:, col:col + 1],
                        c.sq_stage[:, kk * 128:(kk + 1) * 128],
                        c.ones_col_b[:], start=True, stop=True)
                    if kk < 3:
                        p.q("tensor", fn)
                    else:
                        sq_rel = p.i("tensor", fn, "pe")
            else:
                p.w("tensor", "ac", ac_ev)
                tb = fc % 2
                p.w("tensor", "dv", pst_rel[tb])
                for kk in range(4):
                    fn = lambda e, kk=kk, tb=tb: e.transpose(
                        c.ps_t[tb][:, kk * 128:(kk + 1) * 128],
                        c.v_stage[:, kk * 128:(kk + 1) * 128], c.ident_b[:])
                    if kk < 3:
                        p.q("tensor", fn)
                    else:
                        pe_vt = p.i("tensor", fn, "pe")
                vst_rel = pe_vt
                p.w("vector", "pe", pe_vt)
                pst_rel[tb] = p.i("vector", lambda e, h=h, sc=sc, tb=tb: e.tensor_copy(
                    out=c.v_tm[:, (h * 32 + sc * 4) * 128:(h * 32 + sc * 4 + 4) * 128],
                    in_=c.ps_t[tb][:]), "dv")
            psqkv_rel[bb] = cnt["ac"]
        x1fm_rel = cnt["pe"]

    p.w("scalar", "pe")
    ac_n = p.i("scalar", lambda e: e.activation(
        out=c.normsb[:], in_=c.ps_norm[:], func=AF.Sqrt), "ac")
    dv_mk_prev = 0
    for h in range(4):
        p.w("vector", "ac", ac_n)
        dv_m = p.i("vector", lambda e, h=h: e.reduce_max(
            out=c.c_t1[:], in_=c.normsb[:, (4 + h) * 32:(5 + h) * 32], axis=AX.X), "dv")
        p.w("tensor", "dv", dv_m)
        p.w("tensor", "dv", dv_mk_prev)
        pe_tr = p.i("tensor", lambda e: e.transpose(
            c.ps_misc[0:1, 0:128], c.c_t1[:], c.ident_f[:]), "pe")
        p.w("vector", "pe", pe_tr)
        p.i("vector", lambda e: e.tensor_copy(out=c.trow[:],
                                              in_=c.ps_misc[0:1, 0:128]), "dv")
        p.sw("vector", "dv")
        p.i("vector", lambda e: e.reduce_max(out=c.tsc[0:1, 0:1], in_=c.trow[:],
                                             axis=AX.X), "dv")
        p.sw("vector", "dv")
        dv_r = p.i("vector", lambda e: e.tensor_scalar(
            out=c.tsc[0:1, 1:2], in0=c.tsc[0:1, 0:1], scalar1=-SCALE, scalar2=None,
            op0=ALU.mult), "dv")
        p.w("tensor", "dv", dv_r)
        pe_b = p.i("tensor", lambda e: e.matmul(
            c.ps_misc[:, 0:1], c.ones_row[:], c.tsc[0:1, 1:2],
            start=True, stop=True), "pe")
        p.w("vector", "pe", pe_b)
        dv_mk_prev = p.i("vector", lambda e, h=h: e.tensor_copy(
            out=c.mks[:, h:h + 1], in_=c.ps_misc[:, 0:1]), "dv")
        p.sw("vector", "dv")
        p.i("vector", lambda e, h=h: e.tensor_scalar(
            out=c.bias_tm[:, h * 32:(h + 1) * 32],
            in0=c.normsb[:, h * 32:(h + 1) * 32],
            scalar1=c.mks[:, h:h + 1], scalar2=None, op0=ALU.mult), "dv")

    p.i("vector", lambda e: e.reduce_sum(out=c.c_t1[:], in_=c.spsums[:, 0:64],
                                         axis=AX.X), "dv")
    dv_s0 = p.i("vector", lambda e: e.reduce_sum(out=c.c_t2[:], in_=c.spsums[:, 64:96],
                                                 axis=AX.X), "dv")
    p.w("tensor", "dv", dv_s0)
    p.w("tensor", "dv", dv_mk_prev)
    pe_t1 = p.i("tensor", lambda e: e.transpose(
        c.ps_misc[0:1, 0:128], c.c_t1[:], c.ident_f[:]), "pe")
    p.w("vector", "pe", pe_t1)
    p.i("vector", lambda e: e.tensor_copy(out=c.trow[:], in_=c.ps_misc[0:1, 0:128]), "dv")
    p.sw("vector", "dv")
    dv_s1 = p.i("vector", lambda e: e.reduce_sum(
        out=c.svals[0:1, 0:1], in_=c.trow[:], axis=AX.X), "dv")
    p.w("tensor", "dv", dv_s1)
    pe_t2 = p.i("tensor", lambda e: e.transpose(
        c.ps_misc[0:1, 0:128], c.c_t2[:], c.ident_f[:]), "pe")
    p.w("vector", "pe", pe_t2)
    p.i("vector", lambda e: e.tensor_copy(out=c.trow[:], in_=c.ps_misc[0:1, 0:128]), "dv")
    p.sw("vector", "dv")
    dv_s2 = p.i("vector", lambda e: e.reduce_sum(
        out=c.svals[0:1, 1:2], in_=c.trow[:], axis=AX.X), "dv")
    p.w("gpsimd", "dv", dv_s2)
    p.i("gpsimd", lambda e: e.dma_start(out=c.d_ar_in, in_=c.svals[:]), "gd", 16)
    p.w("gpsimd", "gd")
    p.i("gpsimd", lambda e: e.collective_compute(
        "AllReduce", ALU.add, replica_groups=G8,
        ins=[c.d_ar_in], outs=[c.d_ar_out]), "cc", 1)
    p.w("gpsimd", "cc")
    p.i("gpsimd", lambda e: e.dma_start(out=c.svals[:], in_=c.d_ar_out), "gd", 16)
    c.GD_AR = cnt["gd"]
    p.w("vector", "gd", c.GD_AR)
    p.w("vector", "dv")
    p.i("vector", lambda e: e.reciprocal(out=c.tsc[0:1, 0:2], in_=c.svals[0:1, 0:2]), "dv")
    p.sw("vector", "dv")
    p.i("vector", lambda e: e.tensor_scalar(out=c.sfac[0:1, 0:1], in0=c.tsc[0:1, 0:1],
                                            scalar1=N_GV, scalar2=None, op0=ALU.mult), "dv")
    p.i("vector", lambda e: e.tensor_scalar(out=c.sfac[0:1, 1:2], in0=c.tsc[0:1, 1:2],
                                            scalar1=N_WO, scalar2=None, op0=ALU.mult), "dv")
    p.i("vector", lambda e: e.tensor_scalar(out=c.sfac[0:1, 2:3], in0=c.svals[0:1, 0:1],
                                            scalar1=1.0 / (254.0 * N_GV), scalar2=None,
                                            op0=ALU.mult), "dv")
    dv_f = p.i("vector", lambda e: e.tensor_scalar(
        out=c.sfac[0:1, 3:4], in0=c.svals[0:1, 1:2],
        scalar1=1.0 / (254.0 * N_WO), scalar2=None, op0=ALU.mult), "dv")
    p.w("tensor", "dv", dv_f)
    p.w("tensor", "dv", dv_mk_prev)
    for i, dst in enumerate((c.invs_gv, c.invs_wo, c.sgvf_b, c.swof_b)):
        pe_bb = p.i("tensor", lambda e, i=i: e.matmul(
            c.ps_misc[:, i:i + 1], c.ones_row[:], c.sfac[0:1, i:i + 1],
            start=True, stop=True), "pe")
        p.w("vector", "pe", pe_bb)
        p.i("vector", lambda e, dst=dst, i=i: e.tensor_copy(
            out=dst[:], in_=c.ps_misc[:, i:i + 1]), "dv")
    c.DV_FAC = cnt["dv"]

    if c.debug:
        dv_all = cnt["dv"]
        p.w("gpsimd", "dv", dv_all)
        p.w("gpsimd", "ac")
        for nm, src in [("dbg_q", c.q_bf), ("dbg_k", c.k_bf), ("dbg_v", c.v_tm)]:
            p.i("gpsimd", lambda e, nm=nm, src=src: e.dma_start(
                out=c.dbg[nm], in_=src[:]), "gd", 16)
        p.i("gpsimd", lambda e: e.dma_start(out=c.dbg["dbg_bias"][0:128, 0:128],
                                            in_=c.bias_tm[:]), "gd", 16)
        p.i("gpsimd", lambda e: e.dma_start(out=c.dbg["dbg_sfac"], in_=c.sfac[:]), "gd", 16)


def emit_p1(p, c):
    cnt = p.cnt
    wc_rel = 0
    for hc in range(4):
        p.w("sync", "dv", wc_rel)
        p.i("sync", lambda e, hc=hc: e.dma_start(
            out=c.p_raw[:, 0:2048],
            in_=c.d_woutc[hc * 128:(hc + 1) * 128, :]), "ld", 16)
        p.w("vector", "ld")
        wc_rel = p.i("vector", lambda e, hc=hc: e.tensor_copy(
            out=c.woutc_bf[:, hc * D:(hc + 1) * D], in_=c.p_raw[:, 0:2048]), "dv")
    z = []
    for buf in [c.p_raw] + c.p_nrm:
        z.append(p.i("vector", lambda e, buf=buf: e.memset(buf[:], 0.0), "dv"))
    dv_zero = cnt["dv"]

    tz_i = [0]
    tzf_rel = [(0, 0), (0, 0)]
    tzo_rel = [0, 0]
    ag_issued = [False]

    def tern_tiles(n):
        if c.no_tern:
            c.CC_AG = p.cnt["cc"]
            return
        for _ in range(n):
            i = tz_i[0]
            if i >= 96:
                if not ag_issued[0]:
                    ag_issued[0] = True
                    p.w("gpsimd", "ld")
                    p.w("gpsimd", "gd")
                    p.i("gpsimd", lambda e: e.collective_compute(
                        "AllGather", ALU.bypass, replica_groups=G8,
                        ins=[c.d_tgv_in], outs=[c.d_tgv_out]), "cc", 1)
                    p.i("gpsimd", lambda e: e.collective_compute(
                        "AllGather", ALU.bypass, replica_groups=G8,
                        ins=[c.d_two_in], outs=[c.d_two_out]), "cc", 1)
                    c.CC_AG = cnt["cc"]
                return
            tz_i[0] += 1
            b = i % 2
            if i < 64:
                pc, ct = i // 32, i % 32
                src = c.d_wgv[pc * 128:(pc + 1) * 128, ct * 512:(ct + 1) * 512]
                dst = c.d_tgv_in[pc * 128:(pc + 1) * 128, ct * 512:(ct + 1) * 512]
                inv = c.invs_gv
            else:
                jj = i - 64
                pc, ct = jj // 4, jj % 4
                src = c.d_wo[pc * 128:(pc + 1) * 128, ct * 512:(ct + 1) * 512]
                dst = c.d_two_in[pc * 128:(pc + 1) * 128, ct * 512:(ct + 1) * 512]
                inv = c.invs_wo
            p.w("scalar", "ac", tzf_rel[b][1])
            ldw = p.i("scalar", lambda e, src=src, b=b: e.dma_start(
                out=c.tz_f32[b][:], in_=src), "wd", 16)
            p.w("scalar", "wd", ldw)
            if i == 0:
                p.w("scalar", "dv", c.DV_FAC)
            if c.tern_mode == 5:
                p.i("scalar", lambda e, b=b: e.activation(
                    out=c.tz_s1[:], in_=c.tz_f32[b][:], func=AF.Sign), "ac")
                p.w("scalar", "ld", tzo_rel[b])
                ac_s2 = p.i("scalar", lambda e, b=b: e.activation(
                    out=c.tz_out[b][:], in_=c.tz_f32[b][:], func=AF.Sign), "ac")
            elif c.tern_mode == 6:
                p.i("scalar", lambda e, b=b, inv=inv: e.activation(
                    out=c.tz_s1[:], in_=c.tz_f32[b][:], func=AF.Sign,
                    scale=inv[:]), "ac")
                p.w("scalar", "ld", tzo_rel[b])
                ac_s2 = p.i("scalar", lambda e, b=b, inv=inv: e.activation(
                    out=c.tz_out[b][:], in_=c.tz_f32[b][:], func=AF.Sign,
                    scale=inv[:]), "ac")
            elif c.tern_mode == 7:
                p.i("scalar", lambda e, b=b: e.activation(
                    out=c.tz_s1[:], in_=c.tz_f32[b][:], func=AF.Identity,
                    bias=c.half_neg[:]), "ac")
                p.w("scalar", "ld", tzo_rel[b])
                ac_s2 = p.i("scalar", lambda e, b=b: e.activation(
                    out=c.tz_out[b][:], in_=c.tz_f32[b][:], func=AF.Identity,
                    bias=c.half_pos[:]), "ac")
            elif c.tern_mode >= 2:
                p.i("scalar", lambda e, b=b, inv=inv: e.activation(
                    out=c.tz_s1[:], in_=c.tz_f32[b][:], func=AF.Sign,
                    scale=inv[:], bias=c.half_neg[:]), "ac")
                p.w("scalar", "ld", tzo_rel[b])
                ac_s2 = p.i("scalar", lambda e, b=b, inv=inv: e.activation(
                    out=c.tz_out[b][:], in_=c.tz_f32[b][:], func=AF.Sign,
                    scale=inv[:], bias=c.half_pos[:]), "ac")
            else:
                p.w("scalar", "ld", tzo_rel[b])
                ac_s2 = p.i("scalar", lambda e, b=b: e.activation(
                    out=c.tz_out[b][:], in_=c.tz_f32[b][:], func=AF.Identity), "ac")
            tzf_rel[b] = (0, ac_s2)
            p.w("vector", "ac", ac_s2)
            if c.tern_mode >= 3:
                dv_t = p.i("vector", lambda e, b=b: e.tensor_tensor(
                    out=c.tz_out[b][:], in0=c.tz_s1[:], in1=c.tz_out[b][:],
                    op=ALU.add), "dv")
            else:
                dv_t = p.i("vector", lambda e, b=b: e.tensor_scalar(
                    out=c.tz_out[b][:], in0=c.tz_out[b][:], scalar1=1.0,
                    scalar2=None, op0=ALU.mult), "dv")
            p.w("sync", "dv", dv_t)
            tzo_rel[b] = p.i("sync", lambda e, b=b, dst=dst: e.dma_start(
                out=dst, in_=c.tz_out[b][:]), "ld", 16)

    praw_rel = 0
    pnrm_rel = [0, 0, 0, 0]
    pss_rel = [0, 0]
    pspt_rel = [0, 0]
    ptsb_rel = [0, 0]
    psav_rel = [0, 0]
    avsb_rel = [0, 0]
    psw_rel = [0, 0]
    wost_rel = [0, 0]

    si = [0]
    ti = [0]
    p.w("tensor", "dv", dv_zero)

    for j in range(8):
        for h in range(4):
            av_i = (j * 4 + h) % 2
            for l in range(4):
                qc = 4 * j + l
                nseg = j + 1
                exp_acs = []
                for seg in range(nseg):
                    n = 512 if seg < j else 128 * (l + 1)
                    b = si[0] % 2
                    si[0] += 1
                    p.w("tensor", "ac", pss_rel[b])
                    pe_s = p.i("tensor", lambda e, h=h, qc=qc, seg=seg, n=n, b=b: e.matmul(
                        c.ps_s[b][:, 0:n],
                        c.q_bf[:, h * T + qc * 128:h * T + (qc + 1) * 128],
                        c.k_bf[:, h * T + seg * 512: h * T + seg * 512 + n],
                        start=True, stop=True), "pe")
                    if c.debug and j == 7 and h == 0 and l == 0 and seg == 0:
                        p.w("vector", "pe", pe_s)
                        dv_tap = p.i("vector", lambda e, b=b: e.tensor_copy(
                            out=c.wo_stage[0][:, 0:512], in_=c.ps_s[b][:]), "dv")
                        p.w("gpsimd", "dv", dv_tap)
                        p.i("gpsimd", lambda e: e.dma_start(
                            out=c.dbg["dbg_s"][0:128, 0:256],
                            in_=c.wo_stage[0][:, 0:256]), "gd", 16)
                        p.i("gpsimd", lambda e, h=h, qc=qc: e.dma_start(
                            out=c.dbg["dbg_qs"],
                            in_=c.q_bf[:, h * T + qc * 128: h * T + (qc + 1) * 128]), "gd", 16)
                        gd_t = p.i("gpsimd", lambda e, h=h: e.dma_start(
                            out=c.dbg["dbg_ks"],
                            in_=c.k_bf[:, h * T: h * T + 512]), "gd", 16)
                        p.w("scalar", "gd", gd_t)
                        p.w("tensor", "gd", gd_t)
                    if seg == j:
                        p.w("vector", "pe", pe_s)
                        dv_msk = p.i("vector", lambda e, b=b, l=l: e.tensor_tensor(
                            out=c.ps_s[b][:, l * 128:(l + 1) * 128],
                            in0=c.ps_s[b][:, l * 128:(l + 1) * 128],
                            in1=c.tri_neg[:], op=ALU.add), "dv")
                        p.w("scalar", "dv", dv_msk)
                    p.w("scalar", "pe", pe_s)
                    if seg == 0:
                        p.w("scalar", "dv", praw_rel)
                    pss_rel[b] = p.i("scalar", lambda e, b=b, seg=seg, n=n, h=h, qc=qc: e.activation(
                        out=c.p_raw[:, seg * 512: seg * 512 + n],
                        in_=c.ps_s[b][:, 0:n], func=AF.Exp,
                        bias=c.bias_tm[:, h * 32 + qc: h * 32 + qc + 1],
                        scale=SCALE, accum_out=c.ell_segs[:, seg:seg + 1]), "ac")
                ac_exp = cnt["ac"]
                p.w("vector", "ac", ac_exp)
                p.i("vector", lambda e, nseg=nseg: e.reduce_sum(
                    out=c.c_ell[:], in_=c.ell_segs[:, 0:nseg], axis=AX.X), "dv")
                p.w("vector", "dv")
                p.i("vector", lambda e: e.reciprocal(out=c.c_rinv[:], in_=c.c_ell[:]), "dv")
                p.sw("vector", "dv")
                ext = 512 * j + 128 * (l + 1)
                p.w("vector", "pe", pnrm_rel[l])
                praw_rel = p.i("vector", lambda e, l=l, ext=ext: e.tensor_scalar(
                    out=c.p_nrm[l][:, 0:ext], in0=c.p_raw[:, 0:ext],
                    scalar1=c.c_rinv[:], scalar2=None, op0=ALU.mult), "dv")
            dv_nrm = cnt["dv"]
            if c.debug and j == 7 and h == 0:
                p.w("gpsimd", "dv", dv_nrm)
                for l in range(4):
                    p.i("gpsimd", lambda e, l=l: e.dma_start(
                        out=c.dbg["dbg_pnh0"][l], in_=c.p_nrm[l][:]), "gd", 16)
                gd_tap = p.i("gpsimd", lambda e: e.dma_start(
                    out=c.dbg["dbg_prh0"], in_=c.p_raw[:]), "gd", 16)
                p.w("tensor", "gd", gd_tap)
                p.w("scalar", "gd", gd_tap)

            p.w("tensor", "dv", dv_nrm)
            p.w("tensor", "ac", psav_rel[av_i])
            first = True
            for seg in range(j + 1):
                for kc in range(4):
                    tb = ti[0] % 2
                    ti[0] += 1
                    p.w("tensor", "dv", pspt_rel[tb])
                    for l in range(4):
                        fn = lambda e, seg=seg, kc=kc, l=l, tb=tb: e.transpose(
                            c.ps_pt[tb][:, l * 128:(l + 1) * 128],
                            c.p_nrm[l][:, seg * 512 + kc * 128: seg * 512 + (kc + 1) * 128],
                            c.ident_b[:])
                        if l < 3:
                            p.q("tensor", fn)
                        else:
                            pe_pt = p.i("tensor", fn, "pe")
                    p.w("vector", "pe", pe_pt)
                    p.w("vector", "pe", ptsb_rel[tb])
                    pspt_rel[tb] = p.i("vector", lambda e, tb=tb: e.tensor_copy(
                        out=c.pt_sb[tb][:], in_=c.ps_pt[tb][:]), "dv")
                    p.w("tensor", "dv", pspt_rel[tb])
                    kci = seg * 4 + kc
                    last = (seg == j and kc == 3)
                    fn = lambda e, h=h, kci=kci, tb=tb, first=first, last=last, av_i=av_i: e.matmul(
                        c.ps_av[av_i][:],
                        c.v_tm[:, (h * 32 + kci) * 128:(h * 32 + kci + 1) * 128],
                        c.pt_sb[tb][:], start=first, stop=last,
                        skip_group_check=True)
                    ptsb_rel[tb] = p.i("tensor", fn, "pe")
                    first = False
            pnrm_rel = [cnt["pe"]] * 4
            p.w("scalar", "pe")
            if h == 0:
                p.w("scalar", "pe", avsb_rel[j % 2])
            psav_rel[av_i] = p.i("scalar", lambda e, h=h, av_i=av_i, j=j: e.activation(
                out=c.av_sb[j % 2][:, h * 512:(h + 1) * 512],
                in_=c.ps_av[av_i][:], func=AF.Copy), "ac")
            tern_tiles(4)
        ac_av = cnt["ac"]

        for tl in range(4):
            if c.no_wout:
                break
            wb = tl % 2
            p.w("tensor", "ac", ac_av)
            for fc in range(4):
                p.w("tensor", "dv", psw_rel[wb])
                for hc in range(4):
                    fn = lambda e, hc=hc, fc=fc, tl=tl, wb=wb, j=j: e.matmul(
                        c.ps_w[wb][:],
                        c.av_sb[j % 2][:, hc * 512 + tl * 128: hc * 512 + (tl + 1) * 128],
                        c.woutc_bf[:, hc * D + fc * 512: hc * D + fc * 512 + 512],
                        start=(hc == 0), stop=(hc == 3))
                    if hc < 3:
                        p.q("tensor", fn)
                    else:
                        pe_w = p.i("tensor", fn, "pe")
                p.w("vector", "pe", pe_w)
                p.w("vector", "gd", wost_rel[wb])
                psw_rel[wb] = p.i("vector", lambda e, fc=fc, wb=wb: e.tensor_copy(
                    out=c.wo_stage[wb][:, fc * 512:(fc + 1) * 512],
                    in_=c.ps_w[wb][:]), "dv")
            p.w("gpsimd", "dv", psw_rel[wb])
            dst = (c.d_rs_inA if j % 2 == 0 else c.d_rs_inB)
            row = (j // 2) * 512 + tl * 128
            wost_rel[wb] = p.i("gpsimd", lambda e, row=row, wb=wb, dst=dst: e.dma_start(
                out=dst[row:row + 128, :], in_=c.wo_stage[wb][:]), "gd", 16)
        avsb_rel[j % 2] = cnt["pe"]
        if j == 6:
            p.w("gpsimd", "gd")
            p.i("gpsimd", lambda e: e.collective_compute(
                "ReduceScatter", ALU.add, replica_groups=G4,
                ins=[c.d_rs_inA], outs=[c.d_rs_outA]), "cc", 1)
            c.CC_RSA = cnt["cc"]

    tern_tiles(200)

    if c.debug:
        p.w("gpsimd", "ac")
        p.w("gpsimd", "dv")
        p.i("gpsimd", lambda e: e.dma_start(out=c.dbg["dbg_av"],
                                            in_=c.av_sb[7 % 2][:]), "gd", 16)
        for l in range(4):
            p.i("gpsimd", lambda e, l=l: e.dma_start(out=c.dbg[f"dbg_pn{l}"],
                                                     in_=c.p_nrm[l][:]), "gd", 16)
        p.i("gpsimd", lambda e: e.dma_start(out=c.dbg["dbg_praw"], in_=c.p_raw[:]), "gd", 16)
        p.i("gpsimd", lambda e: e.dma_start(out=c.dbg["dbg_ell"], in_=c.ell_segs[:]), "gd", 16)

    p.w("gpsimd", "gd")
    p.w("gpsimd", "ld")
    p.i("gpsimd", lambda e: e.collective_compute(
        "ReduceScatter", ALU.add, replica_groups=G4,
        ins=[c.d_rs_inB], outs=[c.d_rs_outB]), "cc", 1)
    c.CC_RS = cnt["cc"]
    c.DV_P1 = cnt["dv"]


def emit_p2a(p, c):
    cnt = p.cnt
    p.w("sync", "cc", c.CC_RS)
    p.i("sync", lambda e: e.dma_start(out=c.bout_b[:], in_=c.d_bout), "ld", 16)
    p.i("sync", lambda e: e.dma_start(out=c.gamma_b[:], in_=c.d_gamma), "ld", 16)
    LD_BG = cnt["ld"]

    if c.debug:
        p.w("gpsimd", "cc", c.CC_RS)
        p.i("gpsimd", lambda e: e.dma_start(out=c.dbg["dbg_rso"][0:512, :],
                                            in_=c.d_rs_outA), "gd", 16)
        p.i("gpsimd", lambda e: e.dma_start(out=c.dbg["dbg_rso"][512:1024, :],
                                            in_=c.d_rs_outB), "gd", 16)
    rs_rel = [0, 0]
    xo_rel = [0, 0]
    pst2_rel = [0, 0]
    q1fm_parts = []

    p.w("sync", "cc", c.CC_RS)
    for tc in range(8):
        b = tc % 2
        p.w("sync", "dv", rs_rel[b])
        rsrc = (c.d_rs_outA[tc * 128:(tc + 1) * 128, :] if tc < 4
                else c.d_rs_outB[(tc - 4) * 128:(tc - 3) * 128, :])
        ld_rs = p.i("sync", lambda e, rsrc=rsrc, b=b: e.dma_start(
            out=c.rsx[b][:], in_=rsrc), "ld", 16)
        p.w("sync", "dv", xo_rel[b])
        ld_xo = p.i("sync", lambda e, tc=tc, b=b: e.dma_start(
            out=c.xo_f32[b][:], in_=c.d_xown[tc * 128:(tc + 1) * 128, :]), "ld", 16)
        p.w("vector", "ld", ld_rs)
        if tc == 0:
            p.w("vector", "ld", LD_BG)
        rs_rel[b] = p.i("vector", lambda e, b=b: e.tensor_tensor(
            out=c.ln_t1[:], in0=c.rsx[b][:], in1=c.bout_b[:], op=ALU.add), "dv")
        if c.debug and tc == 0:
            p.w("gpsimd", "dv")
            p.i("gpsimd", lambda e: e.dma_start(out=c.dbg["dbg_o1"], in_=c.ln_t1[:]), "gd", 16)
        p.w("scalar", "dv")
        _ln_stats(p, c, c.ln_t1[:], c.ln_t2[:], 1.0 / D)
        p.i("vector", lambda e: e.tensor_scalar(
            out=c.ln_t2[:], in0=c.ln_t1[:], scalar1=c.c_mu[:], scalar2=c.c_rstd[:],
            op0=ALU.add, op1=ALU.mult), "dv")
        p.w("vector", "ld", ld_xo)
        p.sw("vector", "dv")
        xo_rel[b] = p.i("vector", lambda e, tc=tc, b=b: e.tensor_tensor(
            out=c.r1[:, tc * D:(tc + 1) * D], in0=c.ln_t2[:], in1=c.xo_f32[b][:],
            op=ALU.add), "dv")
        p.w("scalar", "dv")
        _ln_stats(p, c, c.r1[:, tc * D:(tc + 1) * D], c.ln_t2[:], 1.0 / D)
        p.i("vector", lambda e, tc=tc: e.tensor_copy(
            out=c.ln3mu[:, tc:tc + 1], in_=c.c_mu[:]), "dv")
        p.i("vector", lambda e, tc=tc: e.tensor_copy(
            out=c.ln3rs[:, tc:tc + 1], in_=c.c_rstd[:]), "dv")
        dv_x2 = p.i("vector", lambda e, tc=tc: e.tensor_scalar(
            out=c.ln_t1[:], in0=c.r1[:, tc * D:(tc + 1) * D],
            scalar1=c.c_mu[:], scalar2=c.c_rstd[:], op0=ALU.add, op1=ALU.mult), "dv")
        if c.debug and tc == 0:
            p.w("gpsimd", "dv", dv_x2)
            gdt = p.i("gpsimd", lambda e: e.dma_start(out=c.dbg["dbg_x2"], in_=c.ln_t1[:]), "gd", 16)
            p.w("scalar", "gd", gdt)
            p.w("vector", "gd", gdt)
        p.w("scalar", "dv", dv_x2)
        p.i("scalar", lambda e: e.activation(out=c.ln_t2[:], in_=c.ln_t1[:],
                                             func=AF.Square, accum_out=c.c_ss[:]), "ac")
        p.w("scalar", "dv", getattr(c, "_recip_dv", 0))
        p.sw("scalar", "ac")
        ac_r = p.i("scalar", lambda e: e.activation(out=c.c_std[:], in_=c.c_ss[:],
                                                    func=AF.Sqrt, bias=c.eps_col[:],
                                                    scale=1.0 / D), "ac")
        c._sqrt_ac = ac_r
        p.w("vector", "ac", ac_r)
        c._recip_dv = p.i("vector", lambda e: e.reciprocal(out=c.c_rstd[:], in_=c.c_std[:]), "dv")
        p.sw("vector", "dv")
        p.i("vector", lambda e: e.tensor_scalar(
            out=c.ln_t1[:], in0=c.ln_t1[:], scalar1=c.c_rstd[:], scalar2=None,
            op0=ALU.mult), "dv")
        p.sw("vector", "dv")
        p.i("vector", lambda e: e.tensor_tensor(
            out=c.ln_t2[:], in0=c.ln_t1[:], in1=c.gamma_b[:], op=ALU.mult), "dv")
        p.i("vector", lambda e: e.reduce_max(out=c.c_am[:], in_=c.ln_t2[:], axis=AX.X,
                                             apply_absolute_value=True), "dv")
        p.sw("vector", "dv")
        p.i("vector", lambda e: e.tensor_scalar(
            out=c.c_amc[:], in0=c.c_am[:], scalar1=1e-5, scalar2=None,
            op0=ALU.max), "dv")
        p.w("vector", "dv")
        p.i("vector", lambda e: e.reciprocal(out=c.c_t1[:], in_=c.c_amc[:]), "dv")
        p.w("vector", "dv")
        p.i("vector", lambda e: e.tensor_scalar(
            out=c.c_rs1[:], in0=c.c_t1[:], scalar1=127.0, scalar2=None,
            op0=ALU.mult), "dv")
        p.i("vector", lambda e, tc=tc: e.scalar_tensor_tensor(
            out=c.d_allc[:, tc:tc + 1], in0=c.c_amc[:], scalar=1.0,
            in1=c.sgvf_b[:], op0=ALU.mult, op1=ALU.mult), "dv")
        p.sw("vector", "dv")
        p.i("vector", lambda e: e.tensor_scalar(
            out=c.ln_t1[:], in0=c.ln_t2[:], scalar1=c.c_rs1[:], scalar2=RT,
            op0=ALU.mult, op1=ALU.add), "dv")
        p.sw("vector", "dv")
        dv_q1 = p.i("vector", lambda e: e.tensor_scalar(
            out=c.q1_bf[:], in0=c.ln_t1[:], scalar1=RT, scalar2=None,
            op0=ALU.subtract), "dv")
        p.w("tensor", "dv", dv_q1)
        for g in range(4):
            tb = g % 2
            p.w("tensor", "dv", pst2_rel[tb])
            for kk in range(4):
                dcg = g * 4 + kk
                fn = lambda e, dcg=dcg, kk=kk, tb=tb: e.transpose(
                    c.ps_t2[tb][:, kk * 128:(kk + 1) * 128],
                    c.q1_bf[:, dcg * 128:(dcg + 1) * 128], c.ident_b[:])
                if kk < 3:
                    p.q("tensor", fn)
                else:
                    pe_t = p.i("tensor", fn, "pe")
            p.w("vector", "pe", pe_t)
            dst = c.q1_fm[:].rearrange("p (dc t) -> p dc t", dc=16)[:, g * 4:(g + 1) * 4,
                                                                   tc * 128:(tc + 1) * 128]
            src = c.ps_t2[tb][:].rearrange("p (a b) -> p a b", a=4)
            pst2_rel[tb] = p.i("vector", lambda e, dst=dst, src=src: e.tensor_copy(
                out=dst, in_=src), "dv")
    c.DV_P2A = cnt["dv"]
    c.PST2A = list(pst2_rel)
    if c.debug:
        p.w("gpsimd", "dv", cnt["dv"])
        p.i("gpsimd", lambda e: e.dma_start(out=c.dbg["dbg_r1"], in_=c.r1[:]), "gd", 16)
        p.i("gpsimd", lambda e: e.dma_start(out=c.dbg["dbg_q1"], in_=c.q1_fm[:]), "gd", 16)
        p.i("gpsimd", lambda e: e.dma_start(out=c.dbg["dbg_d"], in_=c.d_allc[:]), "gd", 16)


def emit_p2b(p, c):
    cnt = p.cnt
    pwh_rel = [0, 0, 0]
    pwh_ld = [0] * 16
    silu_rel = 0
    h2_rel = [0, 0]
    psvg_rel = [0, 0]
    psvg_rel_dv = [0, 0]

    def load_pair(pr):
        wb = pr % 3
        p.w("sync", "pe", pwh_rel[wb])
        if pr == 0:
            p.w("sync", "cc", c.CC_AG)
            p.w("sync", "cc", c.CC_RS)
            p.w("sync", "dv", c.DV_P2A)
        for hf in range(2):
            srcv = c.d_tgv_out[hf * 1024:(hf + 1) * 1024, pr * 512:(pr + 1) * 512]
            srcv = srcv.rearrange("(dc q) m -> q dc m", q=128)
            srcg = c.d_tgv_out[hf * 1024:(hf + 1) * 1024,
                               8192 + pr * 512: 8192 + (pr + 1) * 512]
            srcg = srcg.rearrange("(dc q) m -> q dc m", q=128)
            dstv = c.pwh[wb][:, hf * 8192:(hf + 1) * 8192].rearrange(
                "q (dc m) -> q dc m", m=1024)[:, :, 0:512]
            dstg = c.pwh[wb][:, hf * 8192:(hf + 1) * 8192].rearrange(
                "q (dc m) -> q dc m", m=1024)[:, :, 512:1024]
            p.i("sync", lambda e, dstv=dstv, srcv=srcv: e.dma_start(
                out=dstv, in_=srcv), "ld", 16)
            pwh_ld[pr] = p.i("sync", lambda e, dstg=dstg, srcg=srcg: e.dma_start(
                out=dstg, in_=srcg), "ld", 16)

    for pr in range(3):
        load_pair(pr)
    for pr in range(16):
        wb = pr % 3
        for tc in range(8):
            p.w("tensor", "ld", pwh_ld[pr])
            p.w("tensor", "dv", psvg_rel_dv[0])
            p.w("tensor", "ac", psvg_rel[1])
            if tc == 0 and pr == 0:
                p.w("tensor", "dv", c.DV_P2A)
            for dc in range(16):
                for gg in range(2):
                    fn = lambda e, dc=dc, gg=gg, tc=tc, wb=wb: e.matmul(
                        c.ps_a[gg][:],
                        c.q1_fm[:, dc * 1024 + tc * 128: dc * 1024 + (tc + 1) * 128],
                        c.pwh[wb][:, dc * 1024 + gg * 512: dc * 1024 + (gg + 1) * 512],
                        start=(dc == 0), stop=(dc == 15), skip_group_check=True)
                    if dc == 15 and gg == 1:
                        pe_mm = p.i("tensor", fn, "pe")
                    else:
                        p.q("tensor", fn)
            p.w("scalar", "pe", pe_mm)
            p.w("scalar", "dv", silu_rel)
            ac_g = p.i("scalar", lambda e, tc=tc: e.activation(
                out=c.silu_sb[:], in_=c.ps_a[1][:], func=AF.Silu,
                scale=c.d_allc[:, tc:tc + 1]), "ac")
            psvg_rel[1] = ac_g
            p.w("vector", "ac", ac_g)
            p.w("vector", "ld", h2_rel[tc % 2])
            dv_h = p.i("vector", lambda e, tc=tc: e.scalar_tensor_tensor(
                out=c.h2st[tc % 2][:], in0=c.ps_a[0][:], scalar=1.0,
                in1=c.silu_sb[:], op0=ALU.mult, op1=ALU.mult), "dv")
            silu_rel = dv_h
            psvg_rel_dv[0] = dv_h
            p.w("sync", "dv", dv_h)
            h2_rel[tc % 2] = p.i("sync", lambda e, tc=tc, pr=pr: e.dma_start(
                out=c.d_h2[tc, :, pr * 512:(pr + 1) * 512], in_=c.h2st[tc % 2][:]),
                "ld", 16)
        pwh_rel[wb] = cnt["pe"]
        if pr + 3 < 16:
            load_pair(pr + 3)
    c.LD_P2B = cnt["ld"]
    c.DV_P2B = cnt["dv"]
    c.AC_P2B = cnt["ac"]


def emit_p2c(p, c):
    cnt = p.cnt
    h2_rel = [0, 0]
    pst2_rel = list(c.PST2A)
    q2fm_rel = 0

    p.w("sync", "ld", c.LD_P2B)
    for tc in range(8):
        for hf in range(2):
            b = hf
            p.w("sync", "dv", h2_rel[b])
            p.i("sync", lambda e, tc=tc, hf=hf, b=b: e.dma_start(
                out=c.h2l[b][:], in_=c.d_h2[tc, :, hf * 4096:(hf + 1) * 4096]), "ld", 16)
        LD_H2 = cnt["ld"]
        p.w("vector", "ld", LD_H2)
        p.i("vector", lambda e: e.reduce_max(out=c.c_t1[:], in_=c.h2l[0][:], axis=AX.X,
                                             apply_absolute_value=True), "dv")
        p.i("vector", lambda e: e.reduce_max(out=c.c_t2[:], in_=c.h2l[1][:], axis=AX.X,
                                             apply_absolute_value=True), "dv")
        p.sw("vector", "dv")
        p.i("vector", lambda e: e.tensor_tensor(out=c.c_am[:], in0=c.c_t1[:],
                                                in1=c.c_t2[:], op=ALU.max), "dv")
        p.sw("vector", "dv")
        p.i("vector", lambda e, tc=tc: e.scalar_tensor_tensor(
            out=c.c_am[:], in0=c.c_am[:], scalar=1.0, in1=c.d_allc[:, tc:tc + 1],
            op0=ALU.mult, op1=ALU.mult), "dv")
        p.sw("vector", "dv")
        p.i("vector", lambda e: e.tensor_scalar(out=c.c_amc[:], in0=c.c_am[:],
                                                scalar1=1e-5, scalar2=None,
                                                op0=ALU.max), "dv")
        p.sw("vector", "dv")
        p.i("vector", lambda e, tc=tc: e.scalar_tensor_tensor(
            out=c.d2_allc[:, tc:tc + 1], in0=c.c_amc[:], scalar=1.0, in1=c.swof_b[:],
            op0=ALU.mult, op1=ALU.mult), "dv")
        p.w("vector", "dv")
        p.i("vector", lambda e: e.reciprocal(out=c.c_t1[:], in_=c.c_amc[:]), "dv")
        p.w("vector", "dv")
        p.i("vector", lambda e: e.tensor_scalar(out=c.c_t1[:], in0=c.c_t1[:],
                                                scalar1=127.0, scalar2=None,
                                                op0=ALU.mult), "dv")
        p.sw("vector", "dv")
        p.i("vector", lambda e, tc=tc: e.scalar_tensor_tensor(
            out=c.c_f2[:], in0=c.c_t1[:], scalar=1.0, in1=c.d_allc[:, tc:tc + 1],
            op0=ALU.mult, op1=ALU.mult), "dv")
        for hf in range(2):
            p.sw("vector", "dv")
            p.i("vector", lambda e, hf=hf: e.tensor_scalar(
                out=c.h2l[hf][:], in0=c.h2l[hf][:], scalar1=c.c_f2[:], scalar2=RT,
                op0=ALU.mult, op1=ALU.add), "dv")
            p.sw("vector", "dv")
            h2_rel[hf] = p.i("vector", lambda e, hf=hf: e.tensor_scalar(
                out=c.q2_bf[:, hf * 4096:(hf + 1) * 4096], in0=c.h2l[hf][:],
                scalar1=RT, scalar2=None, op0=ALU.subtract), "dv")
        dv_q2 = cnt["dv"]
        p.w("tensor", "dv", dv_q2)
        p.w("vector", "ld", q2fm_rel)
        for g in range(16):
            tb = g % 2
            p.w("tensor", "dv", pst2_rel[tb])
            for kk in range(4):
                hcg = g * 4 + kk
                fn = lambda e, hcg=hcg, kk=kk, tb=tb: e.transpose(
                    c.ps_t2[tb][:, kk * 128:(kk + 1) * 128],
                    c.q2_bf[:, hcg * 128:(hcg + 1) * 128], c.ident_b[:])
                if kk < 3:
                    p.q("tensor", fn)
                else:
                    pe_t = p.i("tensor", fn, "pe")
            p.w("vector", "pe", pe_t)
            pst2_rel[tb] = p.i("vector", lambda e, g=g, tb=tb: e.tensor_copy(
                out=c.q2fm_st[:, g * 512:(g + 1) * 512], in_=c.ps_t2[tb][:]), "dv")
        p.w("sync", "dv", pst2_rel[1])
        q2fm_rel = p.i("sync", lambda e, tc=tc: e.dma_start(
            out=c.d_q2[tc], in_=c.q2fm_st[:]), "ld", 16)
    c.LD_Q2 = cnt["ld"]


def emit_p2d(p, c):
    cnt = p.cnt
    q2_rel = [0, 0]
    o2st_rel = [0, 0]
    psa_rel = [0, 0]
    wD_rel = 0
    p.w("tensor", "dv", c.DV_P2B)
    p.w("tensor", "ac", c.AC_P2B)
    p.w("scalar", "ld", c.LD_Q2)

    for fc in range(4):
        p.w("scalar", "pe", wD_rel)
        for hg in range(8):
            srcw = c.d_two_out[hg * 1024:(hg + 1) * 1024, fc * 512:(fc + 1) * 512]
            srcw = srcw.rearrange("(hc q) m -> q hc m", q=128)
            dstw = c.wD[:, hg * 4096:(hg + 1) * 4096].rearrange(
                "q (hc m) -> q hc m", m=512)
            p.i("scalar", lambda e, srcw=srcw, dstw=dstw: e.dma_start(
                out=dstw, in_=srcw), "wd", 16)
        WD_W = cnt["wd"]
        for tc in range(8):
            b = tc % 2
            p.w("sync", "pe", q2_rel[b])
            if fc == 0 and tc == 0:
                p.w("sync", "ld", c.LD_Q2)
            ld_q = p.i("sync", lambda e, tc=tc, b=b: e.dma_start(
                out=c.q2sb[b][:], in_=c.d_q2[tc]), "ld", 16)
            ab = tc % 2
            p.w("tensor", "ld", ld_q)
            p.w("tensor", "wd", WD_W)
            p.w("tensor", "ac", psa_rel[ab])
            for hc in range(64):
                fn = lambda e, hc=hc, b=b, ab=ab: e.matmul(
                    c.ps_a[ab][:],
                    c.q2sb[b][:, hc * 128:(hc + 1) * 128],
                    c.wD[:, hc * 512:(hc + 1) * 512],
                    start=(hc == 0), stop=(hc == 63))
                if hc < 63:
                    p.q("tensor", fn)
                else:
                    pe_o = p.i("tensor", fn, "pe")
            q2_rel[b] = pe_o
            p.w("scalar", "pe", pe_o)
            p.w("scalar", "ld", o2st_rel[tc % 2])
            ac_e = p.i("scalar", lambda e, tc=tc, ab=ab: e.activation(
                out=c.o2st[tc % 2][:], in_=c.ps_a[ab][:], func=AF.Copy,
                scale=c.d2_allc[:, tc:tc + 1]), "ac")
            psa_rel[ab] = ac_e
            p.w("sync", "ac", ac_e)
            o2st_rel[tc % 2] = p.i("sync", lambda e, tc=tc, fc=fc: e.dma_start(
                out=c.d_o2[tc, :, fc * 512:(fc + 1) * 512], in_=c.o2st[tc % 2][:]),
                "ld", 16)
        wD_rel = cnt["pe"]
    c.LD_O2 = cnt["ld"]


def emit_tail(p, c):
    cnt = p.cnt
    o2_rel = [0, 0]
    y_rel = [0, 0]
    p.w("sync", "ld", c.LD_O2)
    for tc in range(8):
        b = tc % 2
        p.w("sync", "dv", o2_rel[b])
        ld_o = p.i("sync", lambda e, tc=tc, b=b: e.dma_start(
            out=c.o2sb[b][:], in_=c.d_o2[tc]), "ld", 16)
        p.i("vector", lambda e, tc=tc: e.tensor_scalar(
            out=c.tl_t1[:], in0=c.r1[:, tc * D:(tc + 1) * D],
            scalar1=c.ln3mu[:, tc:tc + 1], scalar2=c.ln3rs[:, tc:tc + 1],
            op0=ALU.add, op1=ALU.mult), "dv")
        p.w("vector", "ld", ld_o)
        p.sw("vector", "dv")
        o2_rel[b] = p.i("vector", lambda e, b=b: e.tensor_tensor(
            out=c.tl_t1[:], in0=c.tl_t1[:], in1=c.o2sb[b][:], op=ALU.add), "dv")
        p.w("vector", "ld", y_rel[b])
        p.w("scalar", "ld", y_rel[b])
        p.w("scalar", "dv")
        _ln_stats(p, c, c.tl_t1[:], c.y_st[b][:], 1.0 / D)
        p.i("vector", lambda e: e.tensor_scalar(
            out=c.tl_t2[:], in0=c.tl_t1[:], scalar1=c.c_mu[:], scalar2=c.c_rstd[:],
            op0=ALU.add, op1=ALU.mult), "dv")
        p.sw("vector", "dv")
        dv_y = p.i("vector", lambda e, tc=tc, b=b: e.tensor_tensor(
            out=c.y_st[b][:], in0=c.tl_t2[:], in1=c.r1[:, tc * D:(tc + 1) * D],
            op=ALU.add), "dv")
        p.w("sync", "dv", dv_y)
        y_rel[b] = p.i("sync", lambda e, tc=tc, b=b: e.dma_start(
            out=c.d_y[tc * 128:(tc + 1) * 128, :], in_=c.y_st[b][:]), "ld", 16)
    p.w("sync", "ld")
    p.w("gpsimd", "gd")


def shard_inputs(x, Wqkv, bqkv, Wout, bout, norm_w, w_gv, w_out):
    x = np.ascontiguousarray(np.asarray(x, dtype=np.float32))
    Wqkv = np.asarray(Wqkv, dtype=np.float32)
    bqkv = np.asarray(bqkv, dtype=np.float32)
    Wout = np.asarray(Wout, dtype=np.float32)
    bout = np.asarray(bout, dtype=np.float32)
    norm_w = np.asarray(norm_w, dtype=np.float32)
    wgvt = np.ascontiguousarray(np.asarray(w_gv, dtype=np.float32).T)
    woutt = np.ascontiguousarray(np.asarray(w_out, dtype=np.float32).T)
    bout_b = np.ascontiguousarray(np.broadcast_to(bout, (128, D)))
    gamma_b = np.ascontiguousarray(np.broadcast_to(norm_w, (128, D)))
    in_maps = []
    for cid in range(8):
        b, r = cid // 4, cid % 4
        heads = range(4 * r, 4 * r + 4)
        cols = []
        for role in range(3):
            for h in heads:
                cols.extend(range(role * D + h * 128, role * D + (h + 1) * 128))
        cols = np.asarray(cols)
        in_maps.append({
            "xb": x[b],
            "xown": np.ascontiguousarray(x[b, r * 1024:(r + 1) * 1024]),
            "wqkv": np.ascontiguousarray(Wqkv[:, cols]),
            "bqkv": np.ascontiguousarray(bqkv[cols]),
            "woutc": np.ascontiguousarray(Wout[4 * r * 128:(4 * r + 4) * 128, :]),
            "boutb": bout_b,
            "gammab": gamma_b,
            "wgv_sh": np.ascontiguousarray(wgvt[cid * 256:(cid + 1) * 256, :]),
            "wo_sh": np.ascontiguousarray(woutt[cid * 1024:(cid + 1) * 1024, :]),
        })
    return in_maps


def assemble(results):
    out = np.empty((2, T, D), dtype=np.float32)
    for cid in range(8):
        b, r = cid // 4, cid % 4
        out[b, r * 1024:(r + 1) * 1024, :] = results[cid]["y"]
    return out

from concourse.bass_utils import run_bass_kernel_spmd

_NC_CACHE = [None]


def kernel(L0_attn_x, L0_attn_Wqkv, L0_attn_bqkv, L0_attn_Wout, L0_attn_bout,
           L0_ffn_norm_w, L0_ffn_w_gv, L0_ffn_w_out):
    in_maps = shard_inputs(L0_attn_x, L0_attn_Wqkv, L0_attn_bqkv, L0_attn_Wout,
                           L0_attn_bout, L0_ffn_norm_w, L0_ffn_w_gv, L0_ffn_w_out)
    if _NC_CACHE[0] is None:
        _NC_CACHE[0] = build(debug=0)
    nc = _NC_CACHE[0]
    res = run_bass_kernel_spmd(nc, in_maps, core_ids=list(range(8)))
    return assemble(res.results)


# version 07
# speedup vs baseline: 1.1545x; 1.0750x over previous
import contextlib
import numpy as np
import concourse.bass as bass
import concourse.mybir as mybir

F32 = mybir.dt.float32
BF16 = mybir.dt.bfloat16
AF = mybir.ActivationFunctionType
ALU = mybir.AluOpType
AX = mybir.AxisListType

T, D = 4096, 2048
HID = 8192
RT = 12582912.0
SCALE = 1.0 / (128.0 ** 0.5)
NEG = -1.0e30
EPS = 1e-5
G4 = [[0, 1, 2, 3], [4, 5, 6, 7]]
G8 = [[0, 1, 2, 3, 4, 5, 6, 7]]
N_GV = float(16384 * 2048)
N_WO = float(2048 * 8192)

ENGS = ("sync", "scalar", "tensor", "vector", "gpsimd")


class Plan:

    def __init__(self, nc, sems, cnt):
        self.nc, self.sems, self.cnt = nc, sems, cnt
        self.ops = {e: [] for e in ENGS}

    def q(self, eng, fn):
        self.ops[eng].append(fn)

    def w(self, eng, sem, val=None):
        v = self.cnt[sem] if val is None else val
        assert v <= self.cnt[sem], f"forward wait {sem}:{v}>{self.cnt[sem]}"
        if v <= 0:
            return
        s = self.sems[sem]
        self.ops[eng].append(lambda e, s=s, v=v: e.wait_ge(s, v))

    def sw(self, eng, sem):
        self.w(eng, sem)

    def i(self, eng, fn, sem, n=1):
        self.cnt[sem] += n
        s = self.sems[sem]
        self.ops[eng].append(lambda e, fn=fn, s=s, n=n: fn(e).then_inc(s, n))
        return self.cnt[sem]

    def run(self, name):
        with self.nc.Block(name) as blk:
            for eng in ENGS:
                ops = self.ops[eng]
                if ops:
                    getattr(blk, eng)(lambda e, ops=ops: [f(e) for f in ops])


class C:
    pass


def build(debug=0, stop_after=None, no_tern=False, no_wout=False, tern_mode=4):
    nc = bass.Bass("TRN2", target_bir_lowering=False, dynamic_dma_scratch_size=16384,
                   detect_race_conditions=False)
    c = C()
    c.debug = debug

    c.d_xb = nc.dram_tensor("xb", [T, D], F32, kind="ExternalInput").ap()
    c.d_xown = nc.dram_tensor("xown", [1024, D], F32, kind="ExternalInput").ap()
    c.d_wqkv = nc.dram_tensor("wqkv", [D, 1536], F32, kind="ExternalInput").ap()
    c.d_bqkv = nc.dram_tensor("bqkv", [1536], F32, kind="ExternalInput").ap()
    c.d_woutc = nc.dram_tensor("woutc", [512, D], F32, kind="ExternalInput").ap()
    c.d_bout = nc.dram_tensor("boutb", [128, D], F32, kind="ExternalInput").ap()
    c.d_gamma = nc.dram_tensor("gammab", [128, D], F32, kind="ExternalInput").ap()
    c.d_wgv = nc.dram_tensor("wgv_sh", [256, 16384], F32, kind="ExternalInput").ap()
    c.d_wo = nc.dram_tensor("wo_sh", [1024, D], F32, kind="ExternalInput").ap()
    c.d_y = nc.dram_tensor("y", [1024, D], F32, kind="ExternalOutput").ap()

    c.d_ar_in = nc.dram_tensor("ar_in", [1, 2], F32, kind="Internal").ap()
    c.d_ar_out = nc.dram_tensor("ar_out", [1, 2], F32, kind="Internal").ap()
    c.d_tgv_in = nc.dram_tensor("tgv_in", [256, 16384], BF16, kind="Internal").ap()
    c.d_tgv_out = nc.dram_tensor("tgv_out", [D, 16384], BF16, kind="Internal", addr_space="Shared").ap()
    c.d_two_in = nc.dram_tensor("two_in", [1024, D], BF16, kind="Internal").ap()
    c.d_two_out = nc.dram_tensor("two_out", [HID, D], BF16, kind="Internal", addr_space="Shared").ap()
    c.d_rs_inA = nc.dram_tensor("rs_inA", [T // 2, D], BF16, kind="Internal").ap()
    c.d_rs_inB = nc.dram_tensor("rs_inB", [T // 2, D], BF16, kind="Internal").ap()
    c.d_rs_outA = nc.dram_tensor("rs_outA", [512, D], BF16, kind="Internal").ap()
    c.d_rs_outB = nc.dram_tensor("rs_outB", [512, D], BF16, kind="Internal").ap()
    c.d_h2 = nc.dram_tensor("h2d", [8, 128, HID], F32, kind="Internal").ap()
    c.d_q2 = nc.dram_tensor("q2d", [8, 128, HID], BF16, kind="Internal").ap()
    c.d_o2 = nc.dram_tensor("o2d", [8, 128, D], F32, kind="Internal").ap()

    c.no_tern = no_tern
    c.no_wout = no_wout
    c.tern_mode = tern_mode
    c.dbg = {}
    if debug:
        for nm, shp, dt in [("dbg_q", [128, 4 * T], BF16), ("dbg_k", [128, 4 * T], BF16),
                            ("dbg_v", [128, 4 * T], BF16), ("dbg_bias", [128, 128], F32),
                            ("dbg_sfac", [1, 4], F32), ("dbg_av", [128, 4 * 512], BF16),
                            ("dbg_d", [128, 8], F32), ("dbg_q1", [128, 16 * 1024], BF16),
                            ("dbg_r1", [128, 8 * D], F32),
                            ("dbg_pn0", [128, T], BF16), ("dbg_pn1", [128, T], BF16),
                            ("dbg_pn2", [128, T], BF16), ("dbg_pn3", [128, T], BF16),
                            ("dbg_praw", [128, T], F32), ("dbg_ell", [128, 8], F32),
                            ("dbg_pnh0", [4, 128, T], BF16), ("dbg_prh0", [128, T], F32),
                            ("dbg_s", [128, 512], F32), ("dbg_qs", [128, 128], BF16),
                            ("dbg_ks", [128, 512], BF16), ("dbg_rso", [1024, D], BF16),
                            ("dbg_o1", [128, D], F32), ("dbg_x2", [128, D], F32)]:
            c.dbg[nm] = nc.dram_tensor(nm, shp, dt, kind="ExternalOutput").ap()

    sems, cnt = {}, {}
    for s in ("ld", "wd", "gd", "pe", "dv", "ac", "cc"):
        sems[s] = nc.alloc_semaphore(s)
        cnt[s] = 0

    def sb(stk, name, shape, dt):
        return stk.enter_context(nc.sbuf_tensor(name, shape, dt))

    def ps(stk, name, shape, dt=F32):
        return stk.enter_context(nc.psum_tensor(name, shape, dt))

    with contextlib.ExitStack() as est:
        S1 = est.enter_context(contextlib.ExitStack())
        c.ident_b = sb(S1, "ident_b", [128, 128], BF16)
        c.ident_f = sb(S1, "ident_f", [128, 128], F32)
        c.tri_neg = sb(S1, "tri_neg", [128, 128], F32)
        c.ones_row = sb(S1, "ones_row", [1, 128], F32)
        c.ones_col_b = sb(S1, "ones_col_b", [128, 1], BF16)
        c.half_neg = sb(S1, "half_neg", [128, 1], F32)
        c.half_pos = sb(S1, "half_pos", [128, 1], F32)
        c.eps_col = sb(S1, "eps_col", [128, 1], F32)
        for nm in ("c_sx", "c_ss", "c_mu", "c_bias", "c_std", "c_rstd",
                   "c_t1", "c_t2", "c_ell", "c_rinv",
                   "invs_gv", "invs_wo", "sgvf_b", "swof_b",
                   "c_am", "c_amc", "c_rs1", "c_f2"):
            setattr(c, nm, sb(S1, nm, [128, 1], F32))
        c.ell_segs = [sb(S1, f"ell_segs{i}", [128, 8], F32) for i in range(2)]
        c.trow = sb(S1, "trow", [1, 128], F32)
        c.tsc = sb(S1, "tsc", [1, 4], F32)
        c.svals = sb(S1, "svals", [1, 2], F32)
        c.sfac = sb(S1, "sfac", [1, 4], F32)
        c.spsums = sb(S1, "spsums", [128, 96], F32)
        c.d_allc = sb(S1, "d_allc", [128, 8], F32)
        c.d2_allc = sb(S1, "d2_allc", [128, 8], F32)
        c.ln3mu = sb(S1, "ln3mu", [128, 8], F32)
        c.ln3rs = sb(S1, "ln3rs", [128, 8], F32)
        c.mks = sb(S1, "mks", [128, 4], F32)

        S2a = est.enter_context(contextlib.ExitStack())
        c.q_bf = sb(S2a, "q_bf", [128, 4 * T], BF16)
        c.k_bf = sb(S2a, "k_bf", [128, 4 * T], BF16)
        c.v_tm = sb(S2a, "v_tm", [128, 4 * T], BF16)
        c.bq_sb = sb(S2a, "bq_sb", [128, 12], F32)
        c.bias_tm = sb(S2a, "bias_tm", [128, 128], F32)

        S3 = est.enter_context(contextlib.ExitStack())
        c.x_in = [sb(S3, f"x_in{i}", [128, D], F32) for i in range(2)]
        c.x1g = sb(S3, "x1g", [128, 2 * D], BF16)
        c.x1_fm = sb(S3, "x1_fm", [128, 16 * 512], BF16)
        c.wq_f32 = sb(S3, "wq_f32", [128, 1536], F32)
        c.wqkv_bf = sb(S3, "wqkv_bf", [128, 16 * 1536], BF16)
        c.v_stage = sb(S3, "v_stage", [128, 512], BF16)
        c.sq_stage = sb(S3, "sq_stage", [128, 512], BF16)
        c.sp_buf = [sb(S3, f"sp_buf{i}", [128, 512], F32) for i in range(2)]
        c.normsb = sb(S3, "normsb", [128, 256], F32)

        PS0 = est.enter_context(contextlib.ExitStack())
        c.ps_qkv = [ps(PS0, f"ps_qkv{i}", [128, 512]) for i in range(2)]
        c.ps_t = [ps(PS0, f"ps_t{i}", [128, 512], BF16) for i in range(2)]
        c.ps_norm = ps(PS0, "ps_norm", [128, 256])
        c.ps_misc = ps(PS0, "ps_misc", [128, 128])

        p = Plan(nc, sems, cnt)
        emit_consts(p, c)
        emit_p0(p, c)
        p.run("p0")

        if stop_after == "p0":
            return nc
        S3.close()

        S2b = est.enter_context(contextlib.ExitStack())
        c.p_raw = [sb(S2b, f"p_raw{i}", [128, T], F32) for i in range(2)]
        c.woutc_bf = sb(S2b, "woutc_bf", [128, 4 * D], BF16)
        c.p_nrm = [sb(S2b, f"p_nrm{l}", [128, T], BF16) for l in range(4)]
        c.pt_sb = [sb(S2b, f"pt_sb{i}", [128, 1024], BF16) for i in range(2)]
        c.av_sb = [sb(S2b, f"av_sb{i}", [128, 4 * 512], BF16) for i in range(2)]
        c.wo_stage = [sb(S2b, f"wo_stage{i}", [128, D], BF16) for i in range(2)]
        c.tz_f32 = [sb(S2b, f"tz_f32{i}", [128, 512], F32) for i in range(2)]
        c.tz_s1 = sb(S2b, "tz_s1", [128, 512], BF16)
        c.tz_out = [sb(S2b, f"tz_out{i}", [128, 512], BF16) for i in range(2)]

        PS0.close()
        PS1 = est.enter_context(contextlib.ExitStack())
        c.ps_s = [ps(PS1, f"ps_s{i}", [128, 1024]) for i in range(2)]
        c.ps_pt = [ps(PS1, f"ps_pt{i}", [128, 1024], BF16) for i in range(2)]
        c.ps_av = [ps(PS1, f"ps_av{i}", [128, 512]) for i in range(2)]
        
        p = Plan(nc, sems, cnt)
        emit_p1(p, c)
        p.run("p1")

        if stop_after == "p1":
            return nc
        S2b.close()
        S2a.close()
        PS1.close()

        S4 = est.enter_context(contextlib.ExitStack())
        c.r1 = sb(S4, "r1", [128, 8 * D], F32)

        S4a1 = est.enter_context(contextlib.ExitStack())
        c.q1_fm = sb(S4a1, "q1_fm", [128, 16 * 1024], BF16)
        S4a = est.enter_context(contextlib.ExitStack())
        c.bout_b = sb(S4a, "bout_b", [128, D], F32)
        c.gamma_b = sb(S4a, "gamma_b", [128, D], F32)
        c.ln_t1 = sb(S4a, "ln_t1", [128, D], F32)
        c.ln_t2 = sb(S4a, "ln_t2", [128, D], F32)
        c.rsx = [sb(S4a, f"rsx{i}", [128, D], BF16) for i in range(2)]
        c.xo_f32 = [sb(S4a, f"xo_f32{i}", [128, D], F32) for i in range(2)]
        c.q1_bf = sb(S4a, "q1_bf", [128, D], BF16)

        PS2 = est.enter_context(contextlib.ExitStack())
        c.ps_a = [ps(PS2, f"ps_a{i}", [128, 512]) for i in range(2)]
        c.ps_t2 = [ps(PS2, f"ps_t2{i}", [128, 512], BF16) for i in range(2)]

        p = Plan(nc, sems, cnt)
        emit_p2a(p, c)

        S4a.close()
        S4b = est.enter_context(contextlib.ExitStack())
        c.pwh = [sb(S4b, f"pwh{i}", [128, 16 * 1024], BF16) for i in range(3)]
        c.silu_sb = sb(S4b, "silu_sb", [128, 512], F32)
        c.h2st = [sb(S4b, f"h2st{i}", [128, 512], F32) for i in range(2)]

        emit_p2b(p, c)
        p.run("p2ab")
        if stop_after == "p2b":
            return nc
        S4b.close()
        S4a1.close()

        S4c = est.enter_context(contextlib.ExitStack())
        c.h2l = [sb(S4c, f"h2l{i}", [128, 4096], F32) for i in range(2)]
        c.q2_bf = sb(S4c, "q2_bf", [128, HID], BF16)
        c.q2fm_st = sb(S4c, "q2fm_st", [128, HID], BF16)

        p = Plan(nc, sems, cnt)
        emit_p2c(p, c)
        p.run("p2c")
        S4c.close()

        S4d1 = est.enter_context(contextlib.ExitStack())
        c.wDh = [sb(S4d1, f"wDh{i}", [128, 32 * 512], BF16) for i in range(3)]
        c.q2sb = [sb(S4d1, f"q2sb{i}", [128, HID], BF16) for i in range(2)]
        c.o2st = [sb(S4d1, f"o2st{i}", [128, 512], F32) for i in range(2)]

        p = Plan(nc, sems, cnt)
        emit_p2d(p, c)
        p.run("p2d")
        S4d1.close()

        S4d2 = est.enter_context(contextlib.ExitStack())
        c.o2sb = [sb(S4d2, f"o2sb{i}", [128, D], F32) for i in range(2)]
        c.y_st = [sb(S4d2, f"y_st{i}", [128, D], F32) for i in range(2)]
        c.tl_t1 = sb(S4d2, "tl_t1", [128, D], F32)
        c.tl_t2 = sb(S4d2, "tl_t2", [128, D], F32)

        p = Plan(nc, sems, cnt)
        emit_tail(p, c)
        p.run("tail")
        S4d2.close()
        PS2.close()

    return nc


def emit_consts(p, c):
    gsem = "cc"
    p.i("gpsimd", lambda e: e.memset(c.ident_b[:], 0.0), gsem, 1)
    p.w("gpsimd", gsem)
    p.i("gpsimd", lambda e: e.affine_select(
        out=c.ident_b[:], in_=c.ident_b[:], compare_op=ALU.not_equal,
        fill=1.0, base=0, pattern=[[-1, 128]], channel_multiplier=1), gsem, 1)
    p.i("gpsimd", lambda e: e.memset(c.ident_f[:], 0.0), gsem, 1)
    p.w("gpsimd", gsem)
    p.i("gpsimd", lambda e: e.affine_select(
        out=c.ident_f[:], in_=c.ident_f[:], compare_op=ALU.not_equal,
        fill=1.0, base=0, pattern=[[-1, 128]], channel_multiplier=1), gsem, 1)
    p.i("gpsimd", lambda e: e.memset(c.tri_neg[:], 0.0), gsem, 1)
    p.w("gpsimd", gsem)
    p.i("gpsimd", lambda e: e.affine_select(
        out=c.tri_neg[:], in_=c.tri_neg[:], compare_op=ALU.is_ge,
        fill=NEG, base=0, pattern=[[-1, 128]], channel_multiplier=1), gsem, 1)
    p.i("gpsimd", lambda e: e.memset(c.ones_row[:], 1.0), gsem, 1)
    p.i("gpsimd", lambda e: e.memset(c.half_neg[:], -0.5), gsem, 1)
    p.i("gpsimd", lambda e: e.memset(c.half_pos[:], 0.5), gsem, 1)
    p.i("gpsimd", lambda e: e.memset(c.eps_col[:], EPS), gsem, 1)
    p.i("gpsimd", lambda e: e.memset(c.ones_col_b[:], 1.0), gsem, 1)
    c.CC_CONSTS = p.cnt["cc"]


def _ln_stats(p, c, src_ap, sq_out_ap, inv_n, eps=EPS):
    p.i("vector", lambda e: e.reduce_sum(out=c.c_sx[:], in_=src_ap, axis=AX.X), "dv")
    sq_ac = p.i("scalar", lambda e: e.activation(out=sq_out_ap, in_=src_ap, func=AF.Square,
                                                 accum_out=c.c_ss[:]), "ac")
    p.sw("vector", "dv")
    p.i("vector", lambda e: e.tensor_scalar(out=c.c_mu[:], in0=c.c_sx[:],
                                            scalar1=-inv_n, scalar2=None,
                                            op0=ALU.mult), "dv")
    p.sw("vector", "dv")
    p.i("vector", lambda e: e.tensor_tensor(out=c.c_t1[:], in0=c.c_mu[:],
                                            in1=c.c_mu[:], op=ALU.mult), "dv")
    p.w("vector", "ac", getattr(c, "_sqrt_ac", 0))
    p.sw("vector", "dv")
    dv_b = p.i("vector", lambda e: e.tensor_scalar(out=c.c_bias[:], in0=c.c_t1[:],
                                                   scalar1=-1.0, scalar2=eps,
                                                   op0=ALU.mult, op1=ALU.add), "dv")
    p.w("scalar", "dv", dv_b)
    p.w("scalar", "dv", getattr(c, "_recip_dv", 0))
    p.w("scalar", "ac", sq_ac)
    ac_s = p.i("scalar", lambda e: e.activation(out=c.c_std[:], in_=c.c_ss[:],
                                                func=AF.Sqrt, bias=c.c_bias[:],
                                                scale=inv_n), "ac")
    c._sqrt_ac = ac_s
    p.w("vector", "ac", ac_s)
    r = p.i("vector", lambda e: e.reciprocal(out=c.c_rstd[:], in_=c.c_std[:]), "dv")
    c._recip_dv = r
    p.sw("vector", "dv")
    return r


def emit_p0(p, c):
    cnt = p.cnt

    for fc in range(12):
        p.i("sync", lambda e, fc=fc: e.dma_start(
            out=c.bq_sb[:, fc:fc + 1],
            in_=c.d_bqkv[fc * 128:(fc + 1) * 128]), "ld", 16)
    LD_BQ = p.cnt["ld"]
    p.w("scalar", "ld", LD_BQ)

    wq_rel = 0
    for dc in range(16):
        p.w("sync", "dv", wq_rel)
        p.i("sync", lambda e, dc=dc: e.dma_start(
            out=c.wq_f32[:, 0:1536],
            in_=c.d_wqkv[dc * 128:(dc + 1) * 128, :]), "ld", 16)
        p.w("vector", "ld")
        wq_rel = p.i("vector", lambda e, dc=dc: e.tensor_copy(
            out=c.wqkv_bf[:, dc * 1536:(dc + 1) * 1536],
            in_=c.wq_f32[:, 0:1536]), "dv")
    p.w("tensor", "cc", c.CC_CONSTS)

    x_rel = [(0, 0), (0, 0)]
    x1g_rel = 0
    x1fm_rel = 0
    pst_rel = [0, 0]
    psqkv_rel = [0, 0]
    sq_rel = 0
    vst_rel = 0
    spi = 0
    sp_rel = [0, 0]

    for sc in range(8):
        evac_done = 0
        for half in range(2):
            for tcl2 in range(2):
                tcl = half * 2 + tcl2
                tc = sc * 4 + tcl
                b = tc % 2
                p.w("sync", "dv", x_rel[b][0])
                p.w("sync", "ac", x_rel[b][1])
                ldx = p.i("sync", lambda e, tc=tc, b=b: e.dma_start(
                    out=c.x_in[b][:], in_=c.d_xb[tc * 128:(tc + 1) * 128, :]), "ld", 16)
                p.w("vector", "ld", ldx)
                p.w("scalar", "ld", ldx)
                p.w("scalar", "pe", x1g_rel)
                _ln_stats(p, c, c.x_in[b][:], c.x1g[:, tcl2 * D:(tcl2 + 1) * D], 1.0 / D)
                ac_used = cnt["ac"]
                p.w("vector", "pe", x1g_rel)
                dv_ap = p.i("vector", lambda e, b=b, tcl2=tcl2: e.tensor_scalar(
                    out=c.x1g[:, tcl2 * D:(tcl2 + 1) * D], in0=c.x_in[b][:],
                    scalar1=c.c_mu[:], scalar2=c.c_rstd[:],
                    op0=ALU.add, op1=ALU.mult), "dv")
                x_rel[b] = (dv_ap, ac_used)
            dv_half = cnt["dv"]

            p.w("tensor", "dv", dv_half)
            if half == 0:
                p.w("tensor", "pe", x1fm_rel)
            for dc in range(16):
                bb = dc % 2
                p.w("tensor", "dv", pst_rel[bb])
                for tcl2 in range(2):
                    fn = lambda e, dc=dc, tcl2=tcl2, bb=bb: e.transpose(
                        c.ps_t[bb][:, tcl2 * 128:(tcl2 + 1) * 128],
                        c.x1g[:, tcl2 * D + dc * 128: tcl2 * D + (dc + 1) * 128],
                        c.ident_b[:])
                    if tcl2 == 0:
                        p.q("tensor", fn)
                    else:
                        pe_t = p.i("tensor", fn, "pe")
                p.w("vector", "pe", pe_t)
                pst_rel[bb] = p.i("vector", lambda e, dc=dc, half=half, bb=bb: e.tensor_copy(
                    out=c.x1_fm[:, dc * 512 + half * 256: dc * 512 + half * 256 + 256],
                    in_=c.ps_t[bb][:, 0:256]), "dv")
            x1g_rel = cnt["pe"]
        evac_done = cnt["dv"]

        for k in range(12):
            i = spi
            spi += 1
            b = i % 2
            if i < 64:
                pc, ct = i // 32, i % 32
                src = c.d_wgv[pc * 128:(pc + 1) * 128, ct * 512:(ct + 1) * 512]
            else:
                jj = i - 64
                pc, ct = jj // 4, jj % 4
                src = c.d_wo[pc * 128:(pc + 1) * 128, ct * 512:(ct + 1) * 512]
            p.w("scalar", "dv", sp_rel[b])
            p.i("scalar", lambda e, src=src, b=b: e.dma_start(
                out=c.sp_buf[b][:], in_=src), "wd", 16)
            p.w("vector", "wd")
            sp_rel[b] = p.i("vector", lambda e, i=i, b=b: e.reduce_sum(
                out=c.spsums[:, i:i + 1], in_=c.sp_buf[b][:], axis=AX.X,
                apply_absolute_value=True), "dv")

        p.w("tensor", "dv", evac_done)
        for fc in range(12):
            bb = fc % 2
            p.w("tensor", "ac", psqkv_rel[bb])
            for dc in range(16):
                fn = lambda e, fc=fc, dc=dc, bb=bb: e.matmul(
                    c.ps_qkv[bb][:],
                    c.wqkv_bf[:, dc * 1536 + fc * 128: dc * 1536 + (fc + 1) * 128],
                    c.x1_fm[:, dc * 512:(dc + 1) * 512],
                    start=(dc == 0), stop=(dc == 15))
                if dc < 15:
                    p.q("tensor", fn)
                else:
                    pe_mm = p.i("tensor", fn, "pe")
            h, role = fc % 4, fc // 4
            if role == 0:
                slot = c.q_bf[:, h * T + sc * 512: h * T + sc * 512 + 512]
            elif role == 1:
                slot = c.k_bf[:, h * T + sc * 512: h * T + sc * 512 + 512]
            else:
                slot = c.v_stage[:]
                p.w("scalar", "pe", vst_rel)
            p.w("scalar", "pe", pe_mm)
            ac_ev = p.i("scalar", lambda e, slot=slot, fc=fc, bb=bb: e.activation(
                out=slot, in_=c.ps_qkv[bb][:], func=AF.Identity,
                bias=c.bq_sb[:, fc:fc + 1]), "ac")
            if role < 2:
                p.w("scalar", "pe", sq_rel)
                ac_ev = p.i("scalar", lambda e, fc=fc, bb=bb: e.activation(
                    out=c.sq_stage[:], in_=c.ps_qkv[bb][:], func=AF.Square,
                    bias=c.bq_sb[:, fc:fc + 1]), "ac")
                p.w("tensor", "ac", ac_ev)
                for kk in range(4):
                    col = (role * 4 + h) * 32 + sc * 4 + kk
                    fn = lambda e, kk=kk, col=col: e.matmul(
                        c.ps_norm[:, col:col + 1],
                        c.sq_stage[:, kk * 128:(kk + 1) * 128],
                        c.ones_col_b[:], start=True, stop=True)
                    if kk < 3:
                        p.q("tensor", fn)
                    else:
                        sq_rel = p.i("tensor", fn, "pe")
            else:
                p.w("tensor", "ac", ac_ev)
                tb = fc % 2
                p.w("tensor", "dv", pst_rel[tb])
                for kk in range(4):
                    fn = lambda e, kk=kk, tb=tb: e.transpose(
                        c.ps_t[tb][:, kk * 128:(kk + 1) * 128],
                        c.v_stage[:, kk * 128:(kk + 1) * 128], c.ident_b[:])
                    if kk < 3:
                        p.q("tensor", fn)
                    else:
                        pe_vt = p.i("tensor", fn, "pe")
                vst_rel = pe_vt
                p.w("vector", "pe", pe_vt)
                pst_rel[tb] = p.i("vector", lambda e, h=h, sc=sc, tb=tb: e.tensor_copy(
                    out=c.v_tm[:, (h * 32 + sc * 4) * 128:(h * 32 + sc * 4 + 4) * 128],
                    in_=c.ps_t[tb][:]), "dv")
            psqkv_rel[bb] = cnt["ac"]
        x1fm_rel = cnt["pe"]

    p.w("scalar", "pe")
    ac_n = p.i("scalar", lambda e: e.activation(
        out=c.normsb[:], in_=c.ps_norm[:], func=AF.Sqrt), "ac")
    dv_mk_prev = 0
    for h in range(4):
        p.w("vector", "ac", ac_n)
        dv_m = p.i("vector", lambda e, h=h: e.reduce_max(
            out=c.c_t1[:], in_=c.normsb[:, (4 + h) * 32:(5 + h) * 32], axis=AX.X), "dv")
        p.w("tensor", "dv", dv_m)
        p.w("tensor", "dv", dv_mk_prev)
        pe_tr = p.i("tensor", lambda e: e.transpose(
            c.ps_misc[0:1, 0:128], c.c_t1[:], c.ident_f[:]), "pe")
        p.w("vector", "pe", pe_tr)
        p.i("vector", lambda e: e.tensor_copy(out=c.trow[:],
                                              in_=c.ps_misc[0:1, 0:128]), "dv")
        p.sw("vector", "dv")
        p.i("vector", lambda e: e.reduce_max(out=c.tsc[0:1, 0:1], in_=c.trow[:],
                                             axis=AX.X), "dv")
        p.sw("vector", "dv")
        dv_r = p.i("vector", lambda e: e.tensor_scalar(
            out=c.tsc[0:1, 1:2], in0=c.tsc[0:1, 0:1], scalar1=-SCALE, scalar2=None,
            op0=ALU.mult), "dv")
        p.w("tensor", "dv", dv_r)
        pe_b = p.i("tensor", lambda e: e.matmul(
            c.ps_misc[:, 0:1], c.ones_row[:], c.tsc[0:1, 1:2],
            start=True, stop=True), "pe")
        p.w("vector", "pe", pe_b)
        dv_mk_prev = p.i("vector", lambda e, h=h: e.tensor_copy(
            out=c.mks[:, h:h + 1], in_=c.ps_misc[:, 0:1]), "dv")
        p.sw("vector", "dv")
        p.i("vector", lambda e, h=h: e.tensor_scalar(
            out=c.bias_tm[:, h * 32:(h + 1) * 32],
            in0=c.normsb[:, h * 32:(h + 1) * 32],
            scalar1=c.mks[:, h:h + 1], scalar2=None, op0=ALU.mult), "dv")

    p.i("vector", lambda e: e.reduce_sum(out=c.c_t1[:], in_=c.spsums[:, 0:64],
                                         axis=AX.X), "dv")
    dv_s0 = p.i("vector", lambda e: e.reduce_sum(out=c.c_t2[:], in_=c.spsums[:, 64:96],
                                                 axis=AX.X), "dv")
    p.w("tensor", "dv", dv_s0)
    p.w("tensor", "dv", dv_mk_prev)
    pe_t1 = p.i("tensor", lambda e: e.transpose(
        c.ps_misc[0:1, 0:128], c.c_t1[:], c.ident_f[:]), "pe")
    p.w("vector", "pe", pe_t1)
    p.i("vector", lambda e: e.tensor_copy(out=c.trow[:], in_=c.ps_misc[0:1, 0:128]), "dv")
    p.sw("vector", "dv")
    dv_s1 = p.i("vector", lambda e: e.reduce_sum(
        out=c.svals[0:1, 0:1], in_=c.trow[:], axis=AX.X), "dv")
    p.w("tensor", "dv", dv_s1)
    pe_t2 = p.i("tensor", lambda e: e.transpose(
        c.ps_misc[0:1, 0:128], c.c_t2[:], c.ident_f[:]), "pe")
    p.w("vector", "pe", pe_t2)
    p.i("vector", lambda e: e.tensor_copy(out=c.trow[:], in_=c.ps_misc[0:1, 0:128]), "dv")
    p.sw("vector", "dv")
    dv_s2 = p.i("vector", lambda e: e.reduce_sum(
        out=c.svals[0:1, 1:2], in_=c.trow[:], axis=AX.X), "dv")
    p.w("gpsimd", "dv", dv_s2)
    p.i("gpsimd", lambda e: e.dma_start(out=c.d_ar_in, in_=c.svals[:]), "gd", 16)
    p.w("gpsimd", "gd")
    p.i("gpsimd", lambda e: e.collective_compute(
        "AllReduce", ALU.add, replica_groups=G8,
        ins=[c.d_ar_in], outs=[c.d_ar_out]), "cc", 1)
    p.w("gpsimd", "cc")
    p.i("gpsimd", lambda e: e.dma_start(out=c.svals[:], in_=c.d_ar_out), "gd", 16)
    c.GD_AR = cnt["gd"]
    p.w("vector", "gd", c.GD_AR)
    p.w("vector", "dv")
    p.i("vector", lambda e: e.reciprocal(out=c.tsc[0:1, 0:2], in_=c.svals[0:1, 0:2]), "dv")
    p.sw("vector", "dv")
    p.i("vector", lambda e: e.tensor_scalar(out=c.sfac[0:1, 0:1], in0=c.tsc[0:1, 0:1],
                                            scalar1=N_GV, scalar2=None, op0=ALU.mult), "dv")
    p.i("vector", lambda e: e.tensor_scalar(out=c.sfac[0:1, 1:2], in0=c.tsc[0:1, 1:2],
                                            scalar1=N_WO, scalar2=None, op0=ALU.mult), "dv")
    p.i("vector", lambda e: e.tensor_scalar(out=c.sfac[0:1, 2:3], in0=c.svals[0:1, 0:1],
                                            scalar1=1.0 / (254.0 * N_GV), scalar2=None,
                                            op0=ALU.mult), "dv")
    dv_f = p.i("vector", lambda e: e.tensor_scalar(
        out=c.sfac[0:1, 3:4], in0=c.svals[0:1, 1:2],
        scalar1=1.0 / (254.0 * N_WO), scalar2=None, op0=ALU.mult), "dv")
    p.w("tensor", "dv", dv_f)
    p.w("tensor", "dv", dv_mk_prev)
    for i, dst in enumerate((c.invs_gv, c.invs_wo, c.sgvf_b, c.swof_b)):
        pe_bb = p.i("tensor", lambda e, i=i: e.matmul(
            c.ps_misc[:, i:i + 1], c.ones_row[:], c.sfac[0:1, i:i + 1],
            start=True, stop=True), "pe")
        p.w("vector", "pe", pe_bb)
        p.i("vector", lambda e, dst=dst, i=i: e.tensor_copy(
            out=dst[:], in_=c.ps_misc[:, i:i + 1]), "dv")
    c.DV_FAC = cnt["dv"]

    if c.debug:
        dv_all = cnt["dv"]
        p.w("gpsimd", "dv", dv_all)
        p.w("gpsimd", "ac")
        for nm, src in [("dbg_q", c.q_bf), ("dbg_k", c.k_bf), ("dbg_v", c.v_tm)]:
            p.i("gpsimd", lambda e, nm=nm, src=src: e.dma_start(
                out=c.dbg[nm], in_=src[:]), "gd", 16)
        p.i("gpsimd", lambda e: e.dma_start(out=c.dbg["dbg_bias"][0:128, 0:128],
                                            in_=c.bias_tm[:]), "gd", 16)
        p.i("gpsimd", lambda e: e.dma_start(out=c.dbg["dbg_sfac"], in_=c.sfac[:]), "gd", 16)


def emit_p1(p, c):
    cnt = p.cnt
    wc_rel = 0
    for hc in range(4):
        p.w("sync", "dv", wc_rel)
        p.i("sync", lambda e, hc=hc: e.dma_start(
            out=c.p_raw[0][:, 0:2048],
            in_=c.d_woutc[hc * 128:(hc + 1) * 128, :]), "ld", 16)
        p.w("vector", "ld")
        wc_rel = p.i("vector", lambda e, hc=hc: e.tensor_copy(
            out=c.woutc_bf[:, hc * D:(hc + 1) * D], in_=c.p_raw[0][:, 0:2048]), "dv")
    z = []
    for buf in c.p_raw + c.p_nrm:
        z.append(p.i("vector", lambda e, buf=buf: e.memset(buf[:], 0.0), "dv"))
    dv_zero = cnt["dv"]

    tz_i = [0]
    tzf_rel = [(0, 0)] * 2
    tzo_rel = [0, 0]
    tzs1_rel = [0]
    ag_issued = [False]

    def tern_tiles(n):
        if c.no_tern:
            c.CC_AG = p.cnt["cc"]
            return
        for _ in range(n):
            i = tz_i[0]
            if i >= 96:
                if not ag_issued[0]:
                    ag_issued[0] = True
                    p.w("gpsimd", "ld")
                    p.w("gpsimd", "gd")
                    p.i("gpsimd", lambda e: e.collective_compute(
                        "AllGather", ALU.bypass, replica_groups=G8,
                        ins=[c.d_tgv_in], outs=[c.d_tgv_out]), "cc", 1)
                    p.i("gpsimd", lambda e: e.collective_compute(
                        "AllGather", ALU.bypass, replica_groups=G8,
                        ins=[c.d_two_in], outs=[c.d_two_out]), "cc", 1)
                    c.CC_AG = cnt["cc"]
                return
            tz_i[0] += 1
            b = i % 2
            bo = i % 2
            if i < 64:
                pc, ct = i // 32, i % 32
                src = c.d_wgv[pc * 128:(pc + 1) * 128, ct * 512:(ct + 1) * 512]
                dst = c.d_tgv_in[pc * 128:(pc + 1) * 128, ct * 512:(ct + 1) * 512]
                inv = c.invs_gv
            else:
                jj = i - 64
                pc, ct = jj // 4, jj % 4
                src = c.d_wo[pc * 128:(pc + 1) * 128, ct * 512:(ct + 1) * 512]
                dst = c.d_two_in[pc * 128:(pc + 1) * 128, ct * 512:(ct + 1) * 512]
                inv = c.invs_wo
            p.w("scalar", "ac", tzf_rel[b][1])
            ldw = p.i("scalar", lambda e, src=src, b=b: e.dma_start(
                out=c.tz_f32[b][:], in_=src), "wd", 16)
            p.w("scalar", "wd", ldw)
            if i == 0:
                p.w("scalar", "dv", c.DV_FAC)
            if c.tern_mode == 5:
                p.i("scalar", lambda e, b=b: e.activation(
                    out=c.tz_s1[:], in_=c.tz_f32[b][:], func=AF.Sign), "ac")
                p.w("scalar", "ld", tzo_rel[bo])
                ac_s2 = p.i("scalar", lambda e, b=b, bo=bo: e.activation(
                    out=c.tz_out[bo][:], in_=c.tz_f32[b][:], func=AF.Sign), "ac")
            elif c.tern_mode == 6:
                p.i("scalar", lambda e, b=b, inv=inv: e.activation(
                    out=c.tz_s1[:], in_=c.tz_f32[b][:], func=AF.Sign,
                    scale=inv[:]), "ac")
                p.w("scalar", "ld", tzo_rel[bo])
                ac_s2 = p.i("scalar", lambda e, b=b, bo=bo, inv=inv: e.activation(
                    out=c.tz_out[bo][:], in_=c.tz_f32[b][:], func=AF.Sign,
                    scale=inv[:]), "ac")
            elif c.tern_mode == 7:
                p.i("scalar", lambda e, b=b: e.activation(
                    out=c.tz_s1[:], in_=c.tz_f32[b][:], func=AF.Identity,
                    bias=c.half_neg[:]), "ac")
                p.w("scalar", "ld", tzo_rel[bo])
                ac_s2 = p.i("scalar", lambda e, b=b, bo=bo: e.activation(
                    out=c.tz_out[bo][:], in_=c.tz_f32[b][:], func=AF.Identity,
                    bias=c.half_pos[:]), "ac")
            elif c.tern_mode >= 2:
                p.w("scalar", "dv", tzs1_rel[0])
                p.i("scalar", lambda e, b=b, inv=inv: e.activation(
                    out=c.tz_s1[:], in_=c.tz_f32[b][:], func=AF.Sign,
                    scale=inv[:], bias=c.half_neg[:]), "ac")
                p.w("scalar", "ld", tzo_rel[bo])
                ac_s2 = p.i("scalar", lambda e, b=b, bo=bo, inv=inv: e.activation(
                    out=c.tz_out[bo][:], in_=c.tz_f32[b][:], func=AF.Sign,
                    scale=inv[:], bias=c.half_pos[:]), "ac")
            else:
                p.w("scalar", "ld", tzo_rel[bo])
                ac_s2 = p.i("scalar", lambda e, b=b, bo=bo: e.activation(
                    out=c.tz_out[bo][:], in_=c.tz_f32[b][:], func=AF.Identity), "ac")
            tzf_rel[b] = (0, ac_s2)
            p.w("vector", "ac", ac_s2)
            if c.tern_mode >= 3:
                dv_t = p.i("vector", lambda e, bo=bo, b=b: e.tensor_tensor(
                    out=c.tz_out[bo][:], in0=c.tz_s1[:], in1=c.tz_out[bo][:],
                    op=ALU.add), "dv")
                tzs1_rel[0] = dv_t
            else:
                dv_t = p.i("vector", lambda e, b=b, bo=bo: e.tensor_scalar(
                    out=c.tz_out[bo][:], in0=c.tz_out[bo][:], scalar1=1.0,
                    scalar2=None, op0=ALU.mult), "dv")
            p.w("sync", "dv", dv_t)
            tzo_rel[bo] = p.i("sync", lambda e, bo=bo, dst=dst: e.dma_start(
                out=dst, in_=c.tz_out[bo][:]), "ld", 16)

    praw_rel = [0, 0]
    pnrm_rel = [0, 0, 0, 0]
    pss_rel = [0, 0]
    pspt_rel = [0, 0]
    ptsb_rel = [0, 0]
    psav_rel = [0, 0]
    avsb_rel = [0, 0]
    psw_rel = [0, 0]
    wost_rel = [0, 0]

    si = [0]
    ti = [0]
    p.w("tensor", "dv", dv_zero)

    qi = [0]
    pending = [None]

    def emit_scores(j, h, ls):
        qc_info = []
        for l in ls:
            qc = 4 * j + l
            par = qi[0] % 2
            qi[0] += 1
            ngroups = (j + 2) // 2
            for g in range(ngroups):
                s0 = 2 * g
                widths = []
                for seg in (s0, s0 + 1):
                    if seg <= j:
                        widths.append(512 if seg < j else 128 * (l + 1))
                gw = sum(widths)
                b = si[0] % 2
                si[0] += 1
                p.w("tensor", "ac", pss_rel[b])
                off = 0
                for k2, wdt in enumerate(widths):
                    seg = s0 + k2
                    fn = (lambda e, h=h, qc=qc, seg=seg, wdt=wdt, off=off, b=b:
                          e.matmul(
                              c.ps_s[b][:, off:off + wdt],
                              c.q_bf[:, h * T + qc * 128:h * T + (qc + 1) * 128],
                              c.k_bf[:, h * T + seg * 512: h * T + seg * 512 + wdt],
                              start=True, stop=True, skip_group_check=True))
                    if k2 == len(widths) - 1:
                        pe_s = p.i("tensor", fn, "pe")
                    else:
                        p.q("tensor", fn)
                    off += wdt
                if s0 <= j <= s0 + 1:
                    doff = (j - s0) * 512 + l * 128
                    p.w("vector", "pe", pe_s)
                    dv_msk = p.i("vector", lambda e, b=b, doff=doff: e.tensor_tensor(
                        out=c.ps_s[b][:, doff:doff + 128],
                        in0=c.ps_s[b][:, doff:doff + 128],
                        in1=c.tri_neg[:], op=ALU.add), "dv")
                    p.w("scalar", "dv", dv_msk)
                p.w("scalar", "pe", pe_s)
                if g == 0:
                    p.w("scalar", "dv", praw_rel[par])
                pss_rel[b] = p.i("scalar", lambda e, b=b, s0=s0, gw=gw, h=h, qc=qc,
                                 g=g, par=par: e.activation(
                    out=c.p_raw[par][:, s0 * 512: s0 * 512 + gw],
                    in_=c.ps_s[b][:, 0:gw], func=AF.Exp,
                    bias=c.bias_tm[:, h * 32 + qc: h * 32 + qc + 1],
                    scale=SCALE, accum_out=c.ell_segs[par][:, g:g + 1]), "ac")
            qc_info.append((l, par, ngroups, cnt["ac"]))
        return qc_info

    def emit_normalize(j, qc_info):
        for (l, par, ngroups, ac_exp) in qc_info:
            p.w("vector", "ac", ac_exp)
            p.i("vector", lambda e, ngroups=ngroups, par=par: e.reduce_sum(
                out=c.c_ell[:], in_=c.ell_segs[par][:, 0:ngroups], axis=AX.X), "dv")
            p.w("vector", "dv")
            p.i("vector", lambda e: e.reciprocal(out=c.c_rinv[:], in_=c.c_ell[:]), "dv")
            p.sw("vector", "dv")
            ext = 512 * j + 128 * (l + 1)
            p.w("vector", "pe", pnrm_rel[l])
            praw_rel[par] = p.i("vector", lambda e, l=l, ext=ext, par=par: e.tensor_scalar(
                out=c.p_nrm[l][:, 0:ext], in0=c.p_raw[par][:, 0:ext],
                scalar1=c.c_rinv[:], scalar2=None, op0=ALU.mult), "dv")

    def emit_tav(j, h, dv_nrm):
        av_i = (j * 4 + h) % 2
        p.w("tensor", "dv", dv_nrm)
        p.w("tensor", "ac", psav_rel[av_i])
        p.w("tensor", "dv", psw_rel[av_i])
        first = True
        nkc = 4 * (j + 1)
        for kg in range(nkc // 2):
            tb = ti[0] % 2
            ti[0] += 1
            p.w("tensor", "dv", pspt_rel[tb])
            for kk in range(2):
                kci = kg * 2 + kk
                seg, kc = kci // 4, kci % 4
                for l in range(4):
                    fn = lambda e, seg=seg, kc=kc, l=l, kk=kk, tb=tb: e.transpose(
                        c.ps_pt[tb][:, kk * 512 + l * 128: kk * 512 + (l + 1) * 128],
                        c.p_nrm[l][:, seg * 512 + kc * 128: seg * 512 + (kc + 1) * 128],
                        c.ident_b[:])
                    if kk == 1 and l == 3:
                        pe_pt = p.i("tensor", fn, "pe")
                    else:
                        p.q("tensor", fn)
            p.w("vector", "pe", pe_pt)
            p.w("vector", "pe", ptsb_rel[tb])
            pspt_rel[tb] = p.i("vector", lambda e, tb=tb: e.tensor_copy(
                out=c.pt_sb[tb][:], in_=c.ps_pt[tb][:]), "dv")
            p.w("tensor", "dv", pspt_rel[tb])
            for kk in range(2):
                kci = kg * 2 + kk
                last = (kci == nkc - 1)
                fn = lambda e, h=h, kci=kci, kk=kk, tb=tb, first=first, last=last, av_i=av_i: e.matmul(
                    c.ps_av[av_i][:],
                    c.v_tm[:, (h * 32 + kci) * 128:(h * 32 + kci + 1) * 128],
                    c.pt_sb[tb][:, kk * 512:(kk + 1) * 512],
                    start=first, stop=last, skip_group_check=True)
                ptsb_rel[tb] = p.i("tensor", fn, "pe")
                first = False
        pnrm_rel[0] = pnrm_rel[1] = pnrm_rel[2] = pnrm_rel[3] = cnt["pe"]
        p.w("scalar", "pe")
        if h == 0:
            p.w("scalar", "pe", avsb_rel[j % 2])
        psav_rel[av_i] = p.i("scalar", lambda e, h=h, av_i=av_i, j=j: e.activation(
            out=c.av_sb[j % 2][:, h * 512:(h + 1) * 512],
            in_=c.ps_av[av_i][:], func=AF.Copy), "ac")

    def emit_wout(j):
        ac_av = cnt["ac"]
        for tl in range(4):
            wb = tl % 2
            p.w("tensor", "ac", ac_av)
            for fc in range(4):
                p.w("tensor", "dv", psw_rel[wb])
                p.w("tensor", "ac", psav_rel[wb])
                for hc in range(4):
                    fn = lambda e, hc=hc, fc=fc, tl=tl, wb=wb, j=j: e.matmul(
                        c.ps_av[wb][:],
                        c.av_sb[j % 2][:, hc * 512 + tl * 128: hc * 512 + (tl + 1) * 128],
                        c.woutc_bf[:, hc * D + fc * 512: hc * D + fc * 512 + 512],
                        start=(hc == 0), stop=(hc == 3), skip_group_check=True)
                    if hc < 3:
                        p.q("tensor", fn)
                    else:
                        pe_w = p.i("tensor", fn, "pe")
                p.w("vector", "pe", pe_w)
                p.w("vector", "gd", wost_rel[wb])
                psw_rel[wb] = p.i("vector", lambda e, fc=fc, wb=wb: e.tensor_copy(
                    out=c.wo_stage[wb][:, fc * 512:(fc + 1) * 512],
                    in_=c.ps_av[wb][:]), "dv")
            p.w("gpsimd", "dv", psw_rel[wb])
            dst = (c.d_rs_inA if j % 2 == 0 else c.d_rs_inB)
            row = (j // 2) * 512 + tl * 128
            wost_rel[wb] = p.i("gpsimd", lambda e, row=row, wb=wb, dst=dst: e.dma_start(
                out=dst[row:row + 128, :], in_=c.wo_stage[wb][:]), "gd", 16)
        avsb_rel[j % 2] = cnt["pe"]
        if j == 6:
            p.w("gpsimd", "gd")
            p.i("gpsimd", lambda e: e.collective_compute(
                "ReduceScatter", ALU.add, replica_groups=G4,
                ins=[c.d_rs_inA], outs=[c.d_rs_outA]), "cc", 1)
            c.CC_RSA = cnt["cc"]

    units = [(j, h) for j in range(8) for h in range(4)]
    for u, (j, h) in enumerate(units):
        qiA = emit_scores(j, h, (0, 1))
        if pending[0] is not None:
            pj, ph, pdv = pending[0]
            emit_tav(pj, ph, pdv)
            if ph == 3:
                emit_wout(pj)
            tern_tiles(4)
        emit_normalize(j, qiA)
        qiB = emit_scores(j, h, (2, 3))
        emit_normalize(j, qiB)
        pending[0] = (j, h, cnt["dv"])
    pj, ph, pdv = pending[0]
    emit_tav(pj, ph, pdv)
    emit_wout(pj)
    tern_tiles(8)

    tern_tiles(200)

    p.w("gpsimd", "gd")
    p.w("gpsimd", "ld")
    p.i("gpsimd", lambda e: e.collective_compute(
        "ReduceScatter", ALU.add, replica_groups=G4,
        ins=[c.d_rs_inB], outs=[c.d_rs_outB]), "cc", 1)
    c.CC_RS = cnt["cc"]
    c.DV_P1 = cnt["dv"]


def emit_p2a(p, c):
    cnt = p.cnt
    p.w("sync", "cc", c.CC_RS)
    p.i("sync", lambda e: e.dma_start(out=c.bout_b[:], in_=c.d_bout), "ld", 16)
    p.i("sync", lambda e: e.dma_start(out=c.gamma_b[:], in_=c.d_gamma), "ld", 16)
    LD_BG = cnt["ld"]

    if c.debug:
        p.w("gpsimd", "cc", c.CC_RS)
        p.i("gpsimd", lambda e: e.dma_start(out=c.dbg["dbg_rso"][0:512, :],
                                            in_=c.d_rs_outA), "gd", 16)
        p.i("gpsimd", lambda e: e.dma_start(out=c.dbg["dbg_rso"][512:1024, :],
                                            in_=c.d_rs_outB), "gd", 16)
    rs_rel = [0, 0]
    xo_rel = [0, 0]
    pst2_rel = [0, 0]
    q1fm_parts = []

    p.w("sync", "cc", c.CC_RS)
    for tc in range(8):
        b = tc % 2
        p.w("sync", "dv", rs_rel[b])
        rsrc = (c.d_rs_outA[tc * 128:(tc + 1) * 128, :] if tc < 4
                else c.d_rs_outB[(tc - 4) * 128:(tc - 3) * 128, :])
        ld_rs = p.i("sync", lambda e, rsrc=rsrc, b=b: e.dma_start(
            out=c.rsx[b][:], in_=rsrc), "ld", 16)
        p.w("sync", "dv", xo_rel[b])
        ld_xo = p.i("sync", lambda e, tc=tc, b=b: e.dma_start(
            out=c.xo_f32[b][:], in_=c.d_xown[tc * 128:(tc + 1) * 128, :]), "ld", 16)
        p.w("vector", "ld", ld_rs)
        if tc == 0:
            p.w("vector", "ld", LD_BG)
        rs_rel[b] = p.i("vector", lambda e, b=b: e.tensor_tensor(
            out=c.ln_t1[:], in0=c.rsx[b][:], in1=c.bout_b[:], op=ALU.add), "dv")
        if c.debug and tc == 0:
            p.w("gpsimd", "dv")
            p.i("gpsimd", lambda e: e.dma_start(out=c.dbg["dbg_o1"], in_=c.ln_t1[:]), "gd", 16)
        p.w("scalar", "dv")
        _ln_stats(p, c, c.ln_t1[:], c.ln_t2[:], 1.0 / D)
        p.i("vector", lambda e: e.tensor_scalar(
            out=c.ln_t2[:], in0=c.ln_t1[:], scalar1=c.c_mu[:], scalar2=c.c_rstd[:],
            op0=ALU.add, op1=ALU.mult), "dv")
        p.w("vector", "ld", ld_xo)
        p.sw("vector", "dv")
        xo_rel[b] = p.i("vector", lambda e, tc=tc, b=b: e.tensor_tensor(
            out=c.r1[:, tc * D:(tc + 1) * D], in0=c.ln_t2[:], in1=c.xo_f32[b][:],
            op=ALU.add), "dv")
        p.w("scalar", "dv")
        _ln_stats(p, c, c.r1[:, tc * D:(tc + 1) * D], c.ln_t2[:], 1.0 / D)
        p.i("vector", lambda e, tc=tc: e.tensor_copy(
            out=c.ln3mu[:, tc:tc + 1], in_=c.c_mu[:]), "dv")
        p.i("vector", lambda e, tc=tc: e.tensor_copy(
            out=c.ln3rs[:, tc:tc + 1], in_=c.c_rstd[:]), "dv")
        dv_x2 = p.i("vector", lambda e, tc=tc: e.tensor_scalar(
            out=c.ln_t1[:], in0=c.r1[:, tc * D:(tc + 1) * D],
            scalar1=c.c_mu[:], scalar2=c.c_rstd[:], op0=ALU.add, op1=ALU.mult), "dv")
        if c.debug and tc == 0:
            p.w("gpsimd", "dv", dv_x2)
            gdt = p.i("gpsimd", lambda e: e.dma_start(out=c.dbg["dbg_x2"], in_=c.ln_t1[:]), "gd", 16)
            p.w("scalar", "gd", gdt)
            p.w("vector", "gd", gdt)
        p.w("scalar", "dv", dv_x2)
        p.i("scalar", lambda e: e.activation(out=c.ln_t2[:], in_=c.ln_t1[:],
                                             func=AF.Square, accum_out=c.c_ss[:]), "ac")
        p.w("scalar", "dv", getattr(c, "_recip_dv", 0))
        p.sw("scalar", "ac")
        ac_r = p.i("scalar", lambda e: e.activation(out=c.c_std[:], in_=c.c_ss[:],
                                                    func=AF.Sqrt, bias=c.eps_col[:],
                                                    scale=1.0 / D), "ac")
        c._sqrt_ac = ac_r
        p.w("vector", "ac", ac_r)
        c._recip_dv = p.i("vector", lambda e: e.reciprocal(out=c.c_rstd[:], in_=c.c_std[:]), "dv")
        p.sw("vector", "dv")
        p.i("vector", lambda e: e.tensor_scalar(
            out=c.ln_t1[:], in0=c.ln_t1[:], scalar1=c.c_rstd[:], scalar2=None,
            op0=ALU.mult), "dv")
        p.sw("vector", "dv")
        p.i("vector", lambda e: e.tensor_tensor(
            out=c.ln_t2[:], in0=c.ln_t1[:], in1=c.gamma_b[:], op=ALU.mult), "dv")
        p.i("vector", lambda e: e.reduce_max(out=c.c_am[:], in_=c.ln_t2[:], axis=AX.X,
                                             apply_absolute_value=True), "dv")
        p.sw("vector", "dv")
        p.i("vector", lambda e: e.tensor_scalar(
            out=c.c_amc[:], in0=c.c_am[:], scalar1=1e-5, scalar2=None,
            op0=ALU.max), "dv")
        p.w("vector", "dv")
        p.i("vector", lambda e: e.reciprocal(out=c.c_t1[:], in_=c.c_amc[:]), "dv")
        p.w("vector", "dv")
        p.i("vector", lambda e: e.tensor_scalar(
            out=c.c_rs1[:], in0=c.c_t1[:], scalar1=127.0, scalar2=None,
            op0=ALU.mult), "dv")
        p.i("vector", lambda e, tc=tc: e.scalar_tensor_tensor(
            out=c.d_allc[:, tc:tc + 1], in0=c.c_amc[:], scalar=1.0,
            in1=c.sgvf_b[:], op0=ALU.mult, op1=ALU.mult), "dv")
        p.sw("vector", "dv")
        p.i("vector", lambda e: e.tensor_scalar(
            out=c.ln_t1[:], in0=c.ln_t2[:], scalar1=c.c_rs1[:], scalar2=RT,
            op0=ALU.mult, op1=ALU.add), "dv")
        p.sw("vector", "dv")
        dv_q1 = p.i("vector", lambda e: e.tensor_scalar(
            out=c.q1_bf[:], in0=c.ln_t1[:], scalar1=RT, scalar2=None,
            op0=ALU.subtract), "dv")
        p.w("tensor", "dv", dv_q1)
        for g in range(4):
            tb = g % 2
            p.w("tensor", "dv", pst2_rel[tb])
            for kk in range(4):
                dcg = g * 4 + kk
                fn = lambda e, dcg=dcg, kk=kk, tb=tb: e.transpose(
                    c.ps_t2[tb][:, kk * 128:(kk + 1) * 128],
                    c.q1_bf[:, dcg * 128:(dcg + 1) * 128], c.ident_b[:])
                if kk < 3:
                    p.q("tensor", fn)
                else:
                    pe_t = p.i("tensor", fn, "pe")
            p.w("vector", "pe", pe_t)
            dst = c.q1_fm[:].rearrange("p (dc t) -> p dc t", dc=16)[:, g * 4:(g + 1) * 4,
                                                                   tc * 128:(tc + 1) * 128]
            src = c.ps_t2[tb][:].rearrange("p (a b) -> p a b", a=4)
            pst2_rel[tb] = p.i("vector", lambda e, dst=dst, src=src: e.tensor_copy(
                out=dst, in_=src), "dv")
    c.DV_P2A = cnt["dv"]
    c.PST2A = list(pst2_rel)
    if c.debug:
        p.w("gpsimd", "dv", cnt["dv"])
        p.i("gpsimd", lambda e: e.dma_start(out=c.dbg["dbg_r1"], in_=c.r1[:]), "gd", 16)
        p.i("gpsimd", lambda e: e.dma_start(out=c.dbg["dbg_q1"], in_=c.q1_fm[:]), "gd", 16)
        p.i("gpsimd", lambda e: e.dma_start(out=c.dbg["dbg_d"], in_=c.d_allc[:]), "gd", 16)


def emit_p2b(p, c):
    cnt = p.cnt
    pwh_rel = [0, 0, 0]
    pwh_ld = [0] * 16
    silu_rel = 0
    h2_rel = [0, 0]
    psvg_rel = [0, 0]
    psvg_rel_dv = [0, 0]

    def load_pair(pr):
        wb = pr % 3
        p.w("sync", "pe", pwh_rel[wb])
        if pr == 0:
            p.w("sync", "cc", c.CC_AG)
            p.w("sync", "cc", c.CC_RS)
            p.w("sync", "dv", c.DV_P2A)
        for hf in range(2):
            srcv = c.d_tgv_out[hf * 1024:(hf + 1) * 1024, pr * 512:(pr + 1) * 512]
            srcv = srcv.rearrange("(dc q) m -> q dc m", q=128)
            srcg = c.d_tgv_out[hf * 1024:(hf + 1) * 1024,
                               8192 + pr * 512: 8192 + (pr + 1) * 512]
            srcg = srcg.rearrange("(dc q) m -> q dc m", q=128)
            dstv = c.pwh[wb][:, hf * 8192:(hf + 1) * 8192].rearrange(
                "q (dc m) -> q dc m", m=1024)[:, :, 0:512]
            dstg = c.pwh[wb][:, hf * 8192:(hf + 1) * 8192].rearrange(
                "q (dc m) -> q dc m", m=1024)[:, :, 512:1024]
            p.i("sync", lambda e, dstv=dstv, srcv=srcv: e.dma_start(
                out=dstv, in_=srcv), "ld", 16)
            pwh_ld[pr] = p.i("sync", lambda e, dstg=dstg, srcg=srcg: e.dma_start(
                out=dstg, in_=srcg), "ld", 16)

    for pr in range(3):
        load_pair(pr)
    for pr in range(16):
        wb = pr % 3
        for tc in range(8):
            p.w("tensor", "ld", pwh_ld[pr])
            p.w("tensor", "dv", psvg_rel_dv[0])
            p.w("tensor", "ac", psvg_rel[1])
            if tc == 0 and pr == 0:
                p.w("tensor", "dv", c.DV_P2A)
            for dc in range(16):
                for gg in range(2):
                    fn = lambda e, dc=dc, gg=gg, tc=tc, wb=wb: e.matmul(
                        c.ps_a[gg][:],
                        c.q1_fm[:, dc * 1024 + tc * 128: dc * 1024 + (tc + 1) * 128],
                        c.pwh[wb][:, dc * 1024 + gg * 512: dc * 1024 + (gg + 1) * 512],
                        start=(dc == 0), stop=(dc == 15), skip_group_check=True)
                    if dc == 15 and gg == 1:
                        pe_mm = p.i("tensor", fn, "pe")
                    else:
                        p.q("tensor", fn)
            p.w("scalar", "pe", pe_mm)
            p.w("scalar", "dv", silu_rel)
            ac_g = p.i("scalar", lambda e, tc=tc: e.activation(
                out=c.silu_sb[:], in_=c.ps_a[1][:], func=AF.Silu,
                scale=c.d_allc[:, tc:tc + 1]), "ac")
            psvg_rel[1] = ac_g
            p.w("vector", "ac", ac_g)
            p.w("vector", "ld", h2_rel[tc % 2])
            dv_h = p.i("vector", lambda e, tc=tc: e.scalar_tensor_tensor(
                out=c.h2st[tc % 2][:], in0=c.ps_a[0][:], scalar=1.0,
                in1=c.silu_sb[:], op0=ALU.mult, op1=ALU.mult), "dv")
            silu_rel = dv_h
            psvg_rel_dv[0] = dv_h
            p.w("sync", "dv", dv_h)
            h2_rel[tc % 2] = p.i("sync", lambda e, tc=tc, pr=pr: e.dma_start(
                out=c.d_h2[tc, :, pr * 512:(pr + 1) * 512], in_=c.h2st[tc % 2][:]),
                "ld", 16)
        pwh_rel[wb] = cnt["pe"]
        if pr + 3 < 16:
            load_pair(pr + 3)
    c.LD_P2B = cnt["ld"]
    c.DV_P2B = cnt["dv"]
    c.AC_P2B = cnt["ac"]


def emit_p2c(p, c):
    cnt = p.cnt
    h2_rel = [0, 0]
    pst2_rel = list(c.PST2A)
    q2fm_rel = 0

    p.w("sync", "ld", c.LD_P2B)
    for tc in range(8):
        for hf in range(2):
            b = hf
            p.w("sync", "dv", h2_rel[b])
            p.i("sync", lambda e, tc=tc, hf=hf, b=b: e.dma_start(
                out=c.h2l[b][:], in_=c.d_h2[tc, :, hf * 4096:(hf + 1) * 4096]), "ld", 16)
        LD_H2 = cnt["ld"]
        p.w("vector", "ld", LD_H2)
        p.i("vector", lambda e: e.reduce_max(out=c.c_t1[:], in_=c.h2l[0][:], axis=AX.X,
                                             apply_absolute_value=True), "dv")
        p.i("vector", lambda e: e.reduce_max(out=c.c_t2[:], in_=c.h2l[1][:], axis=AX.X,
                                             apply_absolute_value=True), "dv")
        p.sw("vector", "dv")
        p.i("vector", lambda e: e.tensor_tensor(out=c.c_am[:], in0=c.c_t1[:],
                                                in1=c.c_t2[:], op=ALU.max), "dv")
        p.sw("vector", "dv")
        p.i("vector", lambda e, tc=tc: e.scalar_tensor_tensor(
            out=c.c_am[:], in0=c.c_am[:], scalar=1.0, in1=c.d_allc[:, tc:tc + 1],
            op0=ALU.mult, op1=ALU.mult), "dv")
        p.sw("vector", "dv")
        p.i("vector", lambda e: e.tensor_scalar(out=c.c_amc[:], in0=c.c_am[:],
                                                scalar1=1e-5, scalar2=None,
                                                op0=ALU.max), "dv")
        p.sw("vector", "dv")
        p.i("vector", lambda e, tc=tc: e.scalar_tensor_tensor(
            out=c.d2_allc[:, tc:tc + 1], in0=c.c_amc[:], scalar=1.0, in1=c.swof_b[:],
            op0=ALU.mult, op1=ALU.mult), "dv")
        p.w("vector", "dv")
        p.i("vector", lambda e: e.reciprocal(out=c.c_t1[:], in_=c.c_amc[:]), "dv")
        p.w("vector", "dv")
        p.i("vector", lambda e: e.tensor_scalar(out=c.c_t1[:], in0=c.c_t1[:],
                                                scalar1=127.0, scalar2=None,
                                                op0=ALU.mult), "dv")
        p.sw("vector", "dv")
        p.i("vector", lambda e, tc=tc: e.scalar_tensor_tensor(
            out=c.c_f2[:], in0=c.c_t1[:], scalar=1.0, in1=c.d_allc[:, tc:tc + 1],
            op0=ALU.mult, op1=ALU.mult), "dv")
        for hf in range(2):
            p.sw("vector", "dv")
            p.i("vector", lambda e, hf=hf: e.tensor_scalar(
                out=c.h2l[hf][:], in0=c.h2l[hf][:], scalar1=c.c_f2[:], scalar2=RT,
                op0=ALU.mult, op1=ALU.add), "dv")
            p.sw("vector", "dv")
            h2_rel[hf] = p.i("vector", lambda e, hf=hf: e.tensor_scalar(
                out=c.q2_bf[:, hf * 4096:(hf + 1) * 4096], in0=c.h2l[hf][:],
                scalar1=RT, scalar2=None, op0=ALU.subtract), "dv")
        dv_q2 = cnt["dv"]
        p.w("tensor", "dv", dv_q2)
        p.w("vector", "ld", q2fm_rel)
        for g in range(16):
            tb = g % 2
            p.w("tensor", "dv", pst2_rel[tb])
            for kk in range(4):
                hcg = g * 4 + kk
                fn = lambda e, hcg=hcg, kk=kk, tb=tb: e.transpose(
                    c.ps_t2[tb][:, kk * 128:(kk + 1) * 128],
                    c.q2_bf[:, hcg * 128:(hcg + 1) * 128], c.ident_b[:])
                if kk < 3:
                    p.q("tensor", fn)
                else:
                    pe_t = p.i("tensor", fn, "pe")
            p.w("vector", "pe", pe_t)
            pst2_rel[tb] = p.i("vector", lambda e, g=g, tb=tb: e.tensor_copy(
                out=c.q2fm_st[:, g * 512:(g + 1) * 512], in_=c.ps_t2[tb][:]), "dv")
        p.w("sync", "dv", pst2_rel[1])
        q2fm_rel = p.i("sync", lambda e, tc=tc: e.dma_start(
            out=c.d_q2[tc], in_=c.q2fm_st[:]), "ld", 16)
    c.LD_Q2 = cnt["ld"]


def emit_p2d(p, c):
    cnt = p.cnt
    q2_rel = [0, 0]
    o2st_rel = [0, 0]
    psa_rel = [0, 0]
    wDh_rel = [0, 0, 0]
    wDh_ld = [0] * 8
    p.w("tensor", "dv", c.DV_P2B)
    p.w("tensor", "ac", c.AC_P2B)

    def load_half(hh):
        fc, half = hh // 2, hh % 2
        wb = hh % 3
        p.w("sync", "pe", wDh_rel[wb])
        if hh == 0:
            p.w("sync", "ld", c.LD_Q2)
        for hg in range(4):
            hc0 = half * 32 + hg * 8
            srcw = c.d_two_out[hc0 * 128:(hc0 + 8) * 128, fc * 512:(fc + 1) * 512]
            srcw = srcw.rearrange("(hc q) m -> q hc m", q=128)
            dstw = c.wDh[wb][:, hg * 4096:(hg + 1) * 4096].rearrange(
                "q (hc m) -> q hc m", m=512)
            wDh_ld[hh] = p.i("sync", lambda e, srcw=srcw, dstw=dstw: e.dma_start(
                out=dstw, in_=srcw), "ld", 16)

    for hh in range(3):
        load_half(hh)
    for fc in range(4):
        wb0, wb1 = (fc * 2) % 3, (fc * 2 + 1) % 3
        for tc in range(8):
            b = tc % 2
            p.w("sync", "pe", q2_rel[b])
            ld_q = p.i("sync", lambda e, tc=tc, b=b: e.dma_start(
                out=c.q2sb[b][:], in_=c.d_q2[tc]), "ld", 16)
            ab = tc % 2
            p.w("tensor", "ld", ld_q)
            p.w("tensor", "ld", wDh_ld[fc * 2 + 1])
            p.w("tensor", "ac", psa_rel[ab])
            for hc in range(64):
                wb = wb0 if hc < 32 else wb1
                hl = hc % 32
                fn = lambda e, hc=hc, hl=hl, b=b, ab=ab, wb=wb: e.matmul(
                    c.ps_a[ab][:],
                    c.q2sb[b][:, hc * 128:(hc + 1) * 128],
                    c.wDh[wb][:, hl * 512:(hl + 1) * 512],
                    start=(hc == 0), stop=(hc == 63), skip_group_check=True)
                if hc < 63:
                    p.q("tensor", fn)
                else:
                    pe_o = p.i("tensor", fn, "pe")
            q2_rel[b] = pe_o
            p.w("scalar", "pe", pe_o)
            p.w("scalar", "ld", o2st_rel[tc % 2])
            ac_e = p.i("scalar", lambda e, tc=tc, ab=ab: e.activation(
                out=c.o2st[tc % 2][:], in_=c.ps_a[ab][:], func=AF.Copy,
                scale=c.d2_allc[:, tc:tc + 1]), "ac")
            psa_rel[ab] = ac_e
            p.w("sync", "ac", ac_e)
            o2st_rel[tc % 2] = p.i("sync", lambda e, tc=tc, fc=fc: e.dma_start(
                out=c.d_o2[tc, :, fc * 512:(fc + 1) * 512], in_=c.o2st[tc % 2][:]),
                "ld", 16)
        wDh_rel[wb0] = cnt["pe"]
        wDh_rel[wb1] = cnt["pe"]
        for hh in (fc * 2 + 3, fc * 2 + 4):
            if hh < 8:
                load_half(hh)
    c.LD_O2 = cnt["ld"]


def emit_tail(p, c):
    cnt = p.cnt
    o2_rel = [0, 0]
    y_rel = [0, 0]
    p.w("sync", "ld", c.LD_O2)
    for tc in range(8):
        b = tc % 2
        p.w("sync", "dv", o2_rel[b])
        ld_o = p.i("sync", lambda e, tc=tc, b=b: e.dma_start(
            out=c.o2sb[b][:], in_=c.d_o2[tc]), "ld", 16)
        p.i("vector", lambda e, tc=tc: e.tensor_scalar(
            out=c.tl_t1[:], in0=c.r1[:, tc * D:(tc + 1) * D],
            scalar1=c.ln3mu[:, tc:tc + 1], scalar2=c.ln3rs[:, tc:tc + 1],
            op0=ALU.add, op1=ALU.mult), "dv")
        p.w("vector", "ld", ld_o)
        p.sw("vector", "dv")
        o2_rel[b] = p.i("vector", lambda e, b=b: e.tensor_tensor(
            out=c.tl_t1[:], in0=c.tl_t1[:], in1=c.o2sb[b][:], op=ALU.add), "dv")
        p.w("vector", "ld", y_rel[b])
        p.w("scalar", "ld", y_rel[b])
        p.w("scalar", "dv")
        _ln_stats(p, c, c.tl_t1[:], c.y_st[b][:], 1.0 / D)
        p.i("vector", lambda e: e.tensor_scalar(
            out=c.tl_t2[:], in0=c.tl_t1[:], scalar1=c.c_mu[:], scalar2=c.c_rstd[:],
            op0=ALU.add, op1=ALU.mult), "dv")
        p.sw("vector", "dv")
        dv_y = p.i("vector", lambda e, tc=tc, b=b: e.tensor_tensor(
            out=c.y_st[b][:], in0=c.tl_t2[:], in1=c.r1[:, tc * D:(tc + 1) * D],
            op=ALU.add), "dv")
        p.w("sync", "dv", dv_y)
        y_rel[b] = p.i("sync", lambda e, tc=tc, b=b: e.dma_start(
            out=c.d_y[tc * 128:(tc + 1) * 128, :], in_=c.y_st[b][:]), "ld", 16)
    p.w("sync", "ld")
    p.w("gpsimd", "gd")


def shard_inputs(x, Wqkv, bqkv, Wout, bout, norm_w, w_gv, w_out):
    x = np.ascontiguousarray(np.asarray(x, dtype=np.float32))
    Wqkv = np.asarray(Wqkv, dtype=np.float32)
    bqkv = np.asarray(bqkv, dtype=np.float32)
    Wout = np.asarray(Wout, dtype=np.float32)
    bout = np.asarray(bout, dtype=np.float32)
    norm_w = np.asarray(norm_w, dtype=np.float32)
    wgvt = np.ascontiguousarray(np.asarray(w_gv, dtype=np.float32).T)
    woutt = np.ascontiguousarray(np.asarray(w_out, dtype=np.float32).T)
    bout_b = np.ascontiguousarray(np.broadcast_to(bout, (128, D)))
    gamma_b = np.ascontiguousarray(np.broadcast_to(norm_w, (128, D)))
    in_maps = []
    for cid in range(8):
        b, r = cid // 4, cid % 4
        heads = range(4 * r, 4 * r + 4)
        cols = []
        for role in range(3):
            for h in heads:
                cols.extend(range(role * D + h * 128, role * D + (h + 1) * 128))
        cols = np.asarray(cols)
        in_maps.append({
            "xb": x[b],
            "xown": np.ascontiguousarray(x[b, r * 1024:(r + 1) * 1024]),
            "wqkv": np.ascontiguousarray(Wqkv[:, cols]),
            "bqkv": np.ascontiguousarray(bqkv[cols]),
            "woutc": np.ascontiguousarray(Wout[4 * r * 128:(4 * r + 4) * 128, :]),
            "boutb": bout_b,
            "gammab": gamma_b,
            "wgv_sh": np.ascontiguousarray(wgvt[cid * 256:(cid + 1) * 256, :]),
            "wo_sh": np.ascontiguousarray(woutt[cid * 1024:(cid + 1) * 1024, :]),
        })
    return in_maps


def assemble(results):
    out = np.empty((2, T, D), dtype=np.float32)
    for cid in range(8):
        b, r = cid // 4, cid % 4
        out[b, r * 1024:(r + 1) * 1024, :] = results[cid]["y"]
    return out

from concourse.bass_utils import run_bass_kernel_spmd

_NC_CACHE = [None]


def kernel(L0_attn_x, L0_attn_Wqkv, L0_attn_bqkv, L0_attn_Wout, L0_attn_bout,
           L0_ffn_norm_w, L0_ffn_w_gv, L0_ffn_w_out):
    in_maps = shard_inputs(L0_attn_x, L0_attn_Wqkv, L0_attn_bqkv, L0_attn_Wout,
                           L0_attn_bout, L0_ffn_norm_w, L0_ffn_w_gv, L0_ffn_w_out)
    if _NC_CACHE[0] is None:
        _NC_CACHE[0] = build(debug=0)
    nc = _NC_CACHE[0]
    res = run_bass_kernel_spmd(nc, in_maps, core_ids=list(range(8)))
    return assemble(res.results)
